# version 01
# baseline (speedup 1.0000x reference)
import numpy as np
import ml_dtypes
import concourse.bass as bass
import concourse.mybir as mybir
from concourse.bass_utils import run_bass_kernel_spmd
from contextlib import ExitStack

F32 = mybir.dt.float32
BF16 = mybir.dt.bfloat16
AF = mybir.ActivationFunctionType
OP = mybir.AluOpType
AX = mybir.AxisListType

T = 2048
D = 1024
NT = 16
NE = 8
EPS_LN = 1e-5
EPS_MOE_LN = 1e-12
PHI_EPS = 6.25e-8
PHI_BIAS = -2.772588722239781
INV_S = 0.3535533905932738
CQ = -0.17677669529663687
SCALE_L1 = 0.10206207261596575

PHASES = 99
DEBUG_SERIAL = False
DBG = []



def _prep_common(i):
    d = {}
    cc = np.ascontiguousarray

    def nat(w, kcn):
        w = np.asarray(w, np.float32)
        return cc(w.reshape(kcn, 128, w.shape[1]).transpose(1, 0, 2))

    def ocsl(w, kcn, ocn):
        w = np.asarray(w, np.float32)
        return cc(w.reshape(kcn, 128, ocn, 128).transpose(2, 0, 1, 3))

    Wq = np.asarray(i["L0_attn_Wq"], np.float32)
    Wk = np.asarray(i["L0_attn_Wk"], np.float32)
    Wv = np.asarray(i["L0_attn_Wv"], np.float32)
    Wo = np.asarray(i["L0_attn_Wo"], np.float32)
    d["l0_wqT"] = cc(Wq.reshape(8, 128, 8, 128).transpose(0, 2, 3, 1))
    d["l0_wkT"] = cc(Wk.reshape(8, 128, 8, 128).transpose(0, 2, 3, 1))
    d["l0_wvT"] = cc(Wv.T.reshape(8, 128, 1024))
    d["l0_woT"] = cc(Wo.T.reshape(8, 128, 1024))
    d["l0_bqk"] = cc(np.stack([np.asarray(i["L0_attn_bq"], np.float32).reshape(8, 128).T,
                               np.asarray(i["L0_attn_bk"], np.float32).reshape(8, 128).T]))
    d["l0_bv"] = cc(np.asarray(i["L0_attn_bv"], np.float32)[None, :])
    d["l0_bo"] = cc(np.asarray(i["L0_attn_bo"], np.float32)[None, :])
    d["l0_orfT"] = cc(np.asarray(i["L0_attn_orf"], np.float32).T)

    Wr = np.asarray(i["L0_ffn_Wr"], np.float32)
    d["moe_wrT"] = cc(Wr.T.reshape(8, 128, 8))
    d["moe_br"] = cc(np.asarray(i["L0_ffn_br"], np.float32)[None, :])
    W1 = np.asarray(i["L0_ffn_W1"], np.float32)
    d["moe_w1"] = cc(W1.reshape(NE, 32, 128, 8, 128).transpose(0, 1, 4, 3, 2))
    W2 = np.asarray(i["L0_ffn_W2"], np.float32)
    d["moe_w2"] = cc(W2.transpose(0, 2, 1).reshape(NE, 32, 128, 1024))
    d["moe_b1"] = cc(np.asarray(i["L0_ffn_b1"], np.float32).reshape(NE * 32, 128).T)
    d["moe_b2"] = cc(np.asarray(i["L0_ffn_b2"], np.float32))
    d["moe_lnw"] = cc(np.asarray(i["L0_ffn_ln_w"], np.float32)[None, :])
    d["moe_lnb"] = cc(np.asarray(i["L0_ffn_ln_b"], np.float32)[None, :])

    d["l1_wdq"] = nat(i["L1_attn_W_DQ"], 8)
    d["l1_wdkv"] = nat(i["L1_attn_W_DKV"], 8)
    d["l1_wuq"] = nat(i["L1_attn_W_UQ"], 2)
    d["l1_wuk"] = nat(i["L1_attn_W_UK"], 2)
    d["l1_wuv"] = nat(i["L1_attn_W_UV"], 2)
    WQR = np.asarray(i["L1_attn_W_QR"], np.float32)
    perm = np.concatenate([np.arange(0, 32, 2) + 32 * h for h in range(16)] +
                          [np.arange(1, 32, 2) + 32 * h for h in range(16)])
    d["l1_wqr"] = nat(WQR[:, perm], 2)
    WKR = np.asarray(i["L1_attn_W_KR"], np.float32)
    permk = np.concatenate([np.arange(0, 32, 2), np.arange(1, 32, 2)])
    d["l1_wkr"] = nat(WKR[:, permk], 8)
    d["l1_wo"] = cc(np.asarray(i["L1_attn_W_O"], np.float32).reshape(8, 128, 1024))

    d["cn_w1"] = ocsl(i["L1_ffn_cn_w1"], 8, 8)
    d["cn_w2"] = ocsl(i["L1_ffn_cn_w2"], 8, 4)
    d["cn_w3"] = nat(i["L1_ffn_cn_w3"], 4)
    d["cn_w4"] = nat(i["L1_ffn_cn_w4"], 2)
    d["gc_w1"] = nat(i["L1_ffn_gc_w1"], 8)
    d["gc_w2"] = nat(i["L1_ffn_gc_w2"], 4)
    d["gc_w3"] = nat(i["L1_ffn_gc_w3"], 2)
    d["cn_b1"] = cc(np.asarray(i["L1_ffn_cn_b1"], np.float32).reshape(8, 128).T)
    d["cn_b2"] = cc(np.asarray(i["L1_ffn_cn_b2"], np.float32).reshape(4, 128).T)
    d["cn_b3"] = cc(np.asarray(i["L1_ffn_cn_b3"], np.float32).reshape(2, 128).T)
    d["cn_b4"] = cc(np.asarray(i["L1_ffn_cn_b4"], np.float32)[None, :])
    d["gc_b1"] = cc(np.asarray(i["L1_ffn_gc_b1"], np.float32).reshape(4, 128).T)
    d["gc_b2"] = cc(np.asarray(i["L1_ffn_gc_b2"], np.float32).reshape(2, 128).T)
    d["gc_b3"] = cc(np.asarray(i["L1_ffn_gc_b3"], np.float32)[None, :])
    d["l1_ar"] = cc(np.asarray(i["L1_ffn_adaptive_range"], np.float32)[None, :])

    d["ident"] = np.eye(128, dtype=np.float32)
    d["identb"] = np.eye(128, dtype=np.float32)
    e2 = np.zeros((128, 2), np.float32)
    e2[:64, 0] = 1.0
    e2[64:, 1] = 1.0
    d["e2"] = e2
    inv = 10000.0 ** (-np.arange(0, 32, 2, dtype=np.float32) / 32.0)
    ang = np.arange(T, dtype=np.float32)[:, None] * inv[None, :]
    d["cos_rep"] = cc(np.tile(np.cos(ang).T, (16, 1)).reshape(2, 128, T))
    d["sin_rep"] = cc(np.tile(np.sin(ang).T, (16, 1)).reshape(2, 128, T))
    qrs = np.zeros((8, 2, 128, 32), np.float32)
    for m_ in range(8):
        for j_ in range(16):
            qrs[m_, 0, 16 * m_ + j_, j_] = 1.0
            qrs[m_, 1, 16 * m_ + j_, 16 + j_] = 1.0
    d["qrsel"] = cc(qrs)
    krs = np.zeros((2, 16, 32), np.float32)
    for j_ in range(16):
        krs[0, j_, j_] = 1.0
        krs[1, j_, 16 + j_] = 1.0
    d["krsel"] = cc(krs)
    cq2 = np.zeros((2, 2, 128), np.float32)
    cq2[0, 0, :] = CQ; cq2[1, 1, :] = CQ
    d["cqsel"] = cq2
    ks2 = np.zeros((2, 2, 256), np.float32)
    ks2[0, 0, :] = CQ; ks2[1, 1, :] = CQ
    d["ksel"] = ks2
    masks = np.empty((4, 128, 512), np.float32)
    kk = np.arange(128)[:, None]
    qq = np.arange(512)[None, :]
    for mi in range(4):
        masks[mi] = np.where(kk <= qq - 128 * mi, 80.0, -1e9)
    d["l1_masks"] = masks
    return d



ENGS = ("t", "v", "s", "g", "y")


class Prog:

    def __init__(self, nc, stack):
        self.nc = nc
        self.stack = stack
        self.q = {k: [] for k in ENGS}
        self.cnt = {k: 0 for k in ENGS}
        self.step = {k: (16 if k in ("g", "y") else 1) for k in ENGS}
        self.waited = {k: {} for k in ENGS}
        self.sem = {k: stack.enter_context(nc.semaphore("sem_" + k)) for k in ENGS}
        self.isdma = {k: False for k in ENGS}
        self.views = {}

    def dsem(self, name):
        if name not in self.sem:
            self.sem[name] = self.stack.enter_context(self.nc.semaphore("sem_" + name))
            self.cnt[name] = 0
            self.step[name] = 16
            self.isdma[name] = True
        return name

    def em(self, eng, mk, inc=False, waits=(), drain=False, sem=None):
        wl = []
        wd = self.waited[eng]
        for tk in waits:
            if tk is None:
                continue
            pe, v = tk
            if pe == eng:
                continue
            if self.isdma.get(pe, False):
                v = self.cnt[pe]
            if v > 0 and wd.get(pe, 0) < v:
                wd[pe] = v
                wl.append((pe, v))
        sname = sem if sem is not None else eng
        if inc:
            self.cnt[sname] += self.step[sname]
        tok = (sname, self.cnt[sname])
        sems = self.sem

        def fn(e, _w=tuple(wl), _mk=mk, _inc=inc, _dr=drain, _sem=sems[sname],
               _stp=self.step[sname]):
            for (pe, v) in _w:
                e.wait_ge(sems[pe], v)
            if _dr:
                e.drain()
            ins = _mk(e)
            if _inc:
                ins.then_inc(_sem, _stp)
        self.q[eng].append(fn)
        return tok

    def tok(self, eng):
        return (eng, self.cnt[eng])

    def bsem(self, name):
        if name not in self.sem:
            self.sem[name] = self.stack.enter_context(self.nc.semaphore("sem_" + name))
            self.cnt[name] = 0
            self.step[name] = 1
            self.isdma[name] = False
        return name

    def barrier(self):
        toks = []
        for k in ENGS:
            toks.append(self.em(k, lambda e: e.nop(), inc=True, sem=self.bsem("b" + k)))
        allsems = [(s, self.cnt[s]) for s in list(self.cnt)
                   if self.cnt[s] > 0 and not s.startswith("b")]
        for k in ENGS:
            self.em(k, lambda e: e.nop(),
                    waits=[t for t, kk in zip(toks, ENGS) if kk != k] + allsems)


def build_nc(dbg=()):
    nc = bass.Bass("TRN2", target_bir_lowering=False)
    stack = ExitStack()
    p = Prog(nc, stack)
    em = p.em

    di = {}

    def inp(name, shape):
        di[name] = nc.dram_tensor(name, list(shape), F32, kind="ExternalInput").ap()

    inp("x", (T, D))
    inp("l0_wqT", (8, 8, 128, 128)); inp("l0_wkT", (8, 8, 128, 128))
    inp("l0_wvT", (8, 128, 1024)); inp("l0_woT", (8, 128, 1024))
    inp("l0_bqk", (2, 128, 8)); inp("l0_bv", (1, 1024)); inp("l0_bo", (1, 1024))
    inp("l0_orfT", (64, 256))
    inp("moe_wrT", (8, 128, 8)); inp("moe_br", (1, 8))
    inp("moe_w1", (NE, 32, 128, 8, 128)); inp("moe_w2", (NE, 32, 128, 1024))
    inp("moe_b1", (128, 256)); inp("moe_b2", (8, 1024))
    inp("moe_lnw", (1, 1024)); inp("moe_lnb", (1, 1024))
    inp("l1_wdq", (128, 8, 256)); inp("l1_wdkv", (128, 8, 256))
    inp("l1_wuq", (128, 2, 1024)); inp("l1_wuk", (128, 2, 1024)); inp("l1_wuv", (128, 2, 1024))
    inp("l1_wqr", (128, 2, 512)); inp("l1_wkr", (128, 8, 32))
    inp("l1_wo", (8, 128, 1024))
    inp("cn_w1", (8, 8, 128, 128)); inp("cn_w2", (4, 8, 128, 128))
    inp("cn_w3", (128, 4, 256)); inp("cn_w4", (128, 2, 1))
    inp("gc_w1", (128, 8, 512)); inp("gc_w2", (128, 4, 256)); inp("gc_w3", (128, 2, 1))
    inp("cn_b1", (128, 8)); inp("cn_b2", (128, 4)); inp("cn_b3", (128, 2)); inp("cn_b4", (1, 1))
    inp("gc_b1", (128, 4)); inp("gc_b2", (128, 2)); inp("gc_b3", (1, 1)); inp("l1_ar", (1, 1))
    inp("ident", (128, 128)); inp("identb", (128, 128)); inp("e2", (128, 2))
    inp("cos_rep", (2, 128, T)); inp("sin_rep", (2, 128, T))
    inp("cqsel", (2, 2, 128)); inp("ksel", (2, 2, 256))
    inp("qrsel", (8, 2, 128, 32)); inp("krsel", (2, 16, 32))
    inp("l1_masks", (4, 128, 512))
    out_h = nc.dram_tensor("out", [T, D], F32, kind="ExternalOutput").ap()

    AREN = 52352
    arena = stack.enter_context(nc.sbuf_tensor("arena", [128, AREN], F32))
    a32 = arena.ap()
    a16 = a32.bitcast(BF16)

    def f32v(off, n, p0=0, pn=128):
        return a32[p0:p0 + pn, off:off + n]

    def b16v(off, nb, p0=0, pn=128):
        return a16[p0:p0 + pn, 2 * off:2 * off + nb]

    XT_O = 0
    XNT_O = 16384
    CONST_O = 24576
    ACC_O = 26624
    FT_O = ACC_O + 16384
    assert FT_O + 9344 == AREN

    def XTt(i):
        return f32v(XT_O + 1024 * i, 1024)

    def acc_t(i):
        return f32v(ACC_O + 1024 * i, 1024)

    def xnt(c, t0=0, tn=T):
        o = 2 * XNT_O + 2048 * c + t0
        return a16[:, o:o + tn]

    xnt_all = a16[:, 2 * XNT_O:2 * XNT_O + 16384].rearrange("p (c t) -> p c t", t=2048)
    xt3 = a32[:, XT_O:XT_O + 16384].rearrange("p (i c) -> p i c", c=1024)

    cpool = [CONST_O]

    def calloc(n):
        o = cpool[0]
        cpool[0] += n
        assert cpool[0] <= ACC_O, "const pool overflow"
        return o

    IDENT_O = calloc(128); identv = f32v(IDENT_O, 128)
    IDB_O = calloc(64); identbv = b16v(IDB_O, 128)
    E2_O = calloc(2); e2v = f32v(E2_O, 2)
    ONES_O = calloc(128); onesFv = f32v(ONES_O, 128, 0, 1)
    ONESC_O = calloc(1); onescv = f32v(ONESC_O, 1)
    BR_O = calloc(8); brv = f32v(BR_O, 8, 0, 1)
    BQK_O = calloc(16)
    B1C_O = calloc(256)
    CWA_O = calloc(128)
    SML_O = calloc(64)
    WR_O = calloc(64); wr32 = f32v(WR_O, 64)
    CNB_O = calloc(16)
    GCB_O = calloc(8)
    COL_O = calloc(64)
    SEL_O = calloc(128)
    assert cpool[0] <= CONST_O + 2048

    bias_cols = {}

    def cbias(val):
        val = float(val)
        if val not in bias_cols:
            o = calloc(1)
            tk = em("v", lambda e, _a=f32v(o, 1), _v=val: e.memset(_a, _v), inc=True, drain=True)
            bias_cols[val] = (f32v(o, 1), tk)
        return bias_cols[val]

    psA = stack.enter_context(nc.psum_tensor("psA", [128, 1024], F32))
    psB = stack.enter_context(nc.psum_tensor("psB", [128, 1024], F32))
    psC = stack.enter_context(nc.psum_tensor("psC", [128, 1024], F32))
    psD = stack.enter_context(nc.psum_tensor("psD", [128, 1024], F32))
    PAIRS = [psA, psB, psC, psD]

    def half(pi, h, p0=0, pn=128):
        return PAIRS[pi].ap()[p0:p0 + pn, 512 * h:512 * h + 512]

    tok_x = em("y", lambda e: e.dma_start(out=xt3, in_=di["x"].rearrange("(i p) c -> p i c", p=128)),
               inc=True, sem=p.dsem("x"))
    tok_ident = em("y", lambda e: e.dma_start(out=identv, in_=di["ident"]), inc=True, sem=p.dsem("cst"))
    tok_e2 = em("y", lambda e: e.dma_start(out=e2v, in_=di["e2"]), inc=True, sem="cst")
    tok_idb = em("g", lambda e: e.dma_start(out=identbv, in_=di["identb"]), inc=True, sem=p.dsem("gc0"))
    tok_ones = em("v", lambda e: e.memset(onesFv, 1.0), inc=True)
    tok_onesc = em("v", lambda e: e.memset(onescv, 1.0), inc=True, drain=True)
    tok_br = em("y", lambda e: e.dma_start(out=brv, in_=di["moe_br"]), inc=True, sem="cst")
    tok_bqk = em("y", lambda e: e.dma_start(
        out=a32[:, BQK_O:BQK_O + 16].rearrange("p (a b) -> p a b", a=2),
        in_=di["l0_bqk"].rearrange("a p b -> p a b")), inc=True, sem="cst")
    tok_b1c = em("y", lambda e: e.dma_start(out=f32v(B1C_O, 256), in_=di["moe_b1"]), inc=True, sem="cst")
    tok_wr = em("y", lambda e: e.dma_start(
        out=a32[:, WR_O:WR_O + 64].rearrange("p (k e) -> p k e", k=8),
        in_=di["moe_wrT"].rearrange("k p e -> p k e")), inc=True, sem="cst")
    tok_cnb = em("y", lambda e: e.dma_start(out=f32v(CNB_O, 8), in_=di["cn_b1"]), inc=True, sem="cst")
    em("y", lambda e: e.dma_start(out=f32v(CNB_O + 8, 4), in_=di["cn_b2"]), inc=True, sem="cst")
    em("y", lambda e: e.dma_start(out=f32v(CNB_O + 12, 2), in_=di["cn_b3"]), inc=True, sem="cst")
    em("y", lambda e: e.dma_start(out=f32v(GCB_O, 4), in_=di["gc_b1"]), inc=True, sem="cst")
    em("y", lambda e: e.dma_start(out=f32v(GCB_O + 4, 2), in_=di["gc_b2"]), inc=True, sem="cst")
    em("y", lambda e: e.dma_start(out=f32v(COL_O + 5, 1, 0, 1), in_=di["gc_b3"]), inc=True, sem="cst")
    em("y", lambda e: e.dma_start(out=f32v(COL_O + 6, 1, 0, 1), in_=di["cn_b4"]), inc=True, sem="cst")
    tok_ar = em("y", lambda e: e.dma_start(out=f32v(COL_O + 7, 1, 0, 1), in_=di["l1_ar"]), inc=True, sem="cst")

    STG_O = FT_O
    LGS_O = FT_O + 2048

    def ln_tile(i, acc_off, eps, src_tile, src_tok=None):
        st = f32v(SML_O, 12)
        mv = f32v(SML_O + 12, 2)
        em("v", lambda e, _s=src_tile: e.bn_stats(f32v(SML_O, 6), _s[:, 0:512]), drain=True,
           waits=[src_tok] if src_tok else [])
        em("v", lambda e, _s=src_tile: e.bn_stats(f32v(SML_O + 6, 6), _s[:, 512:1024]), drain=True)
        t3 = em("v", lambda e: e.bn_aggr(mv, st), inc=True, drain=True)
        sd = f32v(SML_O + 14, 1)
        bc, btk = cbias(eps)
        t4 = em("s", lambda e, _b=bc: e.activation(out=sd, in_=mv[:, 1:2], func=AF.Sqrt, bias=_b,
                                                   scale=1.0), inc=True, waits=[t3, btk])
        isd = f32v(SML_O + 15, 1)
        em("v", lambda e: e.reciprocal(isd, sd), waits=[t4], drain=True)
        nmi = f32v(SML_O + 16, 1)
        t6 = em("v", lambda e: e.tensor_scalar(out=nmi, in0=mv[:, 0:1], scalar1=isd, scalar2=-1.0,
                                               op0=OP.mult, op1=OP.mult), inc=True, drain=True)
        dst = f32v(acc_off + 1024 * i, 1024)
        t7 = em("s", lambda e, _s=src_tile, _d=dst: e.activation(
            out=_d, in_=_s, func=AF.Identity, bias=nmi, scale=isd), inc=True, waits=[t6])
        return t7

    ps_free = [p.tok("v"), p.tok("v")]
    stg_free = [p.tok("t"), p.tok("t")]
    router_ps_free = [p.tok("v")]

    def transpose_router_tile(i, acc_off, apply_tok, do_router):
        vtoks = []
        for g in range(2):
            pr = g
            hv = half(pr, 0)
            last = None
            for q in range(4):
                src = f32v(acc_off + 1024 * i + 128 * (4 * g + q), 128)
                last = em("t", lambda e, _s=src, _o=hv[:, 128 * q:128 * q + 128],
                          _st=(q == 0), _sp=(q == 3):
                          e.matmul(_o, _s, identv, is_transpose=True, start=_st, stop=_sp),
                          inc=(q == 3), waits=[apply_tok, ps_free[pr], tok_ident])
            dst = xnt_all[:, 4 * g:4 * g + 4, 128 * i:128 * i + 128]
            tv = em("v", lambda e, _d=dst, _s=hv.rearrange("p (c t) -> p c t", t=128):
                    e.tensor_copy(_d, _s), inc=True, waits=[last], drain=True)
            vtoks.append(tv)
            if do_router:
                sb = i % 2
                sdst = f32v(STG_O + 1024 * sb + 512 * g, 512)
                tv2 = em("v", lambda e, _d=sdst, _s=hv: e.tensor_copy(_d, _s),
                         inc=True, waits=[last, stg_free[sb]], drain=True)
                vtoks.append(tv2)
            ps_free[pr] = p.tok("v")
        if not do_router:
            return
        sb = i % 2
        stok = p.tok("v")
        lg_ps = PAIRS[2].ap()[:, 0:8]
        for kc in range(8):
            lhs = f32v(STG_O + 1024 * sb + 128 * kc, 128)
            em("t", lambda e, _l=lhs, _w=wr32[:, 8 * kc:8 * kc + 8], _st=(kc == 0):
               e.matmul(lg_ps, _l, _w, start=_st, stop=False),
               waits=[stok, tok_wr, router_ps_free[0]])
        tmm = em("t", lambda e: e.matmul(lg_ps, onesFv, brv, start=False, stop=True),
                 inc=True, waits=[tok_ones, tok_br])
        stg_free[sb] = p.tok("t")
        lgs = f32v(LGS_O + 8 * i, 8)
        c0 = em("v", lambda e: e.tensor_copy(lgs, lg_ps), inc=True, waits=[tmm], drain=True)
        router_ps_free[0] = c0
        m1 = f32v(SML_O + 17, 1); m2 = f32v(SML_O + 18, 1)
        msk = f32v(SML_O + 19, 8); el = f32v(SML_O + 27, 8)
        dnm = f32v(SML_O + 35, 1); nm1 = f32v(SML_O + 36, 1)
        sm2 = f32v(SML_O + 37, 1); ex2 = f32v(SML_O + 38, 1)
        dn1 = f32v(SML_O + 39, 1); ge = f32v(SML_O + 40, 8)
        em("v", lambda e: e.reduce_max(m1, lgs, axis=AX.X), drain=True)
        em("v", lambda e: e.tensor_scalar(out=msk, in0=lgs, scalar1=m1, scalar2=None, op0=OP.is_ge),
           drain=True)
        em("v", lambda e: e.scalar_tensor_tensor(out=msk, in0=msk, scalar=-1e30, in1=lgs,
                                                 op0=OP.mult, op1=OP.add), drain=True)
        em("v", lambda e: e.reduce_max(m2, msk, axis=AX.X), drain=True)
        em("v", lambda e: e.tensor_scalar(out=nm1, in0=m1, scalar1=-1.0, scalar2=None, op0=OP.mult),
           drain=True)
        t8 = em("v", lambda e: e.tensor_tensor(out=sm2, in0=m2, in1=nm1, op=OP.add), inc=True,
                drain=True)
        t9 = em("s", lambda e: e.activation(out=ex2, in_=sm2, func=AF.Exp, bias=0.0, scale=1.0),
                inc=True, waits=[t8])
        em("v", lambda e: e.tensor_scalar(out=dn1, in0=ex2, scalar1=1.0, scalar2=None, op0=OP.add),
           waits=[t9], drain=True)
        t10 = em("v", lambda e: e.reciprocal(dnm, dn1), inc=True, drain=True)
        t11 = em("s", lambda e: e.activation(out=el, in_=lgs, func=AF.Exp, bias=nm1, scale=1.0),
                 inc=True, waits=[t10])
        em("v", lambda e: e.tensor_scalar(out=ge, in0=lgs, scalar1=m2, scalar2=None, op0=OP.is_ge),
           waits=[t11], drain=True)
        em("v", lambda e: e.tensor_tensor(out=ge, in0=ge, in1=el, op=OP.mult), drain=True)
        cwd = f32v(CWA_O + 8 * i, 8)
        em("v", lambda e, _d=cwd: e.tensor_scalar(out=_d, in0=ge, scalar1=dnm, scalar2=None,
                                                  op0=OP.mult), inc=True, drain=True)

    if PHASES >= 1:
        for i in range(NT):
            at = ln_tile(i, ACC_O, EPS_LN, XTt(i), tok_x)
            transpose_router_tile(i, ACC_O, at, True)
            if DEBUG_SERIAL:
                p.barrier()
        p.barrier()

    p.views["acc"] = (a32[:, ACC_O:ACC_O + 16384], (128, 16384))
    p.views["lgs"] = (f32v(LGS_O, 128), (128, 128))
    p.views["cw"] = (f32v(CWA_O, 128), (128, 128))
    p.views["xnt"] = (a16[:, 2 * XNT_O:2 * XNT_O + 16384], (128, 16384))

    H1_O = XT_O
    W2_O = XT_O + 8192
    W1_O = FT_O + 2100
    B2_O = FT_O + 3200

    def h1c(fc, t0=0, tn=T):
        o = 2 * H1_O + 2048 * fc + t0
        return a16[:, o:o + tn]

    def w2sl(qb, fc):
        o = 2 * (W2_O + 4096 * qb) + 1024 * fc
        return a16[:, o:o + 1024]

    def w1b(b):
        o = 2 * (W1_O + 512 * b)
        return a16[:, o:o + 1024].rearrange("p (k j) -> p k j", k=8)

    if PHASES >= 3:
        w1_use = [p.tok("t"), p.tok("t")]
        w2_use = [p.tok("t"), p.tok("t")]
        b2_use = p.tok("t")
        for e_ in range(NE):
            b2tok = em("y", lambda e, _e=e_: e.dma_start(out=f32v(B2_O, 1024, 0, 1),
                                                         in_=di["moe_b2"][_e:_e + 1, :]),
                       inc=True, waits=[b2_use], sem=p.dsem("b2"))
            for fq in range(4):
                qb = fq % 2
                w2toks = []
                for fc in range(8):
                    tk = em("g", lambda e, _qb=qb, _fc=fc, _e=e_, _f=8 * fq + fc:
                            e.dma_start(out=w2sl(_qb, _fc), in_=di["moe_w2"][_e, _f]),
                            inc=True, waits=[w2_use[qb]], sem=p.dsem("w2%d" % qb))
                    w2toks.append(tk)
                w2_ready = w2toks[-1]
                up_ps_free = [p.tok("s")] * 4
                gelu_last = None
                for fc in range(8):
                    fcg = 8 * fq + fc
                    b = fcg % 2
                    w1t = em("g", lambda e, _b=b, _e=e_, _f=fcg:
                             e.dma_start(out=w1b(_b), in_=di["moe_w1"][_e, _f]),
                             inc=True, waits=[w1_use[b]], sem=p.dsem("w1%d" % b))
                    for tt in range(4):
                        slot = (fc * 4 + tt) % 4
                        hv = half(slot // 2, slot % 2)
                        for kc in range(8):
                            em("t", lambda e, _h=hv, _b=b, _k=kc, _tt=tt,
                               _st=(kc == 0), _sp=(kc == 7):
                               e.matmul(_h, w1b(_b)[:, _k, :], xnt(_k, 512 * _tt, 512),
                                        start=_st, stop=_sp),
                               inc=(kc == 7), waits=[w1t, up_ps_free[slot]])
                        mmtok = p.tok("t")
                        w1_use[b] = mmtok
                        bia = f32v(B1C_O + 32 * e_ + fcg, 1)
                        gt = em("s", lambda e, _d=h1c(fc, 512 * tt, 512), _sv=hv, _b=bia:
                                e.activation(out=_d, in_=_sv, func=AF.Gelu, bias=_b, scale=1.0),
                                inc=True, waits=[mmtok, tok_b1c])
                        up_ps_free[slot] = gt
                        gelu_last = gt
                down_ps_free = [p.tok("v"), p.tok("v")]
                for ti in range(NT):
                    pr = 2 + (ti % 2)
                    first_start = True
                    if fq == 0:
                        for hh in range(2):
                            em("t", lambda e, _h=half(pr, hh), _o=512 * hh:
                               e.matmul(_h, onesFv, f32v(B2_O + _o, 512, 0, 1),
                                        start=True, stop=False),
                               waits=[b2tok, tok_ones, down_ps_free[ti % 2], gelu_last])
                        first_start = False
                    for fc in range(8):
                        for hh in range(2):
                            em("t", lambda e, _h=half(pr, hh), _ti=ti, _fc=fc, _qb=qb, _hh=hh,
                               _st=(first_start and fc == 0), _sp=(fc == 7):
                               e.matmul(_h, h1c(_fc, 128 * _ti, 128),
                                        w2sl(_qb, _fc)[:, 512 * _hh:512 * _hh + 512],
                                        start=_st, stop=_sp),
                               inc=(fc == 7 and hh == 1),
                               waits=[w2_ready, down_ps_free[ti % 2], gelu_last])
                    mmt = p.tok("t")
                    w2_use[qb] = mmt
                    if fq == 0:
                        b2_use = mmt
                    cwc = f32v(CWA_O + 8 * ti + e_, 1)
                    vt = em("v", lambda e, _p=PAIRS[pr].ap(), _a=acc_t(ti), _c=cwc:
                            e.scalar_tensor_tensor(out=_a, in0=_p, scalar=_c, in1=_a,
                                                   op0=OP.mult, op1=OP.add),
                            inc=True, waits=[mmt], drain=True)
                    down_ps_free[ti % 2] = vt
        p.barrier()
    p.views["moeacc"] = (a32[:, ACC_O:ACC_O + 16384], (128, 16384))

    if PHASES >= 3.5:
        tok_x2 = em("y", lambda e: e.dma_start(out=xt3,
                                               in_=di["x"].rearrange("(i p) c -> p i c", p=128)),
                    inc=True, sem="x")
        LNR_O = FT_O + 4224
        WB_O = FT_O + 6272
        BB_O = FT_O + 7296
        lnwrow = f32v(LNR_O, 1024, 0, 1)
        lnbrow = f32v(LNR_O + 1024, 1024, 0, 1)
        tw = em("y", lambda e: e.dma_start(out=lnwrow, in_=di["moe_lnw"]), inc=True, sem="cst")
        tb = em("y", lambda e: e.dma_start(out=lnbrow, in_=di["moe_lnb"]), inc=True, sem="cst")
        prev_ev = p.tok("v")
        for g in range(2):
            mt = None
            for hh in range(2):
                mt = em("t", lambda e, _h=half(0, hh), _o=1024 * g + 512 * hh:
                        e.matmul(_h, onesFv, f32v(LNR_O + _o, 512, 0, 1), start=True, stop=True),
                        inc=(hh == 1), waits=[tw, tb, tok_ones, prev_ev])
            dsto = WB_O if g == 0 else BB_O
            prev_ev = em("v", lambda e, _d=f32v(dsto, 1024), _s=PAIRS[0].ap():
                         e.tensor_copy(_d, _s), inc=True, waits=[mt], drain=True)
        tok_wb = prev_ev
        TMP_O = FT_O + 8320
        for i in range(NT):
            at = ln_tile(i, ACC_O, EPS_MOE_LN, acc_t(i))
            tmp = f32v(TMP_O, 1024)
            em("v", lambda e, _i=i, _t=tmp: e.tensor_tensor(out=_t, in0=acc_t(_i),
                                                            in1=f32v(WB_O, 1024), op=OP.mult),
               waits=[at, tok_wb], drain=True)
            em("v", lambda e, _t=tmp: e.tensor_tensor(out=_t, in0=_t, in1=f32v(BB_O, 1024),
                                                      op=OP.add), drain=True)
            em("v", lambda e, _i=i, _t=tmp: e.tensor_tensor(out=XTt(_i), in0=XTt(_i), in1=_t,
                                                            op=OP.add),
               inc=True, waits=[tok_x2], drain=True)
        p.barrier()
    p.views["xt_ffn0"] = (a32[:, XT_O:XT_O + 16384], (128, 16384))

    SCR4 = ACC_O
    V1_O = SCR4
    ATN_O = SCR4 + 2080
    QKP_O = SCR4 + 10272
    SQ_O = SCR4 + 12320
    PHQ_O = SCR4 + 12832
    PHK_O = SCR4 + 14880
    KVA_O = SCR4 + 16928
    NQ_O = SCR4 + 17056
    NK_O = SCR4 + 19104
    RC_O = NQ_O
    SEL_O4 = SCR4 + 21152
    WQ_O = SCR4 + 22176
    WK_O = SCR4 + 22688
    WV_O = SCR4 + 23200
    ORF_O = SCR4 + 23712
    BV_O = SCR4 + 23840
    assert BV_O + 1024 <= AREN

    def v1t(buf, ti, h2, n=65):
        o = 2 * (V1_O + 1040 * buf) + 130 * ti + 65 * h2
        return a16[:, o:o + n]

    def atn(c, t0=0, tn=T, p0=0, pn=128):
        o = 2 * ATN_O + 2048 * c + t0
        return a16[p0:p0 + pn, o:o + tn]

    def qkp(which, p0, pn, t0, tn):
        o = 2 * (QKP_O + 1024 * which) + t0
        return a16[p0:p0 + pn, o:o + tn]

    def phq(jc, t0, tn):
        o = 2 * PHQ_O + 2048 * jc + t0
        return a16[:, o:o + tn]

    def phk(ti, j0=0, jn=256):
        o = 2 * PHK_O + 256 * ti + j0
        return a16[:, o:o + jn]

    def kva(jc, c0=0, cn=65):
        o = 2 * (KVA_O + 64 * jc) + c0
        return a16[:, o:o + cn]

    if PHASES >= 4:
        t_orf = em("g", lambda e: e.dma_start(out=b16v(ORF_O, 256, 0, 64), in_=di["l0_orfT"]),
                   inc=True, sem=p.dsem("p4c"))
        t_orf2 = em("g", lambda e: e.dma_start(out=b16v(ORF_O, 256, 64, 64), in_=di["l0_orfT"]),
                    inc=True, sem="p4c")
        t_bv = em("y", lambda e: e.dma_start(out=f32v(BV_O, 1024, 0, 1), in_=di["l0_bv"]),
                  inc=True, sem=p.dsem("p4y"))
        em("y", lambda e: e.dma_start(
            out=a32[0:2, SEL_O4:SEL_O4 + 256].rearrange("p (a c) -> p a c", a=2),
            in_=di["cqsel"].rearrange("a p c -> p a c")), inc=True, sem=p.dsem("p4y"))
        em("y", lambda e: e.dma_start(
            out=a32[0:2, SEL_O4 + 256:SEL_O4 + 768].rearrange("p (a c) -> p a c", a=2),
            in_=di["ksel"].rearrange("a p c -> p a c")), inc=True, sem="p4y")
        selw = []
        selw.append(em("v", lambda e: e.memset(f32v(SEL_O4 + 768, 64, 64, 1), 1.0), drain=True))
        selw.append(em("v", lambda e: e.memset(f32v(SEL_O4 + 768 + 64, 64, 64, 1), 0.0), drain=True))
        selw.append(em("v", lambda e: e.memset(f32v(SEL_O4 + 896, 64, 64, 1), 0.0), drain=True))
        selw.append(em("v", lambda e: e.memset(f32v(SEL_O4 + 896 + 64, 64, 64, 1), 1.0), drain=True))
        for buf in range(2):
            selw.append(em("v", lambda e, _b=buf: e.memset(
                a16[:, 2 * (V1_O + 1040 * _b):2 * (V1_O + 1040 * _b) + 2080]
                .rearrange("p (t c) -> p t c", c=130)[:, :, 64:65], 1.0), drain=True))
            selw.append(em("v", lambda e, _b=buf: e.memset(
                a16[:, 2 * (V1_O + 1040 * _b):2 * (V1_O + 1040 * _b) + 2080]
                .rearrange("p (t c) -> p t c", c=130)[:, :, 129:130], 1.0), drain=True))
        tok_sel = em("v", lambda e: e.nop(), inc=True, waits=[("p4y", 32)])
        CQsel = [f32v(SEL_O4, 128, 0, 2), f32v(SEL_O4 + 128, 128, 0, 2)]
        Ksel = [f32v(SEL_O4 + 256, 256, 0, 2), f32v(SEL_O4 + 512, 256, 0, 2)]
        tok_selc = p.tok("p4y")
        E64 = [f32v(SEL_O4 + 768, 128, 64, 1), f32v(SEL_O4 + 896, 128, 64, 1)]
        bc_phi, tk_phi = cbias(PHI_BIAS)

        wqk_use = p.tok("t")
        wv_use = p.tok("t")
        v1_use = [p.tok("t"), p.tok("t")]
        qk_ready = None
        prev_head_done = p.tok("v")

        for pr_ in range(8):
            vb = pr_ % 2
            wqt = em("g", lambda e, _p=pr_: e.dma_start(
                out=b16v(WQ_O, 1024).rearrange("p (k j) -> p k j", k=8),
                in_=di["l0_wqT"][_p].rearrange("k p j -> p k j")),
                inc=True, waits=[wqk_use], sem=p.dsem("wqk"))
            wkt = em("g", lambda e, _p=pr_: e.dma_start(
                out=b16v(WK_O, 1024).rearrange("p (k j) -> p k j", k=8),
                in_=di["l0_wkT"][_p].rearrange("k p j -> p k j")),
                inc=True, sem="wqk")
            wvt = em("g", lambda e, _p=pr_: e.dma_start(
                out=b16v(WV_O, 1024).rearrange("p (k j) -> p k j", k=8),
                in_=di["l0_wvT"][:, :, 128 * _p:128 * _p + 128].rearrange("k p j -> p k j")),
                inc=True, waits=[wv_use], sem=p.dsem("wv"))
            v1ev = []
            for ti in range(NT):
                hv = half(0, ti % 2)
                hvv = PAIRS[0].ap()[:, 512 * (ti % 2):512 * (ti % 2) + 128]
                for kc in range(8):
                    em("t", lambda e, _h=hvv, _k=kc, _ti=ti, _st=(kc == 0):
                       e.matmul(_h, xnt(_k, 128 * _ti, 128),
                                b16v(WV_O, 1024).rearrange("p (k j) -> p k j", k=8)[:, _k, :],
                                start=_st, stop=False),
                       waits=[wvt, v1_use[vb], prev_head_done,
                              (v1ev[ti - 2] if ti >= 2 else None)])
                em("t", lambda e, _h=hvv, _p=pr_:
                   e.matmul(_h, onesFv, f32v(BV_O + 128 * _p, 128, 0, 1), start=False, stop=True),
                   inc=True, waits=[t_bv, tok_ones])
                mmv = p.tok("t")
                ev = em("v", lambda e, _d=v1t(vb, ti, 0, 130).rearrange("p (a c) -> p a c", c=65)[:, :, 0:64],
                        _s=hvv.rearrange("p (a c) -> p a c", c=64), _m=mmv:
                        e.tensor_copy(_d, _s), inc=True, waits=[mmv, tok_sel], drain=True)
                v1ev.append(ev)
            v1_ready = v1ev[-1]
            qev = None
            for which, wt_, bcol in ((0, wqt, 0), (1, wkt, 8)):
                for kc in range(8):
                    wsl = b16v(WQ_O + 512 * which, 1024).rearrange("p (k j) -> p k j", k=8)
                    for tt in range(4):
                        hv = half(1 + tt // 2, tt % 2)
                        em("t", lambda e, _h=hv, _w=wsl, _k=kc, _tt=tt,
                           _st=(kc == 0), _sp=(kc == 7):
                           e.matmul(_h, _w[:, _k, :], xnt(_k, 512 * _tt, 512), start=_st, stop=_sp),
                           inc=(kc == 7 and tt == 3),
                           waits=[wt_, qk_ready, prev_head_done, qev])
                mmq = p.tok("t")
                if which == 1:
                    wqk_use = mmq
                for tt in range(4):
                    hv = half(1 + tt // 2, tt % 2)
                    bia = f32v(BQK_O + bcol + pr_, 1)
                    em("s", lambda e, _d=qkp(which, 0, 128, 512 * tt, 512), _sv=hv, _b=bia:
                       e.activation(out=_d, in_=_sv, func=AF.Identity, bias=_b, scale=1.0),
                       inc=(tt == 3), waits=[mmq, tok_bqk], drain=True)
                if which == 0:
                    qev = p.tok("s")
                else:
                    kev = p.tok("s")
            nrm_toks = []
            sq_free = None
            for which, ev_, dsto in ((0, None, NQ_O), (1, None, NK_O)):
                evtok = qev if which == 0 else kev
                for tt in range(4):
                    sq = f32v(SQ_O, 512)
                    t1 = em("s", lambda e, _s=qkp(which, 0, 128, 512 * tt, 512), _d=sq:
                            e.activation(out=_d, in_=_s, func=AF.Square, bias=0.0, scale=1.0),
                            inc=True, waits=[evtok, sq_free], drain=True)
                    nps = PAIRS[2].ap()[0:2, 0:512]
                    t2 = em("t", lambda e, _s=sq: e.matmul(nps, e2v, _s, start=True, stop=True),
                            inc=True, waits=[t1, tok_e2, prev_head_done,
                                             nrm_toks[-1] if nrm_toks else None])
                    sq_free = t2
                    t3 = em("v", lambda e, _d=f32v(dsto + 512 * tt, 512, 0, 2), _s=nps:
                            e.tensor_copy(_d, _s), inc=True, waits=[t2], drain=True)
                    nrm_toks.append(t3)
            norm_done = nrm_toks[-1]
            for h2 in range(2):
                h = 2 * pr_ + h2
                phiq_toks = []
                phiq_slot_free = [None] * 4
                for jc in range(2):
                    for tt in range(4):
                        slot = (jc * 4 + tt) % 4
                        hv = half(1 + slot // 2, slot % 2)
                        em("t", lambda e, _h=hv, _j=jc, _h2=h2, _tt=tt:
                           e.matmul(_h, b16v(ORF_O, 256, 64 * _h2, 64)[:, 128 * _j:128 * _j + 128],
                                    qkp(0, 64 * _h2, 64, 512 * _tt, 512), start=True, stop=False),
                           waits=[t_orf2, norm_done, prev_head_done, phiq_slot_free[slot]])
                        em("t", lambda e, _h=hv, _h2=h2, _tt=tt:
                           e.matmul(_h, CQsel[_h2], f32v(NQ_O + 512 * _tt, 512, 0, 2),
                                    start=False, stop=True), inc=True, waits=[tok_sel])
                        mmp = p.tok("t")
                        sqv = f32v(SQ_O, 512)
                        t1 = em("s", lambda e, _d=sqv, _s=hv, _b=bc_phi:
                                e.activation(out=_d, in_=_s, func=AF.Exp, bias=_b, scale=INV_S),
                                inc=True, waits=[mmp, tk_phi, sq_free], drain=True)
                        phiq_slot_free[slot] = t1
                        t2 = em("v", lambda e, _d=phq(jc, 512 * tt, 512), _s=sqv:
                                e.tensor_scalar(out=_d, in0=_s, scalar1=PHI_EPS, scalar2=None,
                                                op0=OP.add), inc=True, waits=[t1], drain=True)
                        sq_free = t2
                        phiq_toks.append(t2)
                phiq_done = phiq_toks[-1]
                phik_toks = []
                for ti in range(NT):
                    hv = PAIRS[3].ap()[:, 512 * (ti % 2):512 * (ti % 2) + 256]
                    em("t", lambda e, _h=hv, _h2=h2, _ti=ti:
                       e.matmul(_h, qkp(1, 64 * _h2, 64, 128 * _ti, 128),
                                b16v(ORF_O, 256, 64 * _h2, 64), start=True, stop=False),
                       waits=[norm_done, prev_head_done,
                              (phik_toks[ti - 2] if ti >= 2 else None)])
                    em("t", lambda e, _h=hv, _h2=h2, _ti=ti:
                       e.matmul(_h, f32v(NK_O + 128 * _ti, 128, 0, 2), Ksel[_h2],
                                start=False, stop=True), inc=True, waits=[tok_sel])
                    mmp = p.tok("t")
                    sqv = f32v(SQ_O, 256)
                    t1 = em("s", lambda e, _d=sqv, _s=hv, _b=bc_phi:
                            e.activation(out=_d, in_=_s, func=AF.Exp, bias=_b, scale=INV_S),
                            inc=True, waits=[mmp, sq_free], drain=True)
                    t2 = em("v", lambda e, _d=phk(ti), _s=sqv:
                            e.tensor_scalar(out=_d, in0=_s, scalar1=PHI_EPS, scalar2=None,
                                            op0=OP.add), inc=True, waits=[t1], drain=True)
                    sq_free = t2
                    phik_toks.append(t2)
                phik_done = phik_toks[-1]
                kv_toks = []
                for jc in range(2):
                    hv = PAIRS[0].ap()[:, 512 * jc:512 * jc + 65]
                    for ti in range(NT):
                        em("t", lambda e, _h=hv, _j=jc, _ti=ti, _h2=h2, _vb=vb,
                           _st=(ti == 0), _sp=(ti == 15):
                           e.matmul(_h, phk(_ti, 128 * _j, 128), v1t(_vb, _ti, _h2),
                                    start=_st, stop=_sp),
                           inc=(ti == 15), waits=[phik_done, v1_ready])
                    mk = p.tok("t")
                    v1_use[vb] = mk
                    ev = em("v", lambda e, _d=kva(jc), _s=hv: e.tensor_copy(_d, _s),
                            inc=True, waits=[mk], drain=True)
                    kv_toks.append(ev)
                kv_done = kv_toks[-1]
                nd_toks = []

                def dslot(tt):
                    dpair, dh = (0, tt) if tt < 2 else (3, tt - 2)
                    return PAIRS[dpair].ap()[64:65, 512 * dh:512 * dh + 512]
                for jc in range(2):
                    for tt in range(4):
                        pi, hh = (1 + tt // 2, tt % 2)
                        if h2 == 0:
                            outp = PAIRS[pi].ap()[0:65, 512 * hh:512 * hh + 512]
                            em("t", lambda e, _o=outp, _j=jc, _tt=tt, _st=(jc == 0), _sp=(jc == 1):
                               e.matmul(_o, kva(_j), phq(_j, 512 * _tt, 512), start=_st, stop=_sp),
                               inc=(jc == 1), waits=[kv_done, phiq_done])
                        else:
                            outp = PAIRS[pi].ap()[64:128, 512 * hh:512 * hh + 512]
                            em("t", lambda e, _o=outp, _j=jc, _tt=tt, _st=(jc == 0), _sp=(jc == 1):
                               e.matmul(_o, kva(_j, 0, 64), phq(_j, 512 * _tt, 512),
                                        start=_st, stop=_sp),
                               waits=[kv_done, phiq_done])
                            em("t", lambda e, _o=dslot(tt), _j=jc, _tt=tt,
                               _st=(jc == 0), _sp=(jc == 1):
                               e.matmul(_o, kva(_j, 64, 1), phq(_j, 512 * _tt, 512),
                                        start=_st, stop=_sp), inc=(jc == 1))
                        if jc == 1:
                            nd_toks.append(p.tok("t"))
                head_v_toks = []
                for tt in range(4):
                    pi, hh = (1 + tt // 2, tt % 2)
                    ndt = nd_toks[tt]
                    if h2 == 0:
                        nsrc = PAIRS[pi].ap()[0:64, 512 * hh:512 * hh + 512]
                        dsrc = PAIRS[pi].ap()[64:65, 512 * hh:512 * hh + 512]
                    else:
                        nsrc = PAIRS[pi].ap()[64:128, 512 * hh:512 * hh + 512]
                        dsrc = dslot(tt)
                    ev = em("v", lambda e, _d=atn(pr_, 512 * tt, 512, 64 * h2, 64), _s=nsrc:
                            e.tensor_copy(_d, _s), inc=True, waits=[ndt], drain=True)
                    rc = f32v(RC_O + 512 * tt, 512, 64, 1)
                    t1 = em("s", lambda e, _d=rc, _s=dsrc: e.activation(
                        out=_d, in_=_s, func=AF.Ln, bias=0.0, scale=1.0), waits=[ndt], drain=True)
                    t2 = em("s", lambda e, _d=rc: e.activation(
                        out=_d, in_=_d, func=AF.Exp, bias=0.0, scale=-1.0), inc=True, drain=True)
                    rtok = p.tok("s")
                    bcp = PAIRS[pi].ap()[:, 512 * hh:512 * hh + 512]
                    t3 = em("t", lambda e, _o=bcp, _h2=h2, _r=rc: e.matmul(
                        _o, E64[_h2], _r, start=True, stop=True),
                        inc=True, waits=[rtok, tok_sel, ev])
                    bct = p.tok("t")
                    t4 = em("v", lambda e, _a=atn(pr_, 512 * tt, 512, 64 * h2, 64),
                            _b=bcp[64 * h2:64 * h2 + 64, :]:
                            e.tensor_tensor(out=_a, in0=_a, in1=_b, op=OP.mult),
                            inc=True, waits=[bct], drain=True)
                    head_v_toks.append(t4)
                prev_head_done = head_v_toks[-1]
            qk_ready = prev_head_done
        bo_row = f32v(QKP_O, 1024, 0, 1)
        t_bo = em("y", lambda e: e.dma_start(out=bo_row, in_=di["l0_bo"]), inc=True, sem="p4y",
                  waits=[prev_head_done])
        wo_use = [p.tok("t"), p.tok("t")]
        fold_evs = []
        for grp in range(4):
            for kc in range(8):
                wb = kc % 2
                wo_dst = b16v(WV_O if wb == 0 else WQ_O, 1024)
                wot = em("g", lambda e, _d=wo_dst, _k=kc: e.dma_start(out=_d, in_=di["l0_woT"][_k]),
                         inc=True, waits=[wo_use[wb], prev_head_done], sem=p.dsem("wo%d" % wb))
                for t_ in range(4):
                    ti = 4 * grp + t_
                    pv = PAIRS[t_].ap()
                    if kc == 0:
                        for hh in range(2):
                            em("t", lambda e, _h=half(t_, hh), _o=512 * hh:
                               e.matmul(_h, onesFv, f32v(QKP_O + _o, 512, 0, 1),
                                        start=True, stop=False),
                               waits=[t_bo, tok_ones, prev_head_done,
                                      (fold_evs[4 * (grp - 1) + t_] if grp > 0 else None)])
                    for hh in range(2):
                        em("t", lambda e, _h=half(t_, hh), _k=kc, _ti=ti, _hh=hh, _w=wo_dst,
                           _sp=(kc == 7):
                           e.matmul(_h, atn(_k, 128 * _ti, 128),
                                    _w[:, 512 * _hh:512 * _hh + 512], start=False, stop=_sp),
                           inc=((kc == 7 and hh == 1) or (t_ == 3 and hh == 1)), waits=[wot])
                    if kc == 7:
                        mmt = p.tok("t")
                        ev = em("v", lambda e, _x=XTt(ti), _s=pv:
                                e.tensor_tensor(out=_x, in0=_x, in1=_s, op=OP.add),
                                inc=True, waits=[mmt], drain=True)
                        fold_evs.append(ev)
                wo_use[wb] = p.tok("t")
        p.barrier()
    p.views["atn0"] = (a16[:, 2 * ATN_O:2 * ATN_O + 16384], (128, 16384))

    if PHASES >= 5:
        for i in range(NT):
            at = ln_tile(i, ACC_O, EPS_LN, XTt(i))
            transpose_router_tile(i, ACC_O, at, False)
        p.barrier()
    p.views["xnt1"] = (a16[:, 2 * XNT_O:2 * XNT_O + 16384], (128, 16384))

    if PHASES >= 6:
        GC1_O = FT_O
        GC2_O = FT_O + 2048
        GC3_O = FT_O + 2560
        GCB_B = FT_O + 2568
        H1G_O = FT_O + 2572
        H2G_O = FT_O + 2574
        g1t = em("g", lambda e: e.dma_start(
            out=b16v(GC1_O, 4096).rearrange("p (k o) -> p k o", k=8), in_=di["gc_w1"]),
            inc=True, sem=p.dsem("gcw"))
        g2t = em("g", lambda e: e.dma_start(
            out=b16v(GC2_O, 1024).rearrange("p (k o) -> p k o", k=4), in_=di["gc_w2"]),
            inc=True, sem="gcw")
        g3t = em("g", lambda e: e.dma_start(
            out=b16v(GC3_O, 2).rearrange("p (k o) -> p k o", k=2), in_=di["gc_w3"]),
            inc=True, sem="gcw")
        gps = PAIRS[0].ap()[:, 0:1]
        gc_ev = None
        for oc in range(8):
            for ti in range(NT):
                em("t", lambda e, _o=oc, _ti=ti, _st=(ti == 0), _sp=(ti == 15):
                   e.matmul(gps, f32v(ACC_O + 1024 * _ti + 128 * _o, 128), onescv,
                            start=_st, stop=_sp),
                   inc=(ti == 15), waits=[tok_onesc, gc_ev])
            mmt = p.tok("t")
            gc_ev = em("s", lambda e, _d=a16[:, 2 * GCB_B + oc:2 * GCB_B + oc + 1], _s=gps:
                       e.activation(out=_d, in_=_s, func=AF.Identity, bias=0.0,
                                    scale=float(1.0 / 2048.0)),
                       inc=True, waits=[mmt], drain=True)
        h_ev = gc_ev
        for oc in range(4):
            for kc in range(8):
                em("t", lambda e, _k=kc, _o=oc, _st=(kc == 0), _sp=(kc == 7):
                   e.matmul(gps, b16v(GC1_O, 4096).rearrange("p (k o) -> p k o", k=8)
                            [:, _k, 128 * _o:128 * _o + 128],
                            a16[:, 2 * GCB_B + _k:2 * GCB_B + _k + 1],
                            start=_st, stop=_sp),
                   inc=(kc == 7), waits=[g1t, h_ev])
            mmt = p.tok("t")
            h_ev = em("s", lambda e, _d=a16[:, 2 * H1G_O + oc:2 * H1G_O + oc + 1],
                      _s=gps, _b=f32v(GCB_O + oc, 1):
                      e.activation(out=_d, in_=_s, func=AF.Gelu, bias=_b, scale=1.0),
                      inc=True, waits=[mmt, tok_cnb], drain=True)
        for oc in range(2):
            for kc in range(4):
                em("t", lambda e, _k=kc, _o=oc, _st=(kc == 0), _sp=(kc == 3):
                   e.matmul(gps, b16v(GC2_O, 1024).rearrange("p (k o) -> p k o", k=4)
                            [:, _k, 128 * _o:128 * _o + 128],
                            a16[:, 2 * H1G_O + _k:2 * H1G_O + _k + 1],
                            start=_st, stop=_sp),
                   inc=(kc == 3), waits=[g2t, h_ev])
            mmt = p.tok("t")
            h_ev = em("s", lambda e, _d=a16[:, 2 * H2G_O + oc:2 * H2G_O + oc + 1],
                      _s=gps, _b=f32v(GCB_O + 4 + oc, 1):
                      e.activation(out=_d, in_=_s, func=AF.Gelu, bias=_b, scale=1.0),
                      inc=True, waits=[mmt], drain=True)
        for kc in range(2):
            em("t", lambda e, _k=kc, _st=(kc == 0), _sp=(kc == 1):
               e.matmul(gps[0:1, 0:1],
                        b16v(GC3_O, 2).rearrange("p (k o) -> p k o", k=2)[:, _k, :],
                        a16[:, 2 * H2G_O + _k:2 * H2G_O + _k + 1], start=_st, stop=_sp),
               inc=(kc == 1), waits=[g3t, h_ev])
        mmt = p.tok("t")
        gf_ev = em("s", lambda e: e.activation(out=f32v(COL_O + 8, 1, 0, 1),
                                               in_=gps[0:1, 0:1], func=AF.Sigmoid,
                                               bias=f32v(COL_O + 5, 1, 0, 1), scale=1.0),
                   inc=True, waits=[mmt, tok_ar], drain=True)
        bc_ev = gf_ev
        for srcc, dstc in ((8, 3), (7, 4), (6, 2)):
            mt = em("t", lambda e, _s=f32v(COL_O + srcc, 1, 0, 1):
                    e.matmul(gps, onesFv, _s, start=True, stop=True),
                    inc=True, waits=[bc_ev, tok_ones])
            bc_ev = em("v", lambda e, _d=f32v(COL_O + dstc, 1), _s=gps:
                       e.tensor_copy(_d, _s), inc=True, waits=[mt], drain=True)
        em("v", lambda e: e.tensor_tensor(out=f32v(COL_O + 0, 1), in0=f32v(COL_O + 4, 1),
                                          in1=f32v(COL_O + 3, 1), op=OP.mult), drain=True)
        em("v", lambda e: e.tensor_scalar(out=f32v(COL_O + 0, 1), in0=f32v(COL_O + 0, 1),
                                          scalar1=1.4, scalar2=None, op0=OP.mult), drain=True)
        em("v", lambda e: e.scalar_tensor_tensor(out=f32v(COL_O + 0, 1), in0=f32v(COL_O + 4, 1),
                                                 scalar=-1.0, in1=f32v(COL_O + 0, 1),
                                                 op0=OP.mult, op1=OP.add), drain=True)
        em("v", lambda e: e.tensor_scalar(out=f32v(COL_O + 0, 1), in0=f32v(COL_O + 0, 1),
                                          scalar1=0.1, scalar2=None, op0=OP.add), drain=True)
        em("v", lambda e: e.tensor_scalar(out=f32v(COL_O + 1, 1), in0=f32v(COL_O + 4, 1),
                                          scalar1=0.6, scalar2=None, op0=OP.mult),
           inc=True, drain=True)
        p.barrier()

    SCR8 = ACC_O
    ATN1_O = SCR8
    TBL_O = SCR8
    CQT_O = SCR8 + 8192
    CKV_O = SCR8 + 10240
    QRA_O = SCR8 + 12288
    QRP_O = SCR8 + 16384
    QH_O = SCR8 + 16384
    KH_O = SCR8 + 18432
    RA_O = SCR8 + 20480
    assert RA_O + 5248 == AREN

    def atn1(c, t0=0, tn=T, p0=0, pn=128):
        o = 2 * ATN1_O + 2048 * c + t0
        return a16[p0:p0 + pn, o:o + tn]

    def cqt(c, t0=0, tn=T):
        o = 2 * CQT_O + 2048 * c + t0
        return a16[:, o:o + tn]

    def ckvt(c, t0=0, tn=T):
        o = 2 * CKV_O + 2048 * c + t0
        return a16[:, o:o + tn]

    def qrav(c, t0=0, tn=T, p0=0, pn=128):
        o = 2 * QRA_O + 2048 * c + t0
        return a16[p0:p0 + pn, o:o + tn]

    def qhv(b, p0=0, pn=128, t0=0, tn=T):
        o = 2 * (QH_O + 1024 * b) + t0
        return a16[p0:p0 + pn, o:o + tn]

    def khv(b, p0=0, pn=128, t0=0, tn=T):
        o = 2 * (KH_O + 1024 * b) + t0
        return a16[p0:p0 + pn, o:o + tn]

    def esb(b):
        o = 2 * (ES_O8 + 256 * b)
        return a16[:, o:o + 512]

    def v1l(ti, h2, c0=0, cn=65):
        o = 2 * V1L_O8 + 130 * ti + 65 * h2 + c0
        return a16[:, o:o + cn]

    ES_O8 = RA_O
    V1L_O8 = RA_O + 512
    MSK_O8 = RA_O + 1552
    RC_O8 = RA_O + 2576
    UQ_O8 = RA_O + 3088
    SEL_O8 = RA_O + 3472
    QKS_O8 = RA_O + 4960

    if PHASES >= 6.5:
        for ci, (nm, cc_) in enumerate((("cos_rep", 0), ("cos_rep", 1),
                                        ("sin_rep", 0), ("sin_rep", 1))):
            em("y", lambda e, _n=nm, _c=cc_, _i=ci: e.dma_start(
                out=f32v(TBL_O + 2048 * _i, 2048), in_=di[_n][_c]),
                inc=True, sem=p.dsem("tbl"))
        t_wdq = em("g", lambda e: e.dma_start(
            out=b16v(RA_O, 2048).rearrange("p (k o) -> p k o", k=8), in_=di["l1_wdq"]),
            inc=True, sem=p.dsem("l1w"))
        t_wdkv = em("g", lambda e: e.dma_start(
            out=b16v(RA_O + 1024, 2048).rearrange("p (k o) -> p k o", k=8), in_=di["l1_wdkv"]),
            inc=True, sem="l1w")
        t_wqr = em("g", lambda e: e.dma_start(
            out=b16v(RA_O + 2048, 1024).rearrange("p (k o) -> p k o", k=2), in_=di["l1_wqr"]),
            inc=True, sem="l1w")
        t_wkr = em("g", lambda e: e.dma_start(
            out=b16v(RA_O + 2560, 256).rearrange("p (k o) -> p k o", k=8), in_=di["l1_wkr"]),
            inc=True, sem="l1w")
        t_qrs = em("g", lambda e: e.dma_start(
            out=b16v(QKS_O8, 512).rearrange("p (m j) -> p m j", m=16),
            in_=di["qrsel"].rearrange("m v p j -> p (m v) j")), inc=True, sem="l1w")
        t_krs = em("g", lambda e: e.dma_start(
            out=b16v(QKS_O8 + 256, 64, 0, 16).rearrange("p (v j) -> p v j", v=2),
            in_=di["krsel"].rearrange("v p j -> p v j")), inc=True, sem="l1w")

        def wdq_(k, o0, on):
            return b16v(RA_O, 2048).rearrange("p (k o) -> p k o", k=8)[:, k, o0:o0 + on]

        def wdkv_(k, o0, on):
            return b16v(RA_O + 1024, 2048).rearrange("p (k o) -> p k o", k=8)[:, k, o0:o0 + on]

        def wqr_(k, o0, on):
            return b16v(RA_O + 2048, 1024).rearrange("p (k o) -> p k o", k=2)[:, k, o0:o0 + on]

        def qrsel_(m, v):
            return b16v(QKS_O8, 512).rearrange("p (m j) -> p m j", m=16)[:, 2 * m + v, :]

        def krsel_(v):
            return b16v(QKS_O8 + 256, 64, 0, 16).rearrange("p (v j) -> p v j", v=2)[:, v, :]

        prev_ev = None
        for dsti, (dst, wf, wt_) in enumerate(((cqt, wdq_, t_wdq), (ckvt, wdkv_, t_wdkv))):
            for oc in range(2):
                for tt in range(4):
                    hv = half(tt // 2, tt % 2)
                    for kc in range(8):
                        em("t", lambda e, _h=hv, _w=wf, _k=kc, _o=oc, _tt=tt,
                           _st=(kc == 0), _sp=(kc == 7):
                           e.matmul(_h, _w(_k, 128 * _o, 128), xnt(_k, 512 * _tt, 512),
                                    start=_st, stop=_sp),
                           inc=(kc == 7), waits=[wt_, prev_ev])
                    mmt = p.tok("t")
                    prev_ev = em("v", lambda e, _d=dst(oc, 512 * tt, 512), _s=hv:
                                 e.tensor_copy(_d, _s), inc=True, waits=[mmt], drain=True)
        cq_done = prev_ev
        for jc in range(4):
            for tt in range(4):
                hv = half(2 + tt // 2, tt % 2)
                for kc in range(2):
                    em("t", lambda e, _h=hv, _j=jc, _k=kc, _tt=tt,
                       _st=(kc == 0), _sp=(kc == 1):
                       e.matmul(_h, wqr_(_k, 128 * _j, 128), cqt(_k, 512 * _tt, 512),
                                start=_st, stop=_sp),
                       inc=(kc == 1), waits=[t_wqr, cq_done, prev_ev])
                mmt = p.tok("t")
                prev_ev = em("v", lambda e, _j=jc, _tt=tt, _s=hv: e.tensor_copy(
                    a16[:, 2 * QRP_O + 2048 * _j + 512 * _tt:
                        2 * QRP_O + 2048 * _j + 512 * _tt + 512], _s),
                    inc=True, waits=[mmt], drain=True)
        kx1 = lambda t0, tn: a16[0:16, 2 * RA_O + t0:2 * RA_O + t0 + tn]
        kx2 = lambda t0, tn: a16[0:16, 2 * (RA_O + 1024) + t0:2 * (RA_O + 1024) + t0 + tn]
        kb1 = f32v(RA_O + 2688, 512, 0, 16)
        kb2 = f32v(RA_O + 3200, 512, 0, 16)
        ktmp = a16[0:16, 2 * (RA_O + 3712):2 * (RA_O + 3712) + 512]
        cos16 = lambda t0, tn: f32v(TBL_O + t0, tn, 0, 16)
        sin16 = lambda t0, tn: f32v(TBL_O + 4096 + t0, tn, 0, 16)
        for tt in range(4):
            kp1 = PAIRS[0].ap()[0:16, 0:512]
            kp2 = PAIRS[0].ap()[0:16, 512:1024]
            for kc in range(8):
                em("t", lambda e, _k=kc, _tt=tt, _st=(kc == 0), _sp=(kc == 7):
                   e.matmul(kp1, b16v(RA_O + 2560, 256).rearrange("p (k o) -> p k o", k=8)
                            [:, _k, 0:16], xnt(_k, 512 * _tt, 512), start=_st, stop=_sp),
                   waits=[t_wkr, prev_ev])
                em("t", lambda e, _k=kc, _tt=tt, _st=(kc == 0), _sp=(kc == 7):
                   e.matmul(kp2, b16v(RA_O + 2560, 256).rearrange("p (k o) -> p k o", k=8)
                            [:, _k, 16:32], xnt(_k, 512 * _tt, 512), start=_st, stop=_sp),
                   inc=(kc == 7), waits=[])
            mmt = p.tok("t")
            e1 = em("v", lambda e, _d=kb1: e.tensor_copy(_d, kp1), inc=True,
                    waits=[mmt], drain=True)
            e2_ = em("v", lambda e, _d=kb2: e.tensor_copy(_d, kp2), inc=True, drain=True)
            em("v", lambda e, _tt=tt: e.tensor_tensor(
                out=kx1(512 * _tt, 512), in0=kb1, in1=cos16(512 * _tt, 512), op=OP.mult),
               waits=[("tbl", 64)], drain=True)
            em("v", lambda e, _tt=tt: e.tensor_tensor(
                out=ktmp, in0=kb2, in1=sin16(512 * _tt, 512), op=OP.mult), drain=True)
            em("v", lambda e, _tt=tt: e.tensor_tensor(
                out=kx1(512 * _tt, 512), in0=kx1(512 * _tt, 512), in1=ktmp, op=OP.subtract),
               drain=True)
            em("v", lambda e, _tt=tt: e.tensor_tensor(
                out=kx2(512 * _tt, 512), in0=kb1, in1=sin16(512 * _tt, 512), op=OP.mult),
               drain=True)
            em("v", lambda e, _tt=tt: e.tensor_tensor(
                out=ktmp, in0=kb2, in1=cos16(512 * _tt, 512), op=OP.mult), drain=True)
            prev_ev = em("v", lambda e, _tt=tt: e.tensor_tensor(
                out=kx2(512 * _tt, 512), in0=kx2(512 * _tt, 512), in1=ktmp, op=OP.add),
                inc=True, drain=True)
        kr_done = prev_ev
        def tmpf(b, t0=0, tn=T):
            o = 2 * (RA_O + (2688 if b == 0 else 3936)) + t0
            return a16[:, o:o + tn]

        def qrp_(c):
            return a16[:, 2 * QRP_O + 2048 * c:2 * QRP_O + 2048 * c + 2048]

        rope_last = prev_ev
        for c in range(2):
            cosv = f32v(TBL_O + 2048 * c, 2048)
            sinv = f32v(TBL_O + 4096 + 2048 * c, 2048)
            em("v", lambda e, _c=c, _cv=cosv: e.tensor_tensor(out=tmpf(0), in0=qrp_(_c),
                                                              in1=_cv, op=OP.mult),
               waits=[("tbl", 64)], drain=True)
            em("v", lambda e, _c=c, _sv=sinv: e.tensor_tensor(out=tmpf(1), in0=qrp_(_c + 2),
                                                              in1=_sv, op=OP.mult), drain=True)
            em("v", lambda e, _c=c: e.tensor_tensor(out=qrav(_c), in0=tmpf(0), in1=tmpf(1),
                                                    op=OP.subtract), drain=True)
            em("v", lambda e, _c=c, _sv=sinv: e.tensor_tensor(out=tmpf(0), in0=qrp_(_c),
                                                              in1=_sv, op=OP.mult), drain=True)
            em("v", lambda e, _c=c, _cv=cosv: e.tensor_tensor(out=tmpf(1), in0=qrp_(_c + 2),
                                                              in1=_cv, op=OP.mult), drain=True)
            rope_last = em("v", lambda e, _c=c: e.tensor_tensor(out=qrav(_c + 2), in0=tmpf(0),
                                                                in1=tmpf(1), op=OP.add),
                           inc=True, drain=True)
        prev_ev = rope_last
        for b in range(2):
            for tt in range(4):
                hv = PAIRS[1].ap()[64:96, 0:512]
                em("t", lambda e, _h=hv, _tt=tt: e.matmul(_h, krsel_(0), kx1(512 * _tt, 512),
                                                          start=True, stop=False),
                   waits=[kr_done, t_krs, prev_ev])
                em("t", lambda e, _h=hv, _tt=tt: e.matmul(_h, krsel_(1), kx2(512 * _tt, 512),
                                                          start=False, stop=True), inc=True)
                mmt = p.tok("t")
                prev_ev = em("v", lambda e, _b=b, _tt=tt, _s=hv: e.tensor_copy(
                    khv(_b, 64, 32, 512 * _tt, 512), _s), inc=True, waits=[mmt], drain=True)
        p.barrier()
        t_msk = em("g", lambda e: e.dma_start(out=b16v(MSK_O8, 2048).rearrange(
            "p (m j) -> p m j", m=4), in_=di["l1_masks"].rearrange("m p j -> p m j")),
            inc=True, sem=p.dsem("l1m"))
        selw8 = []
        selw8.append(em("v", lambda e: e.memset(f32v(SEL_O8, 64, 64, 1), 1.0), drain=True))
        selw8.append(em("v", lambda e: e.memset(f32v(SEL_O8 + 64, 64, 64, 1), 0.0), drain=True))
        selw8.append(em("v", lambda e: e.memset(f32v(SEL_O8 + 128, 64, 32, 1), 0.0), drain=True))
        selw8.append(em("v", lambda e: e.memset(f32v(SEL_O8 + 128 + 64, 64, 32, 1), 1.0),
                        drain=True))
        selw8.append(em("v", lambda e: e.memset(
            a16[:, 2 * V1L_O8:2 * V1L_O8 + 2080].rearrange("p (t c) -> p t c", c=130)
            [:, :, 64:65], 1.0), drain=True))
        selw8.append(em("v", lambda e: e.memset(
            a16[:, 2 * V1L_O8:2 * V1L_O8 + 2080].rearrange("p (t c) -> p t c", c=130)
            [:, :, 129:130], 1.0), drain=True))
        tok_sel8 = em("v", lambda e: e.nop(), inc=True)
        E64e = f32v(SEL_O8, 128, 64, 1)
        E64o = f32v(SEL_O8 + 128, 128, 32, 1)
        uq_use = p.tok("t")
        v1_use8 = p.tok("t")
        rc_free = p.tok("t")
        qh_free = [p.tok("t"), p.tok("t")]
        kh_free = [p.tok("t"), p.tok("t")]
        es_free = [None, None]
        head_done8 = p.tok("v")
        av_done_pair = p.tok("s")
        for pr_ in range(8):
            tuq = em("g", lambda e, _p=pr_: e.dma_start(
                out=b16v(UQ_O8, 256).rearrange("p (k o) -> p k o", k=2),
                in_=di["l1_wuq"][:, :, 128 * _p:128 * _p + 128]), inc=True,
                waits=[uq_use], sem=p.dsem("uqkv"))
            tuk = em("g", lambda e, _p=pr_: e.dma_start(
                out=b16v(UQ_O8 + 128, 256).rearrange("p (k o) -> p k o", k=2),
                in_=di["l1_wuk"][:, :, 128 * _p:128 * _p + 128]), inc=True, sem="uqkv")
            tuv = em("g", lambda e, _p=pr_: e.dma_start(
                out=b16v(UQ_O8 + 256, 256).rearrange("p (k o) -> p k o", k=2),
                in_=di["l1_wuv"][:, :, 128 * _p:128 * _p + 128]), inc=True, sem="uqkv")

            def uq8(w, k, o0, on):
                return b16v(UQ_O8 + 128 * w, 256).rearrange("p (k o) -> p k o", k=2)[:, k, o0:o0 + on]

            v1ev8 = []
            for ti in range(NT):
                hvv = PAIRS[2].ap()[:, 512 * (ti % 2):512 * (ti % 2) + 128]
                for kc in range(2):
                    em("t", lambda e, _h=hvv, _k=kc, _ti=ti, _st=(kc == 0), _sp=(kc == 1):
                       e.matmul(_h, ckvt(_k, 128 * _ti, 128), uq8(2, _k, 0, 128),
                                start=_st, stop=_sp),
                       inc=(kc == 1), waits=[tuv, v1_use8, head_done8, cq_done,
                                             (v1ev8[ti - 2] if ti >= 2 else None)])
                mmv = p.tok("t")
                ev = em("v", lambda e, _ti=ti,
                        _d=v1l(_ti0 := ti, 0, 0, 130).rearrange("p (a c) -> p a c", c=65)[:, :, 0:64],
                        _s=hvv.rearrange("p (a c) -> p a c", c=64):
                        e.tensor_copy(_d, _s), inc=True, waits=[mmv, tok_sel8], drain=True)
                v1ev8.append(ev)
            v1_ready8 = v1ev8[-1]
            for h2 in range(2):
                h = 2 * pr_ + h2
                b = h2
                asm_toks = []
                for which, wsel, dstf, freetok in ((0, 0, qhv, qh_free[b]),
                                                   (1, 1, khv, kh_free[b])):
                    for tt in range(4):
                        hv = PAIRS[3].ap()[0:64, 512 * (tt % 2):512 * (tt % 2) + 512]
                        for kc in range(2):
                            em("t", lambda e, _h=hv, _w=which, _k=kc, _h2=h2, _tt=tt,
                               _st=(kc == 0), _sp=(kc == 1):
                               e.matmul(_h, uq8(_w, _k, 64 * _h2, 64), cqt(_k, 512 * _tt, 512)
                                        if _w == 0 else ckvt(_k, 512 * _tt, 512),
                                        start=_st, stop=_sp),
                               inc=(kc == 1),
                               waits=[tuq, tuk, freetok, head_done8,
                                      (asm_toks[-2] if len(asm_toks) >= 2 else None)])
                        mmt = p.tok("t")
                        tk = em("v", lambda e, _b=b, _f=dstf, _tt=tt, _s=hv: e.tensor_copy(
                            _f(_b, 0, 64, 512 * _tt, 512), _s), inc=True, waits=[mmt],
                            drain=True)
                        asm_toks.append(tk)
                    if which == 0:
                        for tt in range(4):
                            hv = PAIRS[3].ap()[64:96, 512 * (tt % 2):512 * (tt % 2) + 512]
                            em("t", lambda e, _h=hv, _m=h % 8, _c=h // 8, _tt=tt:
                               e.matmul(_h, qrsel_(_m, 0), qrav(_c, 512 * _tt, 512),
                                        start=True, stop=False),
                               waits=[t_qrs, rope_last, qh_free[b], head_done8,
                                      (asm_toks[-2] if len(asm_toks) >= 2 else None)])
                            em("t", lambda e, _h=hv, _m=h % 8, _c=2 + h // 8, _tt=tt:
                               e.matmul(_h, qrsel_(_m, 1), qrav(_c, 512 * _tt, 512),
                                        start=False, stop=True), inc=True)
                            mmt = p.tok("t")
                            tk = em("v", lambda e, _b=b, _tt=tt, _s=hv: e.tensor_copy(
                                qhv(_b, 64, 32, 512 * _tt, 512), _s), inc=True,
                                waits=[mmt], drain=True)
                            asm_toks.append(tk)
                asm_done = asm_toks[-1]
                es_guard = v1_use8
                es_used = [None, None]
                av_toks = []
                esi = 0
                nd_stop_tok = [None] * 4
                for kc in range(NT):
                    jmin = kc // 4
                    sblks = []
                    for j in range(jmin, 4):
                        sp_i = (kc + j) % 2
                        sps = PAIRS[2].ap()[:, 512 * sp_i:512 * sp_i + 512]
                        em("t", lambda e, _o=sps, _b=b, _kc=kc, _j=j:
                           e.matmul(_o, khv(_b, 0, 96, 128 * _kc, 128),
                                    qhv(_b, 0, 96, 512 * _j, 512), start=True, stop=True),
                           inc=True,
                           waits=[asm_done, v1_ready8,
                                  (sblks[-2][1] if len(sblks) >= 2 else es_used[sp_i])])
                        smt = p.tok("t")
                        if j == kc // 4:
                            ct = em("v", lambda e, _o=sps, _i=kc % 4: e.scalar_tensor_tensor(
                                out=_o, in0=_o, scalar=-80.0,
                                in1=a16[:, 2 * (MSK_O8 + 256 * _i):2 * (MSK_O8 + 256 * _i) + 512],
                                op0=OP.max, op1=OP.min), inc=True,
                                waits=[smt, t_msk], drain=True)
                        else:
                            ct = em("v", lambda e, _o=sps: e.tensor_scalar(
                                out=_o, in0=_o, scalar1=80.0, scalar2=-80.0,
                                op0=OP.min, op1=OP.max), inc=True, waits=[smt], drain=True)
                        eb = esi % 2
                        et = em("s", lambda e, _d=esb(eb), _s=sps: e.activation(
                            out=_d, in_=_s, func=AF.Exp, bias=0.0, scale=SCALE_L1),
                            inc=True,
                            waits=[ct, (av_toks[-2][1] if len(av_toks) >= 2 else es_guard)],
                            drain=True)
                        es_used[0 if False else (esi % 2)] = et
                        sblks.append((j, et))
                        esi += 1
                        pj, hj = (j // 2, j % 2)
                        if h2 == 0:
                            outp = PAIRS[pj].ap()[0:65, 512 * hj:512 * hj + 512]
                            avt = em("t", lambda e, _o=outp, _ti=kc, _h2=h2, _eb=eb,
                                     _st=(kc == 0), _sp=(kc == 4 * j + 3):
                                     e.matmul(_o, v1l(_ti, _h2), esb(_eb), start=_st, stop=_sp),
                                     inc=True, waits=[et, av_done_pair])
                        else:
                            outp = PAIRS[pj].ap()[64:128, 512 * hj:512 * hj + 512]
                            em("t", lambda e, _o=outp, _ti=kc, _h2=h2, _eb=eb,
                               _st=(kc == 0), _sp=(kc == 4 * j + 3):
                               e.matmul(_o, v1l(_ti, _h2, 0, 64), esb(_eb), start=_st,
                                        stop=_sp), waits=[et, av_done_pair])
                            dout = PAIRS[pj].ap()[32:33, 512 * hj:512 * hj + 512]
                            avt = em("t", lambda e, _o=dout, _ti=kc, _h2=h2, _eb=eb,
                                     _st=(kc == 0), _sp=(kc == 4 * j + 3):
                                     e.matmul(_o, v1l(_ti, _h2, 64, 1), esb(_eb),
                                              start=_st, stop=_sp), inc=True)
                        av_toks.append((j, avt))
                        if kc == 4 * j + 3:
                            nd_stop_tok[j] = avt
                last_av = av_toks[-1][1]
                v1_use8 = last_av
                uq_use = last_av
                hv_toks = []
                for j in range(4):
                    pj, hj = (j // 2, j % 2)
                    ndt = nd_stop_tok[j]
                    if h2 == 0:
                        nsrc = PAIRS[pj].ap()[0:64, 512 * hj:512 * hj + 512]
                        dsrc = PAIRS[pj].ap()[64:65, 512 * hj:512 * hj + 512]
                        rcp = 64
                        esel = E64e
                    else:
                        nsrc = PAIRS[pj].ap()[64:128, 512 * hj:512 * hj + 512]
                        dsrc = PAIRS[pj].ap()[32:33, 512 * hj:512 * hj + 512]
                        rcp = 32
                        esel = E64o
                    ev = em("v", lambda e, _d=atn1(pr_, 512 * j, 512, 64 * h2, 64), _s=nsrc:
                            e.tensor_copy(_d, _s), inc=True, waits=[ndt], drain=True)
                    rc = f32v(RC_O8, 512, rcp, 1)
                    em("s", lambda e, _d=rc, _s=dsrc: e.activation(
                        out=_d, in_=_s, func=AF.Ln, bias=0.0, scale=1.0),
                       waits=[ndt, rc_free], drain=True)
                    t2 = em("s", lambda e, _d=rc: e.activation(
                        out=_d, in_=_d, func=AF.Exp, bias=0.0, scale=-1.0),
                        inc=True, drain=True)
                    rtok = p.tok("s")
                    bcp = PAIRS[pj].ap()[:, 512 * hj:512 * hj + 512]
                    mt = em("t", lambda e, _o=bcp, _sl=esel, _r=rc: e.matmul(
                        _o, _sl, _r, start=True, stop=True),
                        inc=True, waits=[rtok, tok_sel8, ev])
                    bct = p.tok("t")
                    rc_free = bct
                    t4 = em("v", lambda e, _a=atn1(pr_, 512 * j, 512, 64 * h2, 64),
                            _b=bcp[64 * h2:64 * h2 + 64, :]:
                            e.tensor_tensor(out=_a, in0=_a, in1=_b, op=OP.mult),
                            inc=True, waits=[bct], drain=True)
                    hv_toks.append(t4)
                head_done8 = hv_toks[-1]
                qh_free[b] = last_av
                kh_free[b] = last_av
                av_done_pair = p.tok("s")
        wo_use8 = [head_done8, head_done8]
        fold_evs8 = []
        for grp in range(4):
            for kc in range(8):
                wb = kc % 2
                wo_dst = b16v(QH_O if wb == 0 else KH_O, 2048)[:, 0:1024] if False else \
                    a16[:, 2 * (QH_O if wb == 0 else KH_O):2 * (QH_O if wb == 0 else KH_O) + 1024]
                wot = em("g", lambda e, _d=wo_dst, _k=kc: e.dma_start(out=_d, in_=di["l1_wo"][_k]),
                         inc=True, waits=[wo_use8[wb], head_done8], sem=p.dsem("wo8%d" % wb))
                for t_ in range(4):
                    ti = 4 * grp + t_
                    pv = PAIRS[t_].ap()
                    for hh in range(2):
                        em("t", lambda e, _h=half(t_, hh), _k=kc, _ti=ti, _hh=hh, _w=wo_dst,
                           _st=(kc == 0), _sp=(kc == 7):
                           e.matmul(_h, atn1(_k, 128 * _ti, 128),
                                    _w[:, 512 * _hh:512 * _hh + 512], start=_st, stop=_sp),
                           inc=((kc == 7 and hh == 1) or (t_ == 3 and hh == 1)),
                           waits=[wot, head_done8,
                                  (fold_evs8[4 * (grp - 1) + t_] if (grp > 0 and kc == 0) else None)])
                    if kc == 7:
                        mmt = p.tok("t")
                        ev = em("v", lambda e, _x=XTt(ti), _s=pv:
                                e.tensor_tensor(out=_x, in0=_x, in1=_s, op=OP.add),
                                inc=True, waits=[mmt], drain=True)
                        fold_evs8.append(ev)
                wo_use8[wb] = p.tok("t")
        p.barrier()
    p.views["atn1"] = (a16[:, 2 * ATN1_O:2 * ATN1_O + 16384], (128, 16384))
    p.views["cq1"] = (a16[:, 2 * CQT_O:2 * CQT_O + 4096], (128, 4096))
    p.views["qh"] = (a16[:, 2 * QH_O:2 * QH_O + 4096], (128, 4096))
    p.views["kh"] = (a16[:, 2 * KH_O:2 * KH_O + 4096], (128, 4096))
    p.views["v1l8"] = (a16[:, 2 * V1L_O8:2 * V1L_O8 + 4160], (128, 4160))
    p.views["ckv1"] = (a16[:, 2 * CKV_O:2 * CKV_O + 4096], (128, 4096))
    p.views["qra"] = (a16[:, 2 * QRA_O:2 * QRA_O + 8192], (128, 8192))
    p.views["kr1"] = (a16[:, 2 * RA_O:2 * RA_O + 4096], (128, 4096))

    if PHASES >= 7:
        H1C_O = FT_O
        CNS_O = FT_O + 8192
        CN3_O = calloc(512)
        CN4_O = calloc(2)
        TQ_O = calloc(256)
        assert CNS_O + 1024 <= AREN
        t_cn3 = em("g", lambda e: e.dma_start(
            out=b16v(CN3_O, 1024).rearrange("p (k o) -> p k o", k=4), in_=di["cn_w3"]),
            inc=True, sem=p.dsem("cnw"))
        t_cn4 = em("g", lambda e: e.dma_start(
            out=b16v(CN4_O, 2).rearrange("p (k o) -> p k o", k=2), in_=di["cn_w4"]),
            inc=True, sem="cnw")
        pb16 = [PAIRS[q].ap().bitcast(BF16) for q in range(4)]
        tr_ev = None
        for i in range(NT):
            pq = i % 2
            hv16 = pb16[pq][:, 0:1024]
            last = None
            for c in range(8):
                last = em("t", lambda e, _o=hv16[:, 128 * c:128 * c + 128],
                          _s=xnt(c, 128 * i, 128), _st=(c == 0), _sp=(c == 7):
                          e.matmul(_o, _s, identbv, is_transpose=True, start=_st, stop=_sp),
                          inc=(c == 7), waits=[tok_idb, tr_ev])
            tr_ev = em("v", lambda e, _d=acc_t(i), _s=hv16: e.tensor_copy(_d, _s),
                       inc=True, waits=[last], drain=True)
        p.barrier()
        for cyc in range(3):
            if cyc == 0:
                def outt(c, t0=0, tn=T):
                    return xnt(c, t0, tn)
            else:
                def outt(c, t0=0, tn=T):
                    return xnt(c, t0, tn)
                tr_ev = None
                for i in range(NT):
                    pq = i % 2
                    hv = half(pq, 0)
                    last = None
                    for g in range(2):
                        hvg = half(pq, g)
                        for q in range(4):
                            c = 4 * g + q
                            last = em("t", lambda e, _o=hvg[:, 128 * q:128 * q + 128],
                                      _s=f32v(ACC_O + 1024 * i + 128 * c, 128),
                                      _st=(q == 0), _sp=(q == 3):
                                      e.matmul(_o, _s, identv, is_transpose=True,
                                               start=_st, stop=_sp),
                                      inc=(q == 3), waits=[tok_ident, tr_ev])
                        dst = xnt_all[:, 4 * g:4 * g + 4, 128 * i:128 * i + 128]
                        tr_ev = em("v", lambda e, _d=dst,
                                   _s=hvg.rearrange("p (c t) -> p c t", t=128):
                                   e.tensor_copy(_d, _s), inc=True, waits=[last], drain=True)
                p.barrier()
            cn_use = [p.tok("t"), p.tok("t")]
            prev_ev = None
            for oc in range(8):
                cb = oc % 2
                w1t = em("g", lambda e, _b=cb, _o=oc: e.dma_start(
                    out=b16v(CNS_O + 512 * _b, 1024).rearrange("p (k j) -> p k j", k=8),
                    in_=di["cn_w1"][_o]), inc=True, waits=[cn_use[cb]], sem=p.dsem("cn%d" % cb))
                for tt in range(4):
                    hv = half(2 + tt // 2, tt % 2)
                    for kc in range(8):
                        em("t", lambda e, _h=hv, _b=cb, _k=kc, _tt=tt,
                           _st=(kc == 0), _sp=(kc == 7):
                           e.matmul(_h, b16v(CNS_O + 512 * _b, 1024)
                                    .rearrange("p (k j) -> p k j", k=8)[:, _k, :],
                                    outt(_k, 512 * _tt, 512), start=_st, stop=_sp),
                           inc=(kc == 7), waits=[w1t, prev_ev])
                    mmt = p.tok("t")
                    cn_use[cb] = mmt
                    prev_ev = em("s", lambda e, _o=oc, _tt=tt, _sv=hv,
                                 _b=f32v(CNB_O + oc, 1), _d=a16[:, 2 * H1C_O + 2048 * oc + 512 * tt:
                                                               2 * H1C_O + 2048 * oc + 512 * tt + 512]:
                                 e.activation(out=_d, in_=_sv, func=AF.Gelu, bias=_b, scale=1.0),
                                 inc=True, waits=[mmt, tok_cnb], drain=True)
            h1_done = prev_ev

            def h1ct(c, t0=0, tn=T):
                return a16[:, 2 * H1C_O + 2048 * c + t0:2 * H1C_O + 2048 * c + t0 + tn]

            for oc in range(4):
                cb = oc % 2
                w2t = em("g", lambda e, _b=cb, _o=oc: e.dma_start(
                    out=b16v(CNS_O + 512 * _b, 1024).rearrange("p (k j) -> p k j", k=8),
                    in_=di["cn_w2"][_o]), inc=True, waits=[cn_use[cb], h1_done],
                    sem="cn%d" % cb)
                for tt in range(4):
                    hv = half(2 + tt // 2, tt % 2)
                    for kc in range(8):
                        em("t", lambda e, _h=hv, _b=cb, _k=kc, _tt=tt,
                           _st=(kc == 0), _sp=(kc == 7):
                           e.matmul(_h, b16v(CNS_O + 512 * _b, 1024)
                                    .rearrange("p (k j) -> p k j", k=8)[:, _k, :],
                                    h1ct(_k, 512 * _tt, 512), start=_st, stop=_sp),
                           inc=(kc == 7), waits=[w2t, h1_done, prev_ev])
                    mmt = p.tok("t")
                    cn_use[cb] = mmt
                    prev_ev = em("s", lambda e, _o=oc, _tt=tt, _sv=hv,
                                 _b=f32v(CNB_O + 8 + oc, 1),
                                 _d=xnt(oc, 512 * tt, 512):
                                 e.activation(out=_d, in_=_sv, func=AF.Gelu, bias=_b, scale=1.0),
                                 inc=True, waits=[mmt], drain=True)
            h2_done = prev_ev
            for oc in range(2):
                for tt in range(4):
                    hv = half(2 + tt // 2, tt % 2)
                    for kc in range(4):
                        em("t", lambda e, _h=hv, _k=kc, _o=oc, _tt=tt,
                           _st=(kc == 0), _sp=(kc == 3):
                           e.matmul(_h, b16v(CN3_O, 1024).rearrange("p (k o) -> p k o", k=4)
                                    [:, _k, 128 * _o:128 * _o + 128],
                                    xnt(_k, 512 * _tt, 512), start=_st, stop=_sp),
                           inc=(kc == 3), waits=[t_cn3, h2_done, prev_ev])
                    mmt = p.tok("t")
                    prev_ev = em("s", lambda e, _o=oc, _tt=tt, _sv=hv,
                                 _b=f32v(CNB_O + 12 + oc, 1),
                                 _d=xnt(4 + oc, 512 * tt, 512):
                                 e.activation(out=_d, in_=_sv, func=AF.Gelu, bias=_b, scale=1.0),
                                 inc=True, waits=[mmt], drain=True)
            h3_done = prev_ev
            tfps = PAIRS[0].ap()[:, 0:16]
            for ti in range(NT):
                for kc in range(2):
                    em("t", lambda e, _ti=ti, _k=kc, _st=(kc == 0), _sp=(kc == 1):
                       e.matmul(tfps[:, _ti:_ti + 1],
                                xnt(4 + _k, 128 * _ti, 128),
                                b16v(CN4_O, 2).rearrange("p (k o) -> p k o", k=2)[:, _k, :],
                                start=_st, stop=_sp),
                       inc=(kc == 1 and ti == 15), waits=[t_cn4, h3_done])
            mmt = p.tok("t")
            tfe = em("s", lambda e: e.activation(out=f32v(COL_O + 9, 16), in_=tfps,
                                                 func=AF.Sigmoid, bias=f32v(COL_O + 2, 1),
                                                 scale=1.0), inc=True, waits=[mmt], drain=True)
            em("v", lambda e: e.tensor_scalar(out=f32v(COL_O + 25, 16), in0=f32v(COL_O + 9, 16),
                                              scalar1=f32v(COL_O + 1, 1), scalar2=None,
                                              op0=OP.mult), waits=[tfe], drain=True)
            em("v", lambda e: e.tensor_scalar(out=f32v(COL_O + 25, 16), in0=f32v(COL_O + 25, 16),
                                              scalar1=f32v(COL_O + 0, 1), scalar2=None,
                                              op0=OP.add), drain=True)
            cpe = em("v", lambda e: e.tensor_scalar(out=f32v(COL_O + 41, 16),
                                                    in0=f32v(COL_O + 25, 16),
                                                    scalar1=-1.0, scalar2=None, op0=OP.mult),
                     inc=True, drain=True)
            tw_last = None
            for ti in range(NT):
                cp = f32v(COL_O + 25 + ti, 1)
                ncp = f32v(COL_O + 41 + ti, 1)
                base = ACC_O + 1024 * ti

                em("v", lambda e, _ti=ti, _c=cp, _b=base: e.scalar_tensor_tensor(
                    out=f32v(TQ_O, 256), in0=f32v(_b + 512, 256), scalar=_c,
                    in1=f32v(_b, 256), op0=OP.mult, op1=OP.add), waits=[cpe, tw_last],
                   drain=True)
                em("v", lambda e, _ti=ti, _n=ncp, _b=base: e.scalar_tensor_tensor(
                    out=f32v(_b + 512, 256), in0=f32v(_b, 256), scalar=_n,
                    in1=f32v(_b + 512, 256), op0=OP.mult, op1=OP.add), drain=True)
                em("v", lambda e, _b=base: e.tensor_copy(f32v(_b, 256), f32v(TQ_O, 256)),
                   drain=True)
                em("v", lambda e, _ti=ti, _n=ncp, _b=base: e.scalar_tensor_tensor(
                    out=f32v(TQ_O, 256), in0=f32v(_b + 768, 256), scalar=_n,
                    in1=f32v(_b + 256, 256), op0=OP.mult, op1=OP.add), drain=True)
                em("v", lambda e, _ti=ti, _c=cp, _b=base: e.scalar_tensor_tensor(
                    out=f32v(_b + 768, 256), in0=f32v(_b + 256, 256), scalar=_c,
                    in1=f32v(_b + 768, 256), op0=OP.mult, op1=OP.add), drain=True)
                tw_last = em("v", lambda e, _b=base: e.tensor_copy(
                    f32v(_b + 256, 256), f32v(TQ_O, 256)), inc=True, drain=True)
            p.barrier()
        for i in range(NT):
            em("v", lambda e, _i=i: e.tensor_tensor(out=XTt(_i), in0=XTt(_i), in1=acc_t(_i),
                                                    op=OP.add), inc=(i == 15), drain=True)
        p.barrier()

    st = em("y", lambda e: e.dma_start(out=out_h.rearrange("(i p) c -> p i c", p=128), in_=xt3),
            inc=True, waits=[p.tok("v"), p.tok("s"), p.tok("t")], sem=p.dsem("st"))
    em("y", lambda e: e.nop(), waits=[st])

    for name in dbg:
        view, shp = p.views[name]
        dh = nc.dram_tensor("dbg_" + name, list(shp),
                            F32 if view.tensor.dtype == F32 else F32, kind="ExternalOutput").ap()
        qe = "y" if view.dtype == F32 else "g"
        dt_ = em(qe, lambda e, _v=view, _h=dh: e.dma_start(out=_h, in_=_v), inc=True,
                 sem=p.dsem("dbg" + qe))
        em(qe, lambda e: e.nop(), waits=[dt_])

    with nc.Block() as block:
        @block.tensor
        def _(e):
            for fn in p.q["t"]:
                fn(e)

        @block.vector
        def _(e):
            for fn in p.q["v"]:
                fn(e)

        @block.scalar
        def _(e):
            for fn in p.q["s"]:
                fn(e)

        @block.gpsimd
        def _(e):
            for fn in p.q["g"]:
                fn(e)

        @block.sync
        def _(e):
            for fn in p.q["y"]:
                fn(e)

    stack.close()
    return nc


def kernel(**inputs):
    common = _prep_common(inputs)
    x = np.asarray(inputs["L0_attn_x"], np.float32)
    nc = build_nc(dbg=tuple(DBG))
    in_maps = []
    for b in range(8):
        m = dict(common)
        m["x"] = np.ascontiguousarray(x[b])
        in_maps.append(m)
    res = run_bass_kernel_spmd(nc, in_maps, core_ids=list(range(8)))
    return np.stack([res.results[b]["out"] for b in range(8)], axis=0)


# version 02
# speedup vs baseline: 1.0302x; 1.0302x over previous
import numpy as np
import ml_dtypes
import concourse.bass as bass
import concourse.mybir as mybir
from concourse.bass_utils import run_bass_kernel_spmd
from contextlib import ExitStack

F32 = mybir.dt.float32
BF16 = mybir.dt.bfloat16
AF = mybir.ActivationFunctionType
OP = mybir.AluOpType
AX = mybir.AxisListType

T = 2048
D = 1024
NT = 16
NE = 8
EPS_LN = 1e-5
EPS_MOE_LN = 1e-12
PHI_EPS = 6.25e-8
PHI_BIAS = -2.772588722239781
INV_S = 0.3535533905932738
CQ = -0.17677669529663687
SCALE_L1 = 0.10206207261596575

PHASES = 99
DEBUG_SERIAL = False
DBG = []



def _prep_common(i):
    d = {}
    cc = np.ascontiguousarray

    def nat(w, kcn):
        w = np.asarray(w, np.float32)
        return cc(w.reshape(kcn, 128, w.shape[1]).transpose(1, 0, 2))

    def ocsl(w, kcn, ocn):
        w = np.asarray(w, np.float32)
        return cc(w.reshape(kcn, 128, ocn, 128).transpose(2, 0, 1, 3))

    Wq = np.asarray(i["L0_attn_Wq"], np.float32)
    Wk = np.asarray(i["L0_attn_Wk"], np.float32)
    Wv = np.asarray(i["L0_attn_Wv"], np.float32)
    Wo = np.asarray(i["L0_attn_Wo"], np.float32)
    d["l0_wqT"] = cc(Wq.reshape(8, 128, 8, 128).transpose(0, 2, 3, 1))
    d["l0_wkT"] = cc(Wk.reshape(8, 128, 8, 128).transpose(0, 2, 3, 1))
    d["l0_wvT"] = cc(Wv.T.reshape(8, 128, 1024))
    d["l0_woT"] = cc(Wo.T.reshape(8, 128, 1024))
    d["l0_bqk"] = cc(np.stack([np.asarray(i["L0_attn_bq"], np.float32).reshape(8, 128).T,
                               np.asarray(i["L0_attn_bk"], np.float32).reshape(8, 128).T]))
    d["l0_bv"] = cc(np.asarray(i["L0_attn_bv"], np.float32)[None, :])
    d["l0_bo"] = cc(np.asarray(i["L0_attn_bo"], np.float32)[None, :])
    d["l0_orfT"] = cc(np.asarray(i["L0_attn_orf"], np.float32).T)

    Wr = np.asarray(i["L0_ffn_Wr"], np.float32)
    d["moe_wrT"] = cc(Wr.T.reshape(8, 128, 8))
    d["moe_br"] = cc(np.asarray(i["L0_ffn_br"], np.float32)[None, :])
    W1 = np.asarray(i["L0_ffn_W1"], np.float32)
    d["moe_w1"] = cc(W1.reshape(NE, 32, 128, 8, 128).transpose(0, 1, 4, 3, 2))
    W2 = np.asarray(i["L0_ffn_W2"], np.float32)
    d["moe_w2"] = cc(W2.transpose(0, 2, 1).reshape(NE, 32, 128, 1024))
    d["moe_b1"] = cc(np.asarray(i["L0_ffn_b1"], np.float32).reshape(NE * 32, 128).T)
    d["moe_b2"] = cc(np.asarray(i["L0_ffn_b2"], np.float32))
    d["moe_lnw"] = cc(np.asarray(i["L0_ffn_ln_w"], np.float32)[None, :])
    d["moe_lnb"] = cc(np.asarray(i["L0_ffn_ln_b"], np.float32)[None, :])

    d["l1_wdq"] = nat(i["L1_attn_W_DQ"], 8)
    d["l1_wdkv"] = nat(i["L1_attn_W_DKV"], 8)
    d["l1_wuq"] = nat(i["L1_attn_W_UQ"], 2)
    d["l1_wuk"] = nat(i["L1_attn_W_UK"], 2)
    d["l1_wuv"] = nat(i["L1_attn_W_UV"], 2)
    WQR = np.asarray(i["L1_attn_W_QR"], np.float32)
    perm = np.concatenate([np.arange(0, 32, 2) + 32 * h for h in range(16)] +
                          [np.arange(1, 32, 2) + 32 * h for h in range(16)])
    d["l1_wqr"] = nat(WQR[:, perm], 2)
    WKR = np.asarray(i["L1_attn_W_KR"], np.float32)
    permk = np.concatenate([np.arange(0, 32, 2), np.arange(1, 32, 2)])
    d["l1_wkr"] = nat(WKR[:, permk], 8)
    d["l1_wo"] = cc(np.asarray(i["L1_attn_W_O"], np.float32).reshape(8, 128, 1024))

    d["cn_w1"] = ocsl(i["L1_ffn_cn_w1"], 8, 8)
    d["cn_w2"] = ocsl(i["L1_ffn_cn_w2"], 8, 4)
    d["cn_w3"] = nat(i["L1_ffn_cn_w3"], 4)
    d["cn_w4"] = nat(i["L1_ffn_cn_w4"], 2)
    d["gc_w1"] = nat(i["L1_ffn_gc_w1"], 8)
    d["gc_w2"] = nat(i["L1_ffn_gc_w2"], 4)
    d["gc_w3"] = nat(i["L1_ffn_gc_w3"], 2)
    d["cn_b1"] = cc(np.asarray(i["L1_ffn_cn_b1"], np.float32).reshape(8, 128).T)
    d["cn_b2"] = cc(np.asarray(i["L1_ffn_cn_b2"], np.float32).reshape(4, 128).T)
    d["cn_b3"] = cc(np.asarray(i["L1_ffn_cn_b3"], np.float32).reshape(2, 128).T)
    d["cn_b4"] = cc(np.asarray(i["L1_ffn_cn_b4"], np.float32)[None, :])
    d["gc_b1"] = cc(np.asarray(i["L1_ffn_gc_b1"], np.float32).reshape(4, 128).T)
    d["gc_b2"] = cc(np.asarray(i["L1_ffn_gc_b2"], np.float32).reshape(2, 128).T)
    d["gc_b3"] = cc(np.asarray(i["L1_ffn_gc_b3"], np.float32)[None, :])
    d["l1_ar"] = cc(np.asarray(i["L1_ffn_adaptive_range"], np.float32)[None, :])

    d["ident"] = np.eye(128, dtype=np.float32)
    e2 = np.zeros((128, 2), np.float32)
    e2[:64, 0] = 1.0
    e2[64:, 1] = 1.0
    d["e2"] = e2
    inv = 10000.0 ** (-np.arange(0, 32, 2, dtype=np.float32) / 32.0)
    ang = np.arange(T, dtype=np.float32)[:, None] * inv[None, :]
    d["cos_rep"] = cc(np.tile(np.cos(ang).T, (16, 1)).reshape(2, 128, T))
    d["sin_rep"] = cc(np.tile(np.sin(ang).T, (16, 1)).reshape(2, 128, T))
    qrs = np.zeros((8, 2, 128, 32), np.float32)
    for m_ in range(8):
        for j_ in range(16):
            qrs[m_, 0, 16 * m_ + j_, j_] = 1.0
            qrs[m_, 1, 16 * m_ + j_, 16 + j_] = 1.0
    d["qrsel"] = cc(qrs)
    krs = np.zeros((2, 16, 32), np.float32)
    for j_ in range(16):
        krs[0, j_, j_] = 1.0
        krs[1, j_, 16 + j_] = 1.0
    d["krsel"] = cc(krs)
    cq2 = np.zeros((2, 2, 128), np.float32)
    cq2[0, 0, :] = CQ; cq2[1, 1, :] = CQ
    d["cqsel"] = cq2
    ks2 = np.zeros((2, 2, 256), np.float32)
    ks2[0, 0, :] = CQ; ks2[1, 1, :] = CQ
    d["ksel"] = ks2
    masks = np.empty((4, 128, 512), np.float32)
    kk = np.arange(128)[:, None]
    qq = np.arange(512)[None, :]
    for mi in range(4):
        masks[mi] = np.where(kk <= qq - 128 * mi, 80.0, -1e9)
    d["l1_masks"] = masks
    return d



ENGS = ("t", "v", "s", "g", "y")


class Prog:

    def __init__(self, nc, stack):
        self.nc = nc
        self.stack = stack
        self.q = {k: [] for k in ENGS}
        self.cnt = {k: 0 for k in ENGS}
        self.step = {k: (16 if k in ("g", "y") else 1) for k in ENGS}
        self.waited = {k: {} for k in ENGS}
        self.sem = {k: stack.enter_context(nc.semaphore("sem_" + k)) for k in ENGS}
        self.isdma = {k: False for k in ENGS}
        self.views = {}

    def dsem(self, name):
        if name not in self.sem:
            self.sem[name] = self.stack.enter_context(self.nc.semaphore("sem_" + name))
            self.cnt[name] = 0
            self.step[name] = 16
            self.isdma[name] = True
        return name

    def em(self, eng, mk, inc=False, waits=(), drain=False, sem=None):
        wl = []
        wd = self.waited[eng]
        for tk in waits:
            if tk is None:
                continue
            pe, v = tk
            if pe == eng:
                continue
            if self.isdma.get(pe, False):
                v = self.cnt[pe]
            if v > 0 and wd.get(pe, 0) < v:
                wd[pe] = v
                wl.append((pe, v))
        sname = sem if sem is not None else eng
        if inc:
            self.cnt[sname] += self.step[sname]
        tok = (sname, self.cnt[sname])
        sems = self.sem

        def fn(e, _w=tuple(wl), _mk=mk, _inc=inc, _dr=drain, _sem=sems[sname],
               _stp=self.step[sname]):
            for (pe, v) in _w:
                e.wait_ge(sems[pe], v)
            if _dr:
                e.drain()
            ins = _mk(e)
            if _inc:
                ins.then_inc(_sem, _stp)
        self.q[eng].append(fn)
        return tok

    def tok(self, eng):
        return (eng, self.cnt[eng])

    def bsem(self, name):
        if name not in self.sem:
            self.sem[name] = self.stack.enter_context(self.nc.semaphore("sem_" + name))
            self.cnt[name] = 0
            self.step[name] = 1
            self.isdma[name] = False
        return name

    def barrier(self):
        toks = []
        for k in ENGS:
            toks.append(self.em(k, lambda e: e.nop(), inc=True, sem=self.bsem("b" + k)))
        allsems = [(s, self.cnt[s]) for s in list(self.cnt)
                   if self.cnt[s] > 0 and not s.startswith("b")]
        for k in ENGS:
            self.em(k, lambda e: e.nop(),
                    waits=[t for t, kk in zip(toks, ENGS) if kk != k] + allsems)


def build_nc(dbg=()):
    nc = bass.Bass("TRN2", target_bir_lowering=False)
    stack = ExitStack()
    p = Prog(nc, stack)
    em = p.em

    di = {}

    def inp(name, shape):
        di[name] = nc.dram_tensor(name, list(shape), F32, kind="ExternalInput").ap()

    inp("x", (T, D))
    inp("l0_wqT", (8, 8, 128, 128)); inp("l0_wkT", (8, 8, 128, 128))
    inp("l0_wvT", (8, 128, 1024)); inp("l0_woT", (8, 128, 1024))
    inp("l0_bqk", (2, 128, 8)); inp("l0_bv", (1, 1024)); inp("l0_bo", (1, 1024))
    inp("l0_orfT", (64, 256))
    inp("moe_wrT", (8, 128, 8)); inp("moe_br", (1, 8))
    inp("moe_w1", (NE, 32, 128, 8, 128)); inp("moe_w2", (NE, 32, 128, 1024))
    inp("moe_b1", (128, 256)); inp("moe_b2", (8, 1024))
    inp("moe_lnw", (1, 1024)); inp("moe_lnb", (1, 1024))
    inp("l1_wdq", (128, 8, 256)); inp("l1_wdkv", (128, 8, 256))
    inp("l1_wuq", (128, 2, 1024)); inp("l1_wuk", (128, 2, 1024)); inp("l1_wuv", (128, 2, 1024))
    inp("l1_wqr", (128, 2, 512)); inp("l1_wkr", (128, 8, 32))
    inp("l1_wo", (8, 128, 1024))
    inp("cn_w1", (8, 8, 128, 128)); inp("cn_w2", (4, 8, 128, 128))
    inp("cn_w3", (128, 4, 256)); inp("cn_w4", (128, 2, 1))
    inp("gc_w1", (128, 8, 512)); inp("gc_w2", (128, 4, 256)); inp("gc_w3", (128, 2, 1))
    inp("cn_b1", (128, 8)); inp("cn_b2", (128, 4)); inp("cn_b3", (128, 2)); inp("cn_b4", (1, 1))
    inp("gc_b1", (128, 4)); inp("gc_b2", (128, 2)); inp("gc_b3", (1, 1)); inp("l1_ar", (1, 1))
    inp("ident", (128, 128)); inp("e2", (128, 2))
    inp("cos_rep", (2, 128, T)); inp("sin_rep", (2, 128, T))
    inp("cqsel", (2, 2, 128)); inp("ksel", (2, 2, 256))
    inp("qrsel", (8, 2, 128, 32)); inp("krsel", (2, 16, 32))
    inp("l1_masks", (4, 128, 512))
    out_h = nc.dram_tensor("out", [T, D], F32, kind="ExternalOutput").ap()

    AREN = 52352
    _B = 16896
    xtf = nc.alloc_sbuf_tensor_at("xtf", [128, 16384], F32, offset=_B)
    xtb = nc.alloc_sbuf_tensor_at("xtb", [128, 32768], BF16, offset=_B)
    xnb = nc.alloc_sbuf_tensor_at("xnb", [128, 16384], BF16, offset=_B + 65536)
    cstf = nc.alloc_sbuf_tensor_at("cstf", [128, 2048], F32, offset=_B + 98304)
    cstb = nc.alloc_sbuf_tensor_at("cstb", [128, 4096], BF16, offset=_B + 98304)
    scrf = nc.alloc_sbuf_tensor_at("scrf", [128, 25728], F32, offset=_B + 106496)
    scrb = nc.alloc_sbuf_tensor_at("scrb", [128, 51456], BF16, offset=_B + 106496)

    class _Rt:
        def __init__(self, regions):
            self.regions = regions

        def __getitem__(self, key):
            pk, ck = key
            c0 = ck.start if ck.start is not None else 0
            c1 = ck.stop
            for (s, e, t) in self.regions:
                if c0 >= s and c1 <= e:
                    return t.ap()[pk, c0 - s:c1 - s]
            raise AssertionError("unrouted sbuf slice [%d:%d)" % (c0, c1))

    a32 = _Rt([(0, 16384, xtf), (24576, 26624, cstf), (26624, 52352, scrf)])
    a16 = _Rt([(0, 32768, xtb), (32768, 49152, xnb), (49152, 53248, cstb),
               (53248, 104704, scrb)])

    def f32v(off, n, p0=0, pn=128):
        return a32[p0:p0 + pn, off:off + n]

    def b16v(off, nb, p0=0, pn=128):
        return a16[p0:p0 + pn, 2 * off:2 * off + nb]

    XT_O = 0
    XNT_O = 16384
    CONST_O = 24576
    ACC_O = 26624
    FT_O = ACC_O + 16384
    assert FT_O + 9344 == AREN

    def XTt(i):
        return f32v(XT_O + 1024 * i, 1024)

    def acc_t(i):
        return f32v(ACC_O + 1024 * i, 1024)

    def xnt(c, t0=0, tn=T):
        o = 2 * XNT_O + 2048 * c + t0
        return a16[:, o:o + tn]

    xnt_all = a16[:, 2 * XNT_O:2 * XNT_O + 16384].rearrange("p (c t) -> p c t", t=2048)
    xt3 = a32[:, XT_O:XT_O + 16384].rearrange("p (i c) -> p i c", c=1024)

    cpool = [CONST_O]

    def calloc(n):
        o = cpool[0]
        cpool[0] += n
        assert cpool[0] <= ACC_O, "const pool overflow"
        return o

    IDENT_O = calloc(128); identv = f32v(IDENT_O, 128)
    E2_O = calloc(2); e2v = f32v(E2_O, 2)
    ONES_O = calloc(128); onesFv = f32v(ONES_O, 128, 0, 1)
    ONESC_O = calloc(1); onescv = f32v(ONESC_O, 1)
    BR_O = calloc(8); brv = f32v(BR_O, 8, 0, 1)
    BQK_O = calloc(16)
    B1C_O = calloc(256)
    CWA_O = calloc(128)
    SML_O = calloc(64)
    WR_O = calloc(64); wr32 = f32v(WR_O, 64)
    CNB_O = calloc(16)
    GCB_O = calloc(8)
    COL_O = calloc(64)
    SEL_O = calloc(128)
    assert cpool[0] <= CONST_O + 2048

    bias_cols = {}

    def cbias(val):
        val = float(val)
        if val not in bias_cols:
            o = calloc(1)
            tk = em("v", lambda e, _a=f32v(o, 1), _v=val: e.memset(_a, _v), inc=True, drain=True)
            bias_cols[val] = (f32v(o, 1), tk)
        return bias_cols[val]

    psA = stack.enter_context(nc.psum_tensor("psA", [128, 1024], F32))
    psB = stack.enter_context(nc.psum_tensor("psB", [128, 1024], F32))
    psC = stack.enter_context(nc.psum_tensor("psC", [128, 1024], F32))
    psD = stack.enter_context(nc.psum_tensor("psD", [128, 1024], F32))
    PAIRS = [psA, psB, psC, psD]

    def half(pi, h, p0=0, pn=128):
        return PAIRS[pi].ap()[p0:p0 + pn, 512 * h:512 * h + 512]

    tok_x = em("y", lambda e: e.dma_start(out=xt3, in_=di["x"].rearrange("(i p) c -> p i c", p=128)),
               inc=True, sem=p.dsem("x"))
    tok_ident = em("y", lambda e: e.dma_start(out=identv, in_=di["ident"]), inc=True, sem=p.dsem("cst"))
    tok_e2 = em("y", lambda e: e.dma_start(out=e2v, in_=di["e2"]), inc=True, sem="cst")
    tok_ones = em("v", lambda e: e.memset(onesFv, 1.0), inc=True)
    tok_onesc = em("v", lambda e: e.memset(onescv, 1.0), inc=True, drain=True)
    tok_br = em("y", lambda e: e.dma_start(out=brv, in_=di["moe_br"]), inc=True, sem="cst")
    tok_bqk = em("y", lambda e: e.dma_start(
        out=a32[:, BQK_O:BQK_O + 16].rearrange("p (a b) -> p a b", a=2),
        in_=di["l0_bqk"].rearrange("a p b -> p a b")), inc=True, sem="cst")
    tok_b1c = em("y", lambda e: e.dma_start(out=f32v(B1C_O, 256), in_=di["moe_b1"]), inc=True, sem="cst")
    tok_wr = em("y", lambda e: e.dma_start(
        out=a32[:, WR_O:WR_O + 64].rearrange("p (k e) -> p k e", k=8),
        in_=di["moe_wrT"].rearrange("k p e -> p k e")), inc=True, sem="cst")
    tok_cnb = em("y", lambda e: e.dma_start(out=f32v(CNB_O, 8), in_=di["cn_b1"]), inc=True, sem="cst")
    em("y", lambda e: e.dma_start(out=f32v(CNB_O + 8, 4), in_=di["cn_b2"]), inc=True, sem="cst")
    em("y", lambda e: e.dma_start(out=f32v(CNB_O + 12, 2), in_=di["cn_b3"]), inc=True, sem="cst")
    em("y", lambda e: e.dma_start(out=f32v(GCB_O, 4), in_=di["gc_b1"]), inc=True, sem="cst")
    em("y", lambda e: e.dma_start(out=f32v(GCB_O + 4, 2), in_=di["gc_b2"]), inc=True, sem="cst")
    em("y", lambda e: e.dma_start(out=f32v(COL_O + 5, 1, 0, 1), in_=di["gc_b3"]), inc=True, sem="cst")
    em("y", lambda e: e.dma_start(out=f32v(COL_O + 6, 1, 0, 1), in_=di["cn_b4"]), inc=True, sem="cst")
    tok_ar = em("y", lambda e: e.dma_start(out=f32v(COL_O + 7, 1, 0, 1), in_=di["l1_ar"]), inc=True, sem="cst")

    STG_O = FT_O
    LGS_O = FT_O + 2048

    def ln_tile(i, acc_off, eps, src_tile, src_tok=None):
        st = f32v(SML_O, 12)
        mv = f32v(SML_O + 12, 2)
        em("v", lambda e, _s=src_tile: e.bn_stats(f32v(SML_O, 6), _s[:, 0:512]), drain=True,
           waits=[src_tok] if src_tok else [])
        em("v", lambda e, _s=src_tile: e.bn_stats(f32v(SML_O + 6, 6), _s[:, 512:1024]), drain=True)
        t3 = em("v", lambda e: e.bn_aggr(mv, st), inc=True, drain=True)
        sd = f32v(SML_O + 14, 1)
        bc, btk = cbias(eps)
        t4 = em("s", lambda e, _b=bc: e.activation(out=sd, in_=mv[:, 1:2], func=AF.Sqrt, bias=_b,
                                                   scale=1.0), inc=True, waits=[t3, btk])
        isd = f32v(SML_O + 15, 1)
        em("v", lambda e: e.reciprocal(isd, sd), waits=[t4], drain=True)
        nmi = f32v(SML_O + 16, 1)
        t6 = em("v", lambda e: e.tensor_scalar(out=nmi, in0=mv[:, 0:1], scalar1=isd, scalar2=-1.0,
                                               op0=OP.mult, op1=OP.mult), inc=True, drain=True)
        dst = f32v(acc_off + 1024 * i, 1024)
        t7 = em("s", lambda e, _s=src_tile, _d=dst: e.activation(
            out=_d, in_=_s, func=AF.Identity, bias=nmi, scale=isd), inc=True, waits=[t6])
        return t7

    ps_free = [p.tok("v"), p.tok("v")]
    stg_free = [p.tok("t"), p.tok("t")]
    router_ps_free = [p.tok("v")]

    def transpose_router_tile(i, acc_off, apply_tok, do_router):
        vtoks = []
        for g in range(2):
            pr = g
            hv = half(pr, 0)
            last = None
            for q in range(4):
                src = f32v(acc_off + 1024 * i + 128 * (4 * g + q), 128)
                last = em("t", lambda e, _s=src, _o=hv[:, 128 * q:128 * q + 128],
                          _st=(q == 0), _sp=(q == 3):
                          e.matmul(_o, _s, identv, is_transpose=True, start=_st, stop=_sp),
                          inc=(q == 3), waits=[apply_tok, ps_free[pr], tok_ident])
            dst = xnt_all[:, 4 * g:4 * g + 4, 128 * i:128 * i + 128]
            tv = em("v", lambda e, _d=dst, _s=hv.rearrange("p (c t) -> p c t", t=128):
                    e.tensor_copy(_d, _s), inc=True, waits=[last], drain=True)
            vtoks.append(tv)
            if do_router:
                sb = i % 2
                sdst = f32v(STG_O + 1024 * sb + 512 * g, 512)
                tv2 = em("v", lambda e, _d=sdst, _s=hv: e.tensor_copy(_d, _s),
                         inc=True, waits=[last, stg_free[sb]], drain=True)
                vtoks.append(tv2)
            ps_free[pr] = p.tok("v")
        if not do_router:
            return
        sb = i % 2
        stok = p.tok("v")
        lg_ps = PAIRS[2].ap()[:, 0:8]
        for kc in range(8):
            lhs = f32v(STG_O + 1024 * sb + 128 * kc, 128)
            em("t", lambda e, _l=lhs, _w=wr32[:, 8 * kc:8 * kc + 8], _st=(kc == 0):
               e.matmul(lg_ps, _l, _w, start=_st, stop=False),
               waits=[stok, tok_wr, router_ps_free[0]])
        tmm = em("t", lambda e: e.matmul(lg_ps, onesFv, brv, start=False, stop=True),
                 inc=True, waits=[tok_ones, tok_br])
        stg_free[sb] = p.tok("t")
        lgs = f32v(LGS_O + 8 * i, 8)
        c0 = em("v", lambda e: e.tensor_copy(lgs, lg_ps), inc=True, waits=[tmm], drain=True)
        router_ps_free[0] = c0
        m1 = f32v(SML_O + 17, 1); m2 = f32v(SML_O + 18, 1)
        msk = f32v(SML_O + 19, 8); el = f32v(SML_O + 27, 8)
        dnm = f32v(SML_O + 35, 1); nm1 = f32v(SML_O + 36, 1)
        sm2 = f32v(SML_O + 37, 1); ex2 = f32v(SML_O + 38, 1)
        dn1 = f32v(SML_O + 39, 1); ge = f32v(SML_O + 40, 8)
        em("v", lambda e: e.reduce_max(m1, lgs, axis=AX.X), drain=True)
        em("v", lambda e: e.tensor_scalar(out=msk, in0=lgs, scalar1=m1, scalar2=None, op0=OP.is_ge),
           drain=True)
        em("v", lambda e: e.scalar_tensor_tensor(out=msk, in0=msk, scalar=-1e30, in1=lgs,
                                                 op0=OP.mult, op1=OP.add), drain=True)
        em("v", lambda e: e.reduce_max(m2, msk, axis=AX.X), drain=True)
        em("v", lambda e: e.tensor_scalar(out=nm1, in0=m1, scalar1=-1.0, scalar2=None, op0=OP.mult),
           drain=True)
        t8 = em("v", lambda e: e.tensor_tensor(out=sm2, in0=m2, in1=nm1, op=OP.add), inc=True,
                drain=True)
        t9 = em("s", lambda e: e.activation(out=ex2, in_=sm2, func=AF.Exp, bias=0.0, scale=1.0),
                inc=True, waits=[t8])
        em("v", lambda e: e.tensor_scalar(out=dn1, in0=ex2, scalar1=1.0, scalar2=None, op0=OP.add),
           waits=[t9], drain=True)
        t10 = em("v", lambda e: e.reciprocal(dnm, dn1), inc=True, drain=True)
        t11 = em("s", lambda e: e.activation(out=el, in_=lgs, func=AF.Exp, bias=nm1, scale=1.0),
                 inc=True, waits=[t10])
        em("v", lambda e: e.tensor_scalar(out=ge, in0=lgs, scalar1=m2, scalar2=None, op0=OP.is_ge),
           waits=[t11], drain=True)
        em("v", lambda e: e.tensor_tensor(out=ge, in0=ge, in1=el, op=OP.mult), drain=True)
        cwd = f32v(CWA_O + 8 * i, 8)
        em("v", lambda e, _d=cwd: e.tensor_scalar(out=_d, in0=ge, scalar1=dnm, scalar2=None,
                                                  op0=OP.mult), inc=True, drain=True)

    if PHASES >= 1:
        for i in range(NT):
            at = ln_tile(i, ACC_O, EPS_LN, XTt(i), tok_x)
            transpose_router_tile(i, ACC_O, at, True)
            if DEBUG_SERIAL:
                p.barrier()
        p.barrier()

    p.views["acc"] = (a32[:, ACC_O:ACC_O + 16384], (128, 16384))
    p.views["lgs"] = (f32v(LGS_O, 128), (128, 128))
    p.views["cw"] = (f32v(CWA_O, 128), (128, 128))
    p.views["xnt"] = (a16[:, 2 * XNT_O:2 * XNT_O + 16384], (128, 16384))

    H1_O = XT_O
    W2_O = XT_O + 8192
    W1_O = FT_O + 2100
    B2_O = FT_O + 3200

    def h1c(fc, t0=0, tn=T):
        o = 2 * H1_O + 2048 * fc + t0
        return a16[:, o:o + tn]

    def w2sl(qb, fc):
        o = 2 * (W2_O + 4096 * qb) + 1024 * fc
        return a16[:, o:o + 1024]

    def w1b(b):
        o = 2 * (W1_O + 512 * b)
        return a16[:, o:o + 1024].rearrange("p (k j) -> p k j", k=8)

    if PHASES >= 3:
        w1_use = [p.tok("t"), p.tok("t")]
        w2_use = [p.tok("t"), p.tok("t")]
        b2_use = p.tok("t")
        for e_ in range(NE):
            b2tok = em("y", lambda e, _e=e_: e.dma_start(out=f32v(B2_O, 1024, 0, 1),
                                                         in_=di["moe_b2"][_e:_e + 1, :]),
                       inc=True, waits=[b2_use], sem=p.dsem("b2"))
            for fq in range(4):
                qb = fq % 2
                w2toks = []
                for fc in range(8):
                    tk = em("g", lambda e, _qb=qb, _fc=fc, _e=e_, _f=8 * fq + fc:
                            e.dma_start(out=w2sl(_qb, _fc), in_=di["moe_w2"][_e, _f]),
                            inc=True, waits=[w2_use[qb]], sem=p.dsem("w2%d" % qb))
                    w2toks.append(tk)
                w2_ready = w2toks[-1]
                up_ps_free = [p.tok("s")] * 4
                gelu_last = None
                for fc in range(8):
                    fcg = 8 * fq + fc
                    b = fcg % 2
                    w1t = em("g", lambda e, _b=b, _e=e_, _f=fcg:
                             e.dma_start(out=w1b(_b), in_=di["moe_w1"][_e, _f]),
                             inc=True, waits=[w1_use[b]], sem=p.dsem("w1%d" % b))
                    for tt in range(4):
                        slot = (fc * 4 + tt) % 4
                        hv = half(slot // 2, slot % 2)
                        for kc in range(8):
                            em("t", lambda e, _h=hv, _b=b, _k=kc, _tt=tt,
                               _st=(kc == 0), _sp=(kc == 7):
                               e.matmul(_h, w1b(_b)[:, _k, :], xnt(_k, 512 * _tt, 512),
                                        start=_st, stop=_sp),
                               inc=(kc == 7), waits=[w1t, up_ps_free[slot]])
                        mmtok = p.tok("t")
                        w1_use[b] = mmtok
                        bia = f32v(B1C_O + 32 * e_ + fcg, 1)
                        gt = em("s", lambda e, _d=h1c(fc, 512 * tt, 512), _sv=hv, _b=bia:
                                e.activation(out=_d, in_=_sv, func=AF.Gelu, bias=_b, scale=1.0),
                                inc=True, waits=[mmtok, tok_b1c])
                        up_ps_free[slot] = gt
                        gelu_last = gt
                down_ps_free = [p.tok("v"), p.tok("v")]
                for ti in range(NT):
                    pr = 2 + (ti % 2)
                    first_start = True
                    if fq == 0:
                        for hh in range(2):
                            em("t", lambda e, _h=half(pr, hh), _o=512 * hh:
                               e.matmul(_h, onesFv, f32v(B2_O + _o, 512, 0, 1),
                                        start=True, stop=False),
                               waits=[b2tok, tok_ones, down_ps_free[ti % 2], gelu_last])
                        first_start = False
                    for fc in range(8):
                        for hh in range(2):
                            em("t", lambda e, _h=half(pr, hh), _ti=ti, _fc=fc, _qb=qb, _hh=hh,
                               _st=(first_start and fc == 0), _sp=(fc == 7):
                               e.matmul(_h, h1c(_fc, 128 * _ti, 128),
                                        w2sl(_qb, _fc)[:, 512 * _hh:512 * _hh + 512],
                                        start=_st, stop=_sp),
                               inc=(fc == 7 and hh == 1),
                               waits=[w2_ready, down_ps_free[ti % 2], gelu_last])
                    mmt = p.tok("t")
                    w2_use[qb] = mmt
                    if fq == 0:
                        b2_use = mmt
                    cwc = f32v(CWA_O + 8 * ti + e_, 1)
                    vt = em("v", lambda e, _p=PAIRS[pr].ap(), _a=acc_t(ti), _c=cwc:
                            e.scalar_tensor_tensor(out=_a, in0=_p, scalar=_c, in1=_a,
                                                   op0=OP.mult, op1=OP.add),
                            inc=True, waits=[mmt], drain=True)
                    down_ps_free[ti % 2] = vt
        p.barrier()
    p.views["moeacc"] = (a32[:, ACC_O:ACC_O + 16384], (128, 16384))

    if PHASES >= 3.5:
        tok_x2 = em("y", lambda e: e.dma_start(out=xt3,
                                               in_=di["x"].rearrange("(i p) c -> p i c", p=128)),
                    inc=True, sem="x")
        LNR_O = FT_O + 4224
        WB_O = FT_O + 6272
        BB_O = FT_O + 7296
        lnwrow = f32v(LNR_O, 1024, 0, 1)
        lnbrow = f32v(LNR_O + 1024, 1024, 0, 1)
        tw = em("y", lambda e: e.dma_start(out=lnwrow, in_=di["moe_lnw"]), inc=True, sem="cst")
        tb = em("y", lambda e: e.dma_start(out=lnbrow, in_=di["moe_lnb"]), inc=True, sem="cst")
        prev_ev = p.tok("v")
        for g in range(2):
            mt = None
            for hh in range(2):
                mt = em("t", lambda e, _h=half(0, hh), _o=1024 * g + 512 * hh:
                        e.matmul(_h, onesFv, f32v(LNR_O + _o, 512, 0, 1), start=True, stop=True),
                        inc=(hh == 1), waits=[tw, tb, tok_ones, prev_ev])
            dsto = WB_O if g == 0 else BB_O
            prev_ev = em("v", lambda e, _d=f32v(dsto, 1024), _s=PAIRS[0].ap():
                         e.tensor_copy(_d, _s), inc=True, waits=[mt], drain=True)
        tok_wb = prev_ev
        TMP_O = FT_O + 8320
        for i in range(NT):
            at = ln_tile(i, ACC_O, EPS_MOE_LN, acc_t(i))
            tmp = f32v(TMP_O, 1024)
            em("v", lambda e, _i=i, _t=tmp: e.tensor_tensor(out=_t, in0=acc_t(_i),
                                                            in1=f32v(WB_O, 1024), op=OP.mult),
               waits=[at, tok_wb], drain=True)
            em("v", lambda e, _t=tmp: e.tensor_tensor(out=_t, in0=_t, in1=f32v(BB_O, 1024),
                                                      op=OP.add), drain=True)
            em("v", lambda e, _i=i, _t=tmp: e.tensor_tensor(out=XTt(_i), in0=XTt(_i), in1=_t,
                                                            op=OP.add),
               inc=True, waits=[tok_x2], drain=True)
        p.barrier()
    p.views["xt_ffn0"] = (a32[:, XT_O:XT_O + 16384], (128, 16384))

    SCR4 = ACC_O
    V1_O = SCR4
    ATN_O = SCR4 + 2080
    QKP_O = SCR4 + 10272
    SQ_O = SCR4 + 12320
    PHQ_O = SCR4 + 12832
    PHK_O = SCR4 + 14880
    KVA_O = SCR4 + 16928
    NQ_O = SCR4 + 17056
    NK_O = SCR4 + 19104
    RC_O = NQ_O
    SEL_O4 = SCR4 + 21152
    WQ_O = SCR4 + 22176
    WK_O = SCR4 + 22688
    WV_O = SCR4 + 23200
    ORF_O = SCR4 + 23712
    BV_O = SCR4 + 23840
    assert BV_O + 1024 <= AREN

    def v1t(buf, ti, h2, n=65):
        o = 2 * (V1_O + 1040 * buf) + 130 * ti + 65 * h2
        return a16[:, o:o + n]

    def atn(c, t0=0, tn=T, p0=0, pn=128):
        o = 2 * ATN_O + 2048 * c + t0
        return a16[p0:p0 + pn, o:o + tn]

    def qkp(which, p0, pn, t0, tn):
        o = 2 * (QKP_O + 1024 * which) + t0
        return a16[p0:p0 + pn, o:o + tn]

    def phq(jc, t0, tn):
        o = 2 * PHQ_O + 2048 * jc + t0
        return a16[:, o:o + tn]

    def phk(ti, j0=0, jn=256):
        o = 2 * PHK_O + 256 * ti + j0
        return a16[:, o:o + jn]

    def kva(jc, c0=0, cn=65):
        o = 2 * (KVA_O + 64 * jc) + c0
        return a16[:, o:o + cn]

    if PHASES >= 4:
        t_orf = em("g", lambda e: e.dma_start(out=b16v(ORF_O, 256, 0, 64), in_=di["l0_orfT"]),
                   inc=True, sem=p.dsem("p4c"))
        t_orf2 = em("g", lambda e: e.dma_start(out=b16v(ORF_O, 256, 64, 64), in_=di["l0_orfT"]),
                    inc=True, sem="p4c")
        t_bv = em("y", lambda e: e.dma_start(out=f32v(BV_O, 1024, 0, 1), in_=di["l0_bv"]),
                  inc=True, sem=p.dsem("p4y"))
        em("y", lambda e: e.dma_start(
            out=a32[0:2, SEL_O4:SEL_O4 + 256].rearrange("p (a c) -> p a c", a=2),
            in_=di["cqsel"].rearrange("a p c -> p a c")), inc=True, sem=p.dsem("p4y"))
        em("y", lambda e: e.dma_start(
            out=a32[0:2, SEL_O4 + 256:SEL_O4 + 768].rearrange("p (a c) -> p a c", a=2),
            in_=di["ksel"].rearrange("a p c -> p a c")), inc=True, sem="p4y")
        selw = []
        selw.append(em("v", lambda e: e.memset(f32v(SEL_O4 + 768, 64, 64, 1), 1.0), drain=True))
        selw.append(em("v", lambda e: e.memset(f32v(SEL_O4 + 768 + 64, 64, 64, 1), 0.0), drain=True))
        selw.append(em("v", lambda e: e.memset(f32v(SEL_O4 + 896, 64, 64, 1), 0.0), drain=True))
        selw.append(em("v", lambda e: e.memset(f32v(SEL_O4 + 896 + 64, 64, 64, 1), 1.0), drain=True))
        for buf in range(2):
            selw.append(em("v", lambda e, _b=buf: e.memset(
                a16[:, 2 * (V1_O + 1040 * _b):2 * (V1_O + 1040 * _b) + 2080]
                .rearrange("p (t c) -> p t c", c=130)[:, :, 64:65], 1.0), drain=True))
            selw.append(em("v", lambda e, _b=buf: e.memset(
                a16[:, 2 * (V1_O + 1040 * _b):2 * (V1_O + 1040 * _b) + 2080]
                .rearrange("p (t c) -> p t c", c=130)[:, :, 129:130], 1.0), drain=True))
        tok_sel = em("v", lambda e: e.nop(), inc=True, waits=[("p4y", 32)])
        CQsel = [f32v(SEL_O4, 128, 0, 2), f32v(SEL_O4 + 128, 128, 0, 2)]
        Ksel = [f32v(SEL_O4 + 256, 256, 0, 2), f32v(SEL_O4 + 512, 256, 0, 2)]
        tok_selc = p.tok("p4y")
        E64 = [f32v(SEL_O4 + 768, 128, 64, 1), f32v(SEL_O4 + 896, 128, 64, 1)]
        bc_phi, tk_phi = cbias(PHI_BIAS)

        wqk_use = p.tok("t")
        wv_use = p.tok("t")
        v1_use = [p.tok("t"), p.tok("t")]
        qk_ready = None
        prev_head_done = p.tok("v")

        for pr_ in range(8):
            vb = pr_ % 2
            wqt = em("g", lambda e, _p=pr_: e.dma_start(
                out=b16v(WQ_O, 1024).rearrange("p (k j) -> p k j", k=8),
                in_=di["l0_wqT"][_p].rearrange("k p j -> p k j")),
                inc=True, waits=[wqk_use], sem=p.dsem("wqk"))
            wkt = em("g", lambda e, _p=pr_: e.dma_start(
                out=b16v(WK_O, 1024).rearrange("p (k j) -> p k j", k=8),
                in_=di["l0_wkT"][_p].rearrange("k p j -> p k j")),
                inc=True, sem="wqk")
            wvt = em("g", lambda e, _p=pr_: e.dma_start(
                out=b16v(WV_O, 1024).rearrange("p (k j) -> p k j", k=8),
                in_=di["l0_wvT"][:, :, 128 * _p:128 * _p + 128].rearrange("k p j -> p k j")),
                inc=True, waits=[wv_use], sem=p.dsem("wv"))
            v1ev = []
            for ti in range(NT):
                hv = half(0, ti % 2)
                hvv = PAIRS[0].ap()[:, 512 * (ti % 2):512 * (ti % 2) + 128]
                for kc in range(8):
                    em("t", lambda e, _h=hvv, _k=kc, _ti=ti, _st=(kc == 0):
                       e.matmul(_h, xnt(_k, 128 * _ti, 128),
                                b16v(WV_O, 1024).rearrange("p (k j) -> p k j", k=8)[:, _k, :],
                                start=_st, stop=False),
                       waits=[wvt, v1_use[vb], prev_head_done,
                              (v1ev[ti - 2] if ti >= 2 else None)])
                em("t", lambda e, _h=hvv, _p=pr_:
                   e.matmul(_h, onesFv, f32v(BV_O + 128 * _p, 128, 0, 1), start=False, stop=True),
                   inc=True, waits=[t_bv, tok_ones])
                mmv = p.tok("t")
                ev = em("v", lambda e, _d=v1t(vb, ti, 0, 130).rearrange("p (a c) -> p a c", c=65)[:, :, 0:64],
                        _s=hvv.rearrange("p (a c) -> p a c", c=64), _m=mmv:
                        e.tensor_copy(_d, _s), inc=True, waits=[mmv, tok_sel], drain=True)
                v1ev.append(ev)
            v1_ready = v1ev[-1]
            qev = None
            for which, wt_, bcol in ((0, wqt, 0), (1, wkt, 8)):
                for kc in range(8):
                    wsl = b16v(WQ_O + 512 * which, 1024).rearrange("p (k j) -> p k j", k=8)
                    for tt in range(4):
                        hv = half(1 + tt // 2, tt % 2)
                        em("t", lambda e, _h=hv, _w=wsl, _k=kc, _tt=tt,
                           _st=(kc == 0), _sp=(kc == 7):
                           e.matmul(_h, _w[:, _k, :], xnt(_k, 512 * _tt, 512), start=_st, stop=_sp),
                           inc=(kc == 7 and tt == 3),
                           waits=[wt_, qk_ready, prev_head_done, qev])
                mmq = p.tok("t")
                if which == 1:
                    wqk_use = mmq
                for tt in range(4):
                    hv = half(1 + tt // 2, tt % 2)
                    bia = f32v(BQK_O + bcol + pr_, 1)
                    em("s", lambda e, _d=qkp(which, 0, 128, 512 * tt, 512), _sv=hv, _b=bia:
                       e.activation(out=_d, in_=_sv, func=AF.Identity, bias=_b, scale=1.0),
                       inc=(tt == 3), waits=[mmq, tok_bqk], drain=True)
                if which == 0:
                    qev = p.tok("s")
                else:
                    kev = p.tok("s")
            nrm_toks = []
            sq_free = None
            for which, ev_, dsto in ((0, None, NQ_O), (1, None, NK_O)):
                evtok = qev if which == 0 else kev
                for tt in range(4):
                    sq = f32v(SQ_O, 512)
                    t1 = em("s", lambda e, _s=qkp(which, 0, 128, 512 * tt, 512), _d=sq:
                            e.activation(out=_d, in_=_s, func=AF.Square, bias=0.0, scale=1.0),
                            inc=True, waits=[evtok, sq_free], drain=True)
                    nps = PAIRS[2].ap()[0:2, 0:512]
                    t2 = em("t", lambda e, _s=sq: e.matmul(nps, e2v, _s, start=True, stop=True),
                            inc=True, waits=[t1, tok_e2, prev_head_done,
                                             nrm_toks[-1] if nrm_toks else None])
                    sq_free = t2
                    t3 = em("v", lambda e, _d=f32v(dsto + 512 * tt, 512, 0, 2), _s=nps:
                            e.tensor_copy(_d, _s), inc=True, waits=[t2], drain=True)
                    nrm_toks.append(t3)
            norm_done = nrm_toks[-1]
            for h2 in range(2):
                h = 2 * pr_ + h2
                phiq_toks = []
                phiq_slot_free = [None] * 4
                for jc in range(2):
                    for tt in range(4):
                        slot = (jc * 4 + tt) % 4
                        hv = half(1 + slot // 2, slot % 2)
                        em("t", lambda e, _h=hv, _j=jc, _h2=h2, _tt=tt:
                           e.matmul(_h, b16v(ORF_O, 256, 64 * _h2, 64)[:, 128 * _j:128 * _j + 128],
                                    qkp(0, 64 * _h2, 64, 512 * _tt, 512), start=True, stop=False),
                           waits=[t_orf2, norm_done, prev_head_done, phiq_slot_free[slot]])
                        em("t", lambda e, _h=hv, _h2=h2, _tt=tt:
                           e.matmul(_h, CQsel[_h2], f32v(NQ_O + 512 * _tt, 512, 0, 2),
                                    start=False, stop=True), inc=True, waits=[tok_sel])
                        mmp = p.tok("t")
                        sqv = f32v(SQ_O, 512)
                        t1 = em("s", lambda e, _d=sqv, _s=hv, _b=bc_phi:
                                e.activation(out=_d, in_=_s, func=AF.Exp, bias=_b, scale=INV_S),
                                inc=True, waits=[mmp, tk_phi, sq_free], drain=True)
                        phiq_slot_free[slot] = t1
                        t2 = em("v", lambda e, _d=phq(jc, 512 * tt, 512), _s=sqv:
                                e.tensor_scalar(out=_d, in0=_s, scalar1=PHI_EPS, scalar2=None,
                                                op0=OP.add), inc=True, waits=[t1], drain=True)
                        sq_free = t2
                        phiq_toks.append(t2)
                phiq_done = phiq_toks[-1]
                phik_toks = []
                for ti in range(NT):
                    hv = PAIRS[3].ap()[:, 512 * (ti % 2):512 * (ti % 2) + 256]
                    em("t", lambda e, _h=hv, _h2=h2, _ti=ti:
                       e.matmul(_h, qkp(1, 64 * _h2, 64, 128 * _ti, 128),
                                b16v(ORF_O, 256, 64 * _h2, 64), start=True, stop=False),
                       waits=[norm_done, prev_head_done,
                              (phik_toks[ti - 2] if ti >= 2 else None)])
                    em("t", lambda e, _h=hv, _h2=h2, _ti=ti:
                       e.matmul(_h, f32v(NK_O + 128 * _ti, 128, 0, 2), Ksel[_h2],
                                start=False, stop=True), inc=True, waits=[tok_sel])
                    mmp = p.tok("t")
                    sqv = f32v(SQ_O, 256)
                    t1 = em("s", lambda e, _d=sqv, _s=hv, _b=bc_phi:
                            e.activation(out=_d, in_=_s, func=AF.Exp, bias=_b, scale=INV_S),
                            inc=True, waits=[mmp, sq_free], drain=True)
                    t2 = em("v", lambda e, _d=phk(ti), _s=sqv:
                            e.tensor_scalar(out=_d, in0=_s, scalar1=PHI_EPS, scalar2=None,
                                            op0=OP.add), inc=True, waits=[t1], drain=True)
                    sq_free = t2
                    phik_toks.append(t2)
                phik_done = phik_toks[-1]
                kv_toks = []
                for jc in range(2):
                    hv = PAIRS[0].ap()[:, 512 * jc:512 * jc + 65]
                    for ti in range(NT):
                        em("t", lambda e, _h=hv, _j=jc, _ti=ti, _h2=h2, _vb=vb,
                           _st=(ti == 0), _sp=(ti == 15):
                           e.matmul(_h, phk(_ti, 128 * _j, 128), v1t(_vb, _ti, _h2),
                                    start=_st, stop=_sp),
                           inc=(ti == 15), waits=[phik_done, v1_ready])
                    mk = p.tok("t")
                    v1_use[vb] = mk
                    ev = em("v", lambda e, _d=kva(jc), _s=hv: e.tensor_copy(_d, _s),
                            inc=True, waits=[mk], drain=True)
                    kv_toks.append(ev)
                kv_done = kv_toks[-1]
                nd_toks = []

                def dslot(tt):
                    dpair, dh = (0, tt) if tt < 2 else (3, tt - 2)
                    return PAIRS[dpair].ap()[64:65, 512 * dh:512 * dh + 512]
                for jc in range(2):
                    for tt in range(4):
                        pi, hh = (1 + tt // 2, tt % 2)
                        if h2 == 0:
                            outp = PAIRS[pi].ap()[0:65, 512 * hh:512 * hh + 512]
                            em("t", lambda e, _o=outp, _j=jc, _tt=tt, _st=(jc == 0), _sp=(jc == 1):
                               e.matmul(_o, kva(_j), phq(_j, 512 * _tt, 512), start=_st, stop=_sp),
                               inc=(jc == 1), waits=[kv_done, phiq_done])
                        else:
                            outp = PAIRS[pi].ap()[64:128, 512 * hh:512 * hh + 512]
                            em("t", lambda e, _o=outp, _j=jc, _tt=tt, _st=(jc == 0), _sp=(jc == 1):
                               e.matmul(_o, kva(_j, 0, 64), phq(_j, 512 * _tt, 512),
                                        start=_st, stop=_sp),
                               waits=[kv_done, phiq_done])
                            em("t", lambda e, _o=dslot(tt), _j=jc, _tt=tt,
                               _st=(jc == 0), _sp=(jc == 1):
                               e.matmul(_o, kva(_j, 64, 1), phq(_j, 512 * _tt, 512),
                                        start=_st, stop=_sp), inc=(jc == 1))
                        if jc == 1:
                            nd_toks.append(p.tok("t"))
                head_v_toks = []
                for tt in range(4):
                    pi, hh = (1 + tt // 2, tt % 2)
                    ndt = nd_toks[tt]
                    if h2 == 0:
                        nsrc = PAIRS[pi].ap()[0:64, 512 * hh:512 * hh + 512]
                        dsrc = PAIRS[pi].ap()[64:65, 512 * hh:512 * hh + 512]
                    else:
                        nsrc = PAIRS[pi].ap()[64:128, 512 * hh:512 * hh + 512]
                        dsrc = dslot(tt)
                    ev = em("v", lambda e, _d=atn(pr_, 512 * tt, 512, 64 * h2, 64), _s=nsrc:
                            e.tensor_copy(_d, _s), inc=True, waits=[ndt], drain=True)
                    rc = f32v(RC_O + 512 * tt, 512, 64, 1)
                    t1 = em("s", lambda e, _d=rc, _s=dsrc: e.activation(
                        out=_d, in_=_s, func=AF.Ln, bias=0.0, scale=1.0), waits=[ndt], drain=True)
                    t2 = em("s", lambda e, _d=rc: e.activation(
                        out=_d, in_=_d, func=AF.Exp, bias=0.0, scale=-1.0), inc=True, drain=True)
                    rtok = p.tok("s")
                    bcp = PAIRS[pi].ap()[:, 512 * hh:512 * hh + 512]
                    t3 = em("t", lambda e, _o=bcp, _h2=h2, _r=rc: e.matmul(
                        _o, E64[_h2], _r, start=True, stop=True),
                        inc=True, waits=[rtok, tok_sel, ev])
                    bct = p.tok("t")
                    t4 = em("v", lambda e, _a=atn(pr_, 512 * tt, 512, 64 * h2, 64),
                            _b=bcp[64 * h2:64 * h2 + 64, :]:
                            e.tensor_tensor(out=_a, in0=_a, in1=_b, op=OP.mult),
                            inc=True, waits=[bct], drain=True)
                    head_v_toks.append(t4)
                prev_head_done = head_v_toks[-1]
            qk_ready = prev_head_done
        bo_row = f32v(QKP_O, 1024, 0, 1)
        t_bo = em("y", lambda e: e.dma_start(out=bo_row, in_=di["l0_bo"]), inc=True, sem="p4y",
                  waits=[prev_head_done])
        wo_use = [p.tok("t"), p.tok("t")]
        fold_evs = []
        for grp in range(4):
            for kc in range(8):
                wb = kc % 2
                wo_dst = b16v(WV_O if wb == 0 else WQ_O, 1024)
                wot = em("g", lambda e, _d=wo_dst, _k=kc: e.dma_start(out=_d, in_=di["l0_woT"][_k]),
                         inc=True, waits=[wo_use[wb], prev_head_done], sem=p.dsem("wo%d" % wb))
                for t_ in range(4):
                    ti = 4 * grp + t_
                    pv = PAIRS[t_].ap()
                    if kc == 0:
                        for hh in range(2):
                            em("t", lambda e, _h=half(t_, hh), _o=512 * hh:
                               e.matmul(_h, onesFv, f32v(QKP_O + _o, 512, 0, 1),
                                        start=True, stop=False),
                               waits=[t_bo, tok_ones, prev_head_done,
                                      (fold_evs[4 * (grp - 1) + t_] if grp > 0 else None)])
                    for hh in range(2):
                        em("t", lambda e, _h=half(t_, hh), _k=kc, _ti=ti, _hh=hh, _w=wo_dst,
                           _sp=(kc == 7):
                           e.matmul(_h, atn(_k, 128 * _ti, 128),
                                    _w[:, 512 * _hh:512 * _hh + 512], start=False, stop=_sp),
                           inc=((kc == 7 and hh == 1) or (t_ == 3 and hh == 1)), waits=[wot])
                    if kc == 7:
                        mmt = p.tok("t")
                        ev = em("v", lambda e, _x=XTt(ti), _s=pv:
                                e.tensor_tensor(out=_x, in0=_x, in1=_s, op=OP.add),
                                inc=True, waits=[mmt], drain=True)
                        fold_evs.append(ev)
                wo_use[wb] = p.tok("t")
        p.barrier()
    p.views["atn0"] = (a16[:, 2 * ATN_O:2 * ATN_O + 16384], (128, 16384))

    if PHASES >= 5:
        for i in range(NT):
            at = ln_tile(i, ACC_O, EPS_LN, XTt(i))
            transpose_router_tile(i, ACC_O, at, False)
        p.barrier()
    p.views["xnt1"] = (a16[:, 2 * XNT_O:2 * XNT_O + 16384], (128, 16384))

    if PHASES >= 6:
        GC1_O = FT_O
        GC2_O = FT_O + 2048
        GC3_O = FT_O + 2560
        GCB_B = FT_O + 2568
        H1G_O = FT_O + 2572
        H2G_O = FT_O + 2574
        g1t = em("g", lambda e: e.dma_start(
            out=b16v(GC1_O, 4096).rearrange("p (k o) -> p k o", k=8), in_=di["gc_w1"]),
            inc=True, sem=p.dsem("gcw"))
        g2t = em("g", lambda e: e.dma_start(
            out=b16v(GC2_O, 1024).rearrange("p (k o) -> p k o", k=4), in_=di["gc_w2"]),
            inc=True, sem="gcw")
        g3t = em("g", lambda e: e.dma_start(
            out=b16v(GC3_O, 2).rearrange("p (k o) -> p k o", k=2), in_=di["gc_w3"]),
            inc=True, sem="gcw")
        gps = PAIRS[0].ap()[:, 0:1]
        gc_ev = None
        for oc in range(8):
            for ti in range(NT):
                em("t", lambda e, _o=oc, _ti=ti, _st=(ti == 0), _sp=(ti == 15):
                   e.matmul(gps, f32v(ACC_O + 1024 * _ti + 128 * _o, 128), onescv,
                            start=_st, stop=_sp),
                   inc=(ti == 15), waits=[tok_onesc, gc_ev])
            mmt = p.tok("t")
            gc_ev = em("s", lambda e, _d=a16[:, 2 * GCB_B + oc:2 * GCB_B + oc + 1], _s=gps:
                       e.activation(out=_d, in_=_s, func=AF.Identity, bias=0.0,
                                    scale=float(1.0 / 2048.0)),
                       inc=True, waits=[mmt], drain=True)
        h_ev = gc_ev
        for oc in range(4):
            for kc in range(8):
                em("t", lambda e, _k=kc, _o=oc, _st=(kc == 0), _sp=(kc == 7):
                   e.matmul(gps, b16v(GC1_O, 4096).rearrange("p (k o) -> p k o", k=8)
                            [:, _k, 128 * _o:128 * _o + 128],
                            a16[:, 2 * GCB_B + _k:2 * GCB_B + _k + 1],
                            start=_st, stop=_sp),
                   inc=(kc == 7), waits=[g1t, h_ev])
            mmt = p.tok("t")
            h_ev = em("s", lambda e, _d=a16[:, 2 * H1G_O + oc:2 * H1G_O + oc + 1],
                      _s=gps, _b=f32v(GCB_O + oc, 1):
                      e.activation(out=_d, in_=_s, func=AF.Gelu, bias=_b, scale=1.0),
                      inc=True, waits=[mmt, tok_cnb], drain=True)
        for oc in range(2):
            for kc in range(4):
                em("t", lambda e, _k=kc, _o=oc, _st=(kc == 0), _sp=(kc == 3):
                   e.matmul(gps, b16v(GC2_O, 1024).rearrange("p (k o) -> p k o", k=4)
                            [:, _k, 128 * _o:128 * _o + 128],
                            a16[:, 2 * H1G_O + _k:2 * H1G_O + _k + 1],
                            start=_st, stop=_sp),
                   inc=(kc == 3), waits=[g2t, h_ev])
            mmt = p.tok("t")
            h_ev = em("s", lambda e, _d=a16[:, 2 * H2G_O + oc:2 * H2G_O + oc + 1],
                      _s=gps, _b=f32v(GCB_O + 4 + oc, 1):
                      e.activation(out=_d, in_=_s, func=AF.Gelu, bias=_b, scale=1.0),
                      inc=True, waits=[mmt], drain=True)
        for kc in range(2):
            em("t", lambda e, _k=kc, _st=(kc == 0), _sp=(kc == 1):
               e.matmul(gps[0:1, 0:1],
                        b16v(GC3_O, 2).rearrange("p (k o) -> p k o", k=2)[:, _k, :],
                        a16[:, 2 * H2G_O + _k:2 * H2G_O + _k + 1], start=_st, stop=_sp),
               inc=(kc == 1), waits=[g3t, h_ev])
        mmt = p.tok("t")
        gf_ev = em("s", lambda e: e.activation(out=f32v(COL_O + 8, 1, 0, 1),
                                               in_=gps[0:1, 0:1], func=AF.Sigmoid,
                                               bias=f32v(COL_O + 5, 1, 0, 1), scale=1.0),
                   inc=True, waits=[mmt, tok_ar], drain=True)
        bc_ev = gf_ev
        for srcc, dstc in ((8, 3), (7, 4), (6, 2)):
            mt = em("t", lambda e, _s=f32v(COL_O + srcc, 1, 0, 1):
                    e.matmul(gps, onesFv, _s, start=True, stop=True),
                    inc=True, waits=[bc_ev, tok_ones])
            bc_ev = em("v", lambda e, _d=f32v(COL_O + dstc, 1), _s=gps:
                       e.tensor_copy(_d, _s), inc=True, waits=[mt], drain=True)
        em("v", lambda e: e.tensor_tensor(out=f32v(COL_O + 0, 1), in0=f32v(COL_O + 4, 1),
                                          in1=f32v(COL_O + 3, 1), op=OP.mult), drain=True)
        em("v", lambda e: e.tensor_scalar(out=f32v(COL_O + 0, 1), in0=f32v(COL_O + 0, 1),
                                          scalar1=1.4, scalar2=None, op0=OP.mult), drain=True)
        em("v", lambda e: e.scalar_tensor_tensor(out=f32v(COL_O + 0, 1), in0=f32v(COL_O + 4, 1),
                                                 scalar=-1.0, in1=f32v(COL_O + 0, 1),
                                                 op0=OP.mult, op1=OP.add), drain=True)
        em("v", lambda e: e.tensor_scalar(out=f32v(COL_O + 0, 1), in0=f32v(COL_O + 0, 1),
                                          scalar1=0.1, scalar2=None, op0=OP.add), drain=True)
        em("v", lambda e: e.tensor_scalar(out=f32v(COL_O + 1, 1), in0=f32v(COL_O + 4, 1),
                                          scalar1=0.6, scalar2=None, op0=OP.mult),
           inc=True, drain=True)
        p.barrier()

    SCR8 = ACC_O
    ATN1_O = SCR8
    TBL_O = SCR8
    CQT_O = SCR8 + 8192
    CKV_O = SCR8 + 10240
    QRA_O = SCR8 + 12288
    QRP_O = SCR8 + 16384
    QH_O = SCR8 + 16384
    KH_O = SCR8 + 18432
    RA_O = SCR8 + 20480
    assert RA_O + 5248 == AREN

    def atn1(c, t0=0, tn=T, p0=0, pn=128):
        o = 2 * ATN1_O + 2048 * c + t0
        return a16[p0:p0 + pn, o:o + tn]

    def cqt(c, t0=0, tn=T):
        o = 2 * CQT_O + 2048 * c + t0
        return a16[:, o:o + tn]

    def ckvt(c, t0=0, tn=T):
        o = 2 * CKV_O + 2048 * c + t0
        return a16[:, o:o + tn]

    def qrav(c, t0=0, tn=T, p0=0, pn=128):
        o = 2 * QRA_O + 2048 * c + t0
        return a16[p0:p0 + pn, o:o + tn]

    def qhv(b, p0=0, pn=128, t0=0, tn=T):
        o = 2 * (QH_O + 1024 * b) + t0
        return a16[p0:p0 + pn, o:o + tn]

    def khv(b, p0=0, pn=128, t0=0, tn=T):
        o = 2 * (KH_O + 1024 * b) + t0
        return a16[p0:p0 + pn, o:o + tn]

    def esb(b):
        o = 2 * (ES_O8 + 256 * b)
        return a16[:, o:o + 512]

    def v1l(ti, h2, c0=0, cn=65):
        o = 2 * V1L_O8 + 130 * ti + 65 * h2 + c0
        return a16[:, o:o + cn]

    ES_O8 = RA_O
    V1L_O8 = RA_O + 512
    MSK_O8 = RA_O + 1552
    RC_O8 = RA_O + 2576
    UQ_O8 = RA_O + 3088
    SEL_O8 = RA_O + 3472
    QKS_O8 = RA_O + 4960

    if PHASES >= 6.5:
        for ci, (nm, cc_) in enumerate((("cos_rep", 0), ("cos_rep", 1),
                                        ("sin_rep", 0), ("sin_rep", 1))):
            em("y", lambda e, _n=nm, _c=cc_, _i=ci: e.dma_start(
                out=f32v(TBL_O + 2048 * _i, 2048), in_=di[_n][_c]),
                inc=True, sem=p.dsem("tbl"))
        t_wdq = em("g", lambda e: e.dma_start(
            out=b16v(RA_O, 2048).rearrange("p (k o) -> p k o", k=8), in_=di["l1_wdq"]),
            inc=True, sem=p.dsem("l1w"))
        t_wdkv = em("g", lambda e: e.dma_start(
            out=b16v(RA_O + 1024, 2048).rearrange("p (k o) -> p k o", k=8), in_=di["l1_wdkv"]),
            inc=True, sem="l1w")
        t_wqr = em("g", lambda e: e.dma_start(
            out=b16v(RA_O + 2048, 1024).rearrange("p (k o) -> p k o", k=2), in_=di["l1_wqr"]),
            inc=True, sem="l1w")
        t_wkr = em("g", lambda e: e.dma_start(
            out=b16v(RA_O + 2560, 256).rearrange("p (k o) -> p k o", k=8), in_=di["l1_wkr"]),
            inc=True, sem="l1w")
        t_qrs = em("g", lambda e: e.dma_start(
            out=b16v(QKS_O8, 512).rearrange("p (m j) -> p m j", m=16),
            in_=di["qrsel"].rearrange("m v p j -> p (m v) j")), inc=True, sem="l1w")
        t_krs = em("g", lambda e: e.dma_start(
            out=b16v(QKS_O8 + 256, 64, 0, 16).rearrange("p (v j) -> p v j", v=2),
            in_=di["krsel"].rearrange("v p j -> p v j")), inc=True, sem="l1w")

        def wdq_(k, o0, on):
            return b16v(RA_O, 2048).rearrange("p (k o) -> p k o", k=8)[:, k, o0:o0 + on]

        def wdkv_(k, o0, on):
            return b16v(RA_O + 1024, 2048).rearrange("p (k o) -> p k o", k=8)[:, k, o0:o0 + on]

        def wqr_(k, o0, on):
            return b16v(RA_O + 2048, 1024).rearrange("p (k o) -> p k o", k=2)[:, k, o0:o0 + on]

        def qrsel_(m, v):
            return b16v(QKS_O8, 512).rearrange("p (m j) -> p m j", m=16)[:, 2 * m + v, :]

        def krsel_(v):
            return b16v(QKS_O8 + 256, 64, 0, 16).rearrange("p (v j) -> p v j", v=2)[:, v, :]

        prev_ev = None
        for dsti, (dst, wf, wt_) in enumerate(((cqt, wdq_, t_wdq), (ckvt, wdkv_, t_wdkv))):
            for oc in range(2):
                for tt in range(4):
                    hv = half(tt // 2, tt % 2)
                    for kc in range(8):
                        em("t", lambda e, _h=hv, _w=wf, _k=kc, _o=oc, _tt=tt,
                           _st=(kc == 0), _sp=(kc == 7):
                           e.matmul(_h, _w(_k, 128 * _o, 128), xnt(_k, 512 * _tt, 512),
                                    start=_st, stop=_sp),
                           inc=(kc == 7), waits=[wt_, prev_ev])
                    mmt = p.tok("t")
                    prev_ev = em("v", lambda e, _d=dst(oc, 512 * tt, 512), _s=hv:
                                 e.tensor_copy(_d, _s), inc=True, waits=[mmt], drain=True)
        cq_done = prev_ev
        for jc in range(4):
            for tt in range(4):
                hv = half(2 + tt // 2, tt % 2)
                for kc in range(2):
                    em("t", lambda e, _h=hv, _j=jc, _k=kc, _tt=tt,
                       _st=(kc == 0), _sp=(kc == 1):
                       e.matmul(_h, wqr_(_k, 128 * _j, 128), cqt(_k, 512 * _tt, 512),
                                start=_st, stop=_sp),
                       inc=(kc == 1), waits=[t_wqr, cq_done, prev_ev])
                mmt = p.tok("t")
                prev_ev = em("v", lambda e, _j=jc, _tt=tt, _s=hv: e.tensor_copy(
                    a16[:, 2 * QRP_O + 2048 * _j + 512 * _tt:
                        2 * QRP_O + 2048 * _j + 512 * _tt + 512], _s),
                    inc=True, waits=[mmt], drain=True)
        kx1 = lambda t0, tn: a16[0:16, 2 * RA_O + t0:2 * RA_O + t0 + tn]
        kx2 = lambda t0, tn: a16[0:16, 2 * (RA_O + 1024) + t0:2 * (RA_O + 1024) + t0 + tn]
        kb1 = f32v(RA_O + 2688, 512, 0, 16)
        kb2 = f32v(RA_O + 3200, 512, 0, 16)
        ktmp = a16[0:16, 2 * (RA_O + 3712):2 * (RA_O + 3712) + 512]
        cos16 = lambda t0, tn: f32v(TBL_O + t0, tn, 0, 16)
        sin16 = lambda t0, tn: f32v(TBL_O + 4096 + t0, tn, 0, 16)
        for tt in range(4):
            kp1 = PAIRS[0].ap()[0:16, 0:512]
            kp2 = PAIRS[0].ap()[0:16, 512:1024]
            for kc in range(8):
                em("t", lambda e, _k=kc, _tt=tt, _st=(kc == 0), _sp=(kc == 7):
                   e.matmul(kp1, b16v(RA_O + 2560, 256).rearrange("p (k o) -> p k o", k=8)
                            [:, _k, 0:16], xnt(_k, 512 * _tt, 512), start=_st, stop=_sp),
                   waits=[t_wkr, prev_ev])
                em("t", lambda e, _k=kc, _tt=tt, _st=(kc == 0), _sp=(kc == 7):
                   e.matmul(kp2, b16v(RA_O + 2560, 256).rearrange("p (k o) -> p k o", k=8)
                            [:, _k, 16:32], xnt(_k, 512 * _tt, 512), start=_st, stop=_sp),
                   inc=(kc == 7), waits=[])
            mmt = p.tok("t")
            e1 = em("v", lambda e, _d=kb1: e.tensor_copy(_d, kp1), inc=True,
                    waits=[mmt], drain=True)
            e2_ = em("v", lambda e, _d=kb2: e.tensor_copy(_d, kp2), inc=True, drain=True)
            em("v", lambda e, _tt=tt: e.tensor_tensor(
                out=kx1(512 * _tt, 512), in0=kb1, in1=cos16(512 * _tt, 512), op=OP.mult),
               waits=[("tbl", 64)], drain=True)
            em("v", lambda e, _tt=tt: e.tensor_tensor(
                out=ktmp, in0=kb2, in1=sin16(512 * _tt, 512), op=OP.mult), drain=True)
            em("v", lambda e, _tt=tt: e.tensor_tensor(
                out=kx1(512 * _tt, 512), in0=kx1(512 * _tt, 512), in1=ktmp, op=OP.subtract),
               drain=True)
            em("v", lambda e, _tt=tt: e.tensor_tensor(
                out=kx2(512 * _tt, 512), in0=kb1, in1=sin16(512 * _tt, 512), op=OP.mult),
               drain=True)
            em("v", lambda e, _tt=tt: e.tensor_tensor(
                out=ktmp, in0=kb2, in1=cos16(512 * _tt, 512), op=OP.mult), drain=True)
            prev_ev = em("v", lambda e, _tt=tt: e.tensor_tensor(
                out=kx2(512 * _tt, 512), in0=kx2(512 * _tt, 512), in1=ktmp, op=OP.add),
                inc=True, drain=True)
        kr_done = prev_ev
        def tmpf(b, t0=0, tn=T):
            o = 2 * (RA_O + (2688 if b == 0 else 3936)) + t0
            return a16[:, o:o + tn]

        def qrp_(c):
            return a16[:, 2 * QRP_O + 2048 * c:2 * QRP_O + 2048 * c + 2048]

        rope_last = prev_ev
        for c in range(2):
            cosv = f32v(TBL_O + 2048 * c, 2048)
            sinv = f32v(TBL_O + 4096 + 2048 * c, 2048)
            em("v", lambda e, _c=c, _cv=cosv: e.tensor_tensor(out=tmpf(0), in0=qrp_(_c),
                                                              in1=_cv, op=OP.mult),
               waits=[("tbl", 64)], drain=True)
            em("v", lambda e, _c=c, _sv=sinv: e.tensor_tensor(out=tmpf(1), in0=qrp_(_c + 2),
                                                              in1=_sv, op=OP.mult), drain=True)
            em("v", lambda e, _c=c: e.tensor_tensor(out=qrav(_c), in0=tmpf(0), in1=tmpf(1),
                                                    op=OP.subtract), drain=True)
            em("v", lambda e, _c=c, _sv=sinv: e.tensor_tensor(out=tmpf(0), in0=qrp_(_c),
                                                              in1=_sv, op=OP.mult), drain=True)
            em("v", lambda e, _c=c, _cv=cosv: e.tensor_tensor(out=tmpf(1), in0=qrp_(_c + 2),
                                                              in1=_cv, op=OP.mult), drain=True)
            rope_last = em("v", lambda e, _c=c: e.tensor_tensor(out=qrav(_c + 2), in0=tmpf(0),
                                                                in1=tmpf(1), op=OP.add),
                           inc=True, drain=True)
        prev_ev = rope_last
        for b in range(2):
            for tt in range(4):
                hv = PAIRS[1].ap()[64:96, 0:512]
                em("t", lambda e, _h=hv, _tt=tt: e.matmul(_h, krsel_(0), kx1(512 * _tt, 512),
                                                          start=True, stop=False),
                   waits=[kr_done, t_krs, prev_ev])
                em("t", lambda e, _h=hv, _tt=tt: e.matmul(_h, krsel_(1), kx2(512 * _tt, 512),
                                                          start=False, stop=True), inc=True)
                mmt = p.tok("t")
                prev_ev = em("v", lambda e, _b=b, _tt=tt, _s=hv: e.tensor_copy(
                    khv(_b, 64, 32, 512 * _tt, 512), _s), inc=True, waits=[mmt], drain=True)
        p.barrier()
        t_msk = em("g", lambda e: e.dma_start(out=b16v(MSK_O8, 2048).rearrange(
            "p (m j) -> p m j", m=4), in_=di["l1_masks"].rearrange("m p j -> p m j")),
            inc=True, sem=p.dsem("l1m"))
        selw8 = []
        selw8.append(em("v", lambda e: e.memset(f32v(SEL_O8, 64, 64, 1), 1.0), drain=True))
        selw8.append(em("v", lambda e: e.memset(f32v(SEL_O8 + 64, 64, 64, 1), 0.0), drain=True))
        selw8.append(em("v", lambda e: e.memset(f32v(SEL_O8 + 128, 64, 32, 1), 0.0), drain=True))
        selw8.append(em("v", lambda e: e.memset(f32v(SEL_O8 + 128 + 64, 64, 32, 1), 1.0),
                        drain=True))
        selw8.append(em("v", lambda e: e.memset(
            a16[:, 2 * V1L_O8:2 * V1L_O8 + 2080].rearrange("p (t c) -> p t c", c=130)
            [:, :, 64:65], 1.0), drain=True))
        selw8.append(em("v", lambda e: e.memset(
            a16[:, 2 * V1L_O8:2 * V1L_O8 + 2080].rearrange("p (t c) -> p t c", c=130)
            [:, :, 129:130], 1.0), drain=True))
        tok_sel8 = em("v", lambda e: e.nop(), inc=True)
        E64e = f32v(SEL_O8, 128, 64, 1)
        E64o = f32v(SEL_O8 + 128, 128, 32, 1)
        uq_use = p.tok("t")
        v1_use8 = p.tok("t")
        rc_free = p.tok("t")
        qh_free = [p.tok("t"), p.tok("t")]
        kh_free = [p.tok("t"), p.tok("t")]
        es_free = [None, None]
        head_done8 = p.tok("v")
        av_done_pair = p.tok("s")
        for pr_ in range(8):
            tuq = em("g", lambda e, _p=pr_: e.dma_start(
                out=b16v(UQ_O8, 256).rearrange("p (k o) -> p k o", k=2),
                in_=di["l1_wuq"][:, :, 128 * _p:128 * _p + 128]), inc=True,
                waits=[uq_use], sem=p.dsem("uqkv"))
            tuk = em("g", lambda e, _p=pr_: e.dma_start(
                out=b16v(UQ_O8 + 128, 256).rearrange("p (k o) -> p k o", k=2),
                in_=di["l1_wuk"][:, :, 128 * _p:128 * _p + 128]), inc=True, sem="uqkv")
            tuv = em("g", lambda e, _p=pr_: e.dma_start(
                out=b16v(UQ_O8 + 256, 256).rearrange("p (k o) -> p k o", k=2),
                in_=di["l1_wuv"][:, :, 128 * _p:128 * _p + 128]), inc=True, sem="uqkv")

            def uq8(w, k, o0, on):
                return b16v(UQ_O8 + 128 * w, 256).rearrange("p (k o) -> p k o", k=2)[:, k, o0:o0 + on]

            v1ev8 = []
            for ti in range(NT):
                hvv = PAIRS[2].ap()[:, 512 * (ti % 2):512 * (ti % 2) + 128]
                for kc in range(2):
                    em("t", lambda e, _h=hvv, _k=kc, _ti=ti, _st=(kc == 0), _sp=(kc == 1):
                       e.matmul(_h, ckvt(_k, 128 * _ti, 128), uq8(2, _k, 0, 128),
                                start=_st, stop=_sp),
                       inc=(kc == 1), waits=[tuv, v1_use8, head_done8, cq_done,
                                             (v1ev8[ti - 2] if ti >= 2 else None)])
                mmv = p.tok("t")
                ev = em("v", lambda e, _ti=ti,
                        _d=v1l(_ti0 := ti, 0, 0, 130).rearrange("p (a c) -> p a c", c=65)[:, :, 0:64],
                        _s=hvv.rearrange("p (a c) -> p a c", c=64):
                        e.tensor_copy(_d, _s), inc=True, waits=[mmv, tok_sel8], drain=True)
                v1ev8.append(ev)
            v1_ready8 = v1ev8[-1]
            for h2 in range(2):
                h = 2 * pr_ + h2
                b = h2
                asm_toks = []
                for which, wsel, dstf, freetok in ((0, 0, qhv, qh_free[b]),
                                                   (1, 1, khv, kh_free[b])):
                    for tt in range(4):
                        hv = PAIRS[3].ap()[0:64, 512 * (tt % 2):512 * (tt % 2) + 512]
                        for kc in range(2):
                            em("t", lambda e, _h=hv, _w=which, _k=kc, _h2=h2, _tt=tt,
                               _st=(kc == 0), _sp=(kc == 1):
                               e.matmul(_h, uq8(_w, _k, 64 * _h2, 64), cqt(_k, 512 * _tt, 512)
                                        if _w == 0 else ckvt(_k, 512 * _tt, 512),
                                        start=_st, stop=_sp),
                               inc=(kc == 1),
                               waits=[tuq, tuk, freetok, head_done8,
                                      (asm_toks[-2] if len(asm_toks) >= 2 else None)])
                        mmt = p.tok("t")
                        tk = em("v", lambda e, _b=b, _f=dstf, _tt=tt, _s=hv: e.tensor_copy(
                            _f(_b, 0, 64, 512 * _tt, 512), _s), inc=True, waits=[mmt],
                            drain=True)
                        asm_toks.append(tk)
                    if which == 0:
                        for tt in range(4):
                            hv = PAIRS[3].ap()[64:96, 512 * (tt % 2):512 * (tt % 2) + 512]
                            em("t", lambda e, _h=hv, _m=h % 8, _c=h // 8, _tt=tt:
                               e.matmul(_h, qrsel_(_m, 0), qrav(_c, 512 * _tt, 512),
                                        start=True, stop=False),
                               waits=[t_qrs, rope_last, qh_free[b], head_done8,
                                      (asm_toks[-2] if len(asm_toks) >= 2 else None)])
                            em("t", lambda e, _h=hv, _m=h % 8, _c=2 + h // 8, _tt=tt:
                               e.matmul(_h, qrsel_(_m, 1), qrav(_c, 512 * _tt, 512),
                                        start=False, stop=True), inc=True)
                            mmt = p.tok("t")
                            tk = em("v", lambda e, _b=b, _tt=tt, _s=hv: e.tensor_copy(
                                qhv(_b, 64, 32, 512 * _tt, 512), _s), inc=True,
                                waits=[mmt], drain=True)
                            asm_toks.append(tk)
                asm_done = asm_toks[-1]
                es_guard = v1_use8
                es_used = [None, None]
                av_toks = []
                esi = 0
                nd_stop_tok = [None] * 4
                for kc in range(NT):
                    jmin = kc // 4
                    sblks = []
                    for j in range(jmin, 4):
                        sp_i = (kc + j) % 2
                        sps = PAIRS[2].ap()[:, 512 * sp_i:512 * sp_i + 512]
                        em("t", lambda e, _o=sps, _b=b, _kc=kc, _j=j:
                           e.matmul(_o, khv(_b, 0, 96, 128 * _kc, 128),
                                    qhv(_b, 0, 96, 512 * _j, 512), start=True, stop=True),
                           inc=True,
                           waits=[asm_done, v1_ready8,
                                  (sblks[-2][1] if len(sblks) >= 2 else es_used[sp_i])])
                        smt = p.tok("t")
                        if j == kc // 4:
                            ct = em("v", lambda e, _o=sps, _i=kc % 4: e.scalar_tensor_tensor(
                                out=_o, in0=_o, scalar=-80.0,
                                in1=a16[:, 2 * (MSK_O8 + 256 * _i):2 * (MSK_O8 + 256 * _i) + 512],
                                op0=OP.max, op1=OP.min), inc=True,
                                waits=[smt, t_msk], drain=True)
                        else:
                            ct = em("v", lambda e, _o=sps: e.tensor_scalar(
                                out=_o, in0=_o, scalar1=80.0, scalar2=-80.0,
                                op0=OP.min, op1=OP.max), inc=True, waits=[smt], drain=True)
                        eb = esi % 2
                        et = em("s", lambda e, _d=esb(eb), _s=sps: e.activation(
                            out=_d, in_=_s, func=AF.Exp, bias=0.0, scale=SCALE_L1),
                            inc=True,
                            waits=[ct, (av_toks[-2][1] if len(av_toks) >= 2 else es_guard)],
                            drain=True)
                        es_used[0 if False else (esi % 2)] = et
                        sblks.append((j, et))
                        esi += 1
                        pj, hj = (j // 2, j % 2)
                        if h2 == 0:
                            outp = PAIRS[pj].ap()[0:65, 512 * hj:512 * hj + 512]
                            avt = em("t", lambda e, _o=outp, _ti=kc, _h2=h2, _eb=eb,
                                     _st=(kc == 0), _sp=(kc == 4 * j + 3):
                                     e.matmul(_o, v1l(_ti, _h2), esb(_eb), start=_st, stop=_sp),
                                     inc=True, waits=[et, av_done_pair])
                        else:
                            outp = PAIRS[pj].ap()[64:128, 512 * hj:512 * hj + 512]
                            em("t", lambda e, _o=outp, _ti=kc, _h2=h2, _eb=eb,
                               _st=(kc == 0), _sp=(kc == 4 * j + 3):
                               e.matmul(_o, v1l(_ti, _h2, 0, 64), esb(_eb), start=_st,
                                        stop=_sp), waits=[et, av_done_pair])
                            dout = PAIRS[pj].ap()[32:33, 512 * hj:512 * hj + 512]
                            avt = em("t", lambda e, _o=dout, _ti=kc, _h2=h2, _eb=eb,
                                     _st=(kc == 0), _sp=(kc == 4 * j + 3):
                                     e.matmul(_o, v1l(_ti, _h2, 64, 1), esb(_eb),
                                              start=_st, stop=_sp), inc=True)
                        av_toks.append((j, avt))
                        if kc == 4 * j + 3:
                            nd_stop_tok[j] = avt
                last_av = av_toks[-1][1]
                v1_use8 = last_av
                uq_use = last_av
                hv_toks = []
                for j in range(4):
                    pj, hj = (j // 2, j % 2)
                    ndt = nd_stop_tok[j]
                    if h2 == 0:
                        nsrc = PAIRS[pj].ap()[0:64, 512 * hj:512 * hj + 512]
                        dsrc = PAIRS[pj].ap()[64:65, 512 * hj:512 * hj + 512]
                        rcp = 64
                        esel = E64e
                    else:
                        nsrc = PAIRS[pj].ap()[64:128, 512 * hj:512 * hj + 512]
                        dsrc = PAIRS[pj].ap()[32:33, 512 * hj:512 * hj + 512]
                        rcp = 32
                        esel = E64o
                    ev = em("v", lambda e, _d=atn1(pr_, 512 * j, 512, 64 * h2, 64), _s=nsrc:
                            e.tensor_copy(_d, _s), inc=True, waits=[ndt], drain=True)
                    rc = f32v(RC_O8, 512, rcp, 1)
                    em("s", lambda e, _d=rc, _s=dsrc: e.activation(
                        out=_d, in_=_s, func=AF.Ln, bias=0.0, scale=1.0),
                       waits=[ndt, rc_free], drain=True)
                    t2 = em("s", lambda e, _d=rc: e.activation(
                        out=_d, in_=_d, func=AF.Exp, bias=0.0, scale=-1.0),
                        inc=True, drain=True)
                    rtok = p.tok("s")
                    bcp = PAIRS[pj].ap()[:, 512 * hj:512 * hj + 512]
                    mt = em("t", lambda e, _o=bcp, _sl=esel, _r=rc: e.matmul(
                        _o, _sl, _r, start=True, stop=True),
                        inc=True, waits=[rtok, tok_sel8, ev])
                    bct = p.tok("t")
                    rc_free = bct
                    t4 = em("v", lambda e, _a=atn1(pr_, 512 * j, 512, 64 * h2, 64),
                            _b=bcp[64 * h2:64 * h2 + 64, :]:
                            e.tensor_tensor(out=_a, in0=_a, in1=_b, op=OP.mult),
                            inc=True, waits=[bct], drain=True)
                    hv_toks.append(t4)
                head_done8 = hv_toks[-1]
                qh_free[b] = last_av
                kh_free[b] = last_av
                av_done_pair = p.tok("s")
        wo_use8 = [head_done8, head_done8]
        fold_evs8 = []
        for grp in range(4):
            for kc in range(8):
                wb = kc % 2
                wo_dst = b16v(QH_O if wb == 0 else KH_O, 2048)[:, 0:1024] if False else \
                    a16[:, 2 * (QH_O if wb == 0 else KH_O):2 * (QH_O if wb == 0 else KH_O) + 1024]
                wot = em("g", lambda e, _d=wo_dst, _k=kc: e.dma_start(out=_d, in_=di["l1_wo"][_k]),
                         inc=True, waits=[wo_use8[wb], head_done8], sem=p.dsem("wo8%d" % wb))
                for t_ in range(4):
                    ti = 4 * grp + t_
                    pv = PAIRS[t_].ap()
                    for hh in range(2):
                        em("t", lambda e, _h=half(t_, hh), _k=kc, _ti=ti, _hh=hh, _w=wo_dst,
                           _st=(kc == 0), _sp=(kc == 7):
                           e.matmul(_h, atn1(_k, 128 * _ti, 128),
                                    _w[:, 512 * _hh:512 * _hh + 512], start=_st, stop=_sp),
                           inc=((kc == 7 and hh == 1) or (t_ == 3 and hh == 1)),
                           waits=[wot, head_done8,
                                  (fold_evs8[4 * (grp - 1) + t_] if (grp > 0 and kc == 0) else None)])
                    if kc == 7:
                        mmt = p.tok("t")
                        ev = em("v", lambda e, _x=XTt(ti), _s=pv:
                                e.tensor_tensor(out=_x, in0=_x, in1=_s, op=OP.add),
                                inc=True, waits=[mmt], drain=True)
                        fold_evs8.append(ev)
                wo_use8[wb] = p.tok("t")
        p.barrier()
    p.views["atn1"] = (a16[:, 2 * ATN1_O:2 * ATN1_O + 16384], (128, 16384))
    p.views["cq1"] = (a16[:, 2 * CQT_O:2 * CQT_O + 4096], (128, 4096))
    p.views["qh"] = (a16[:, 2 * QH_O:2 * QH_O + 4096], (128, 4096))
    p.views["kh"] = (a16[:, 2 * KH_O:2 * KH_O + 4096], (128, 4096))
    p.views["v1l8"] = (a16[:, 2 * V1L_O8:2 * V1L_O8 + 4160], (128, 4160))
    p.views["ckv1"] = (a16[:, 2 * CKV_O:2 * CKV_O + 4096], (128, 4096))
    p.views["qra"] = (a16[:, 2 * QRA_O:2 * QRA_O + 8192], (128, 8192))
    p.views["kr1"] = (a16[:, 2 * RA_O:2 * RA_O + 4096], (128, 4096))

    if PHASES >= 7:
        H1C_O = FT_O
        CNS_O = FT_O + 8192
        CN3_O = calloc(512)
        CN4_O = calloc(2)
        TQ_O = FT_O + 7936
        assert CNS_O + 1024 <= AREN
        t_cn3 = em("g", lambda e: e.dma_start(
            out=b16v(CN3_O, 1024).rearrange("p (k o) -> p k o", k=4), in_=di["cn_w3"]),
            inc=True, sem=p.dsem("cnw"))
        t_cn4 = em("g", lambda e: e.dma_start(
            out=b16v(CN4_O, 2).rearrange("p (k o) -> p k o", k=2), in_=di["cn_w4"]),
            inc=True, sem="cnw")
        XSTG_O = FT_O
        tr_ev = None
        cast_ev = None
        for i in range(NT):
            for g in range(2):
                pq = g
                hvg = half(pq, 0)
                stg4 = f32v(XSTG_O, 512)
                cts = []
                for q in range(4):
                    c = 4 * g + q
                    cts.append(em("v", lambda e, _d=f32v(XSTG_O + 128 * q, 128),
                                  _s=xnt(c, 128 * i, 128), _w=tr_ev:
                                  e.tensor_copy(_d, _s), inc=(q == 3),
                                  waits=[tr_ev], drain=True))
                cast_ev = cts[-1]
                last = None
                for q in range(4):
                    last = em("t", lambda e, _o=hvg[:, 128 * q:128 * q + 128],
                              _s=f32v(XSTG_O + 128 * q, 128), _st=(q == 0), _sp=(q == 3):
                              e.matmul(_o, _s, identv, is_transpose=True, start=_st, stop=_sp),
                              inc=(q == 3), waits=[tok_ident, cast_ev, tr_ev])
                tr_ev = em("v", lambda e, _d=f32v(ACC_O + 1024 * i + 512 * g, 512), _s=hvg:
                           e.tensor_copy(_d, _s), inc=True, waits=[last], drain=True)
        p.barrier()
        for cyc in range(3):
            if cyc == 0:
                def outt(c, t0=0, tn=T):
                    return xnt(c, t0, tn)
            else:
                def outt(c, t0=0, tn=T):
                    return xnt(c, t0, tn)
                tr_ev = None
                for i in range(NT):
                    pq = i % 2
                    hv = half(pq, 0)
                    last = None
                    for g in range(2):
                        hvg = half(pq, g)
                        for q in range(4):
                            c = 4 * g + q
                            last = em("t", lambda e, _o=hvg[:, 128 * q:128 * q + 128],
                                      _s=f32v(ACC_O + 1024 * i + 128 * c, 128),
                                      _st=(q == 0), _sp=(q == 3):
                                      e.matmul(_o, _s, identv, is_transpose=True,
                                               start=_st, stop=_sp),
                                      inc=(q == 3), waits=[tok_ident, tr_ev])
                        dst = xnt_all[:, 4 * g:4 * g + 4, 128 * i:128 * i + 128]
                        tr_ev = em("v", lambda e, _d=dst,
                                   _s=hvg.rearrange("p (c t) -> p c t", t=128):
                                   e.tensor_copy(_d, _s), inc=True, waits=[last], drain=True)
                p.barrier()
            cn_use = [p.tok("t"), p.tok("t")]
            prev_ev = None
            for oc in range(8):
                cb = oc % 2
                w1t = em("g", lambda e, _b=cb, _o=oc: e.dma_start(
                    out=b16v(CNS_O + 512 * _b, 1024).rearrange("p (k j) -> p k j", k=8),
                    in_=di["cn_w1"][_o]), inc=True, waits=[cn_use[cb]], sem=p.dsem("cn%d" % cb))
                for tt in range(4):
                    hv = half(2 + tt // 2, tt % 2)
                    for kc in range(8):
                        em("t", lambda e, _h=hv, _b=cb, _k=kc, _tt=tt,
                           _st=(kc == 0), _sp=(kc == 7):
                           e.matmul(_h, b16v(CNS_O + 512 * _b, 1024)
                                    .rearrange("p (k j) -> p k j", k=8)[:, _k, :],
                                    outt(_k, 512 * _tt, 512), start=_st, stop=_sp),
                           inc=(kc == 7), waits=[w1t, prev_ev])
                    mmt = p.tok("t")
                    cn_use[cb] = mmt
                    prev_ev = em("s", lambda e, _o=oc, _tt=tt, _sv=hv,
                                 _b=f32v(CNB_O + oc, 1), _d=a16[:, 2 * H1C_O + 2048 * oc + 512 * tt:
                                                               2 * H1C_O + 2048 * oc + 512 * tt + 512]:
                                 e.activation(out=_d, in_=_sv, func=AF.Gelu, bias=_b, scale=1.0),
                                 inc=True, waits=[mmt, tok_cnb], drain=True)
            h1_done = prev_ev

            def h1ct(c, t0=0, tn=T):
                return a16[:, 2 * H1C_O + 2048 * c + t0:2 * H1C_O + 2048 * c + t0 + tn]

            for oc in range(4):
                cb = oc % 2
                w2t = em("g", lambda e, _b=cb, _o=oc: e.dma_start(
                    out=b16v(CNS_O + 512 * _b, 1024).rearrange("p (k j) -> p k j", k=8),
                    in_=di["cn_w2"][_o]), inc=True, waits=[cn_use[cb], h1_done],
                    sem="cn%d" % cb)
                for tt in range(4):
                    hv = half(2 + tt // 2, tt % 2)
                    for kc in range(8):
                        em("t", lambda e, _h=hv, _b=cb, _k=kc, _tt=tt,
                           _st=(kc == 0), _sp=(kc == 7):
                           e.matmul(_h, b16v(CNS_O + 512 * _b, 1024)
                                    .rearrange("p (k j) -> p k j", k=8)[:, _k, :],
                                    h1ct(_k, 512 * _tt, 512), start=_st, stop=_sp),
                           inc=(kc == 7), waits=[w2t, h1_done, prev_ev])
                    mmt = p.tok("t")
                    cn_use[cb] = mmt
                    prev_ev = em("s", lambda e, _o=oc, _tt=tt, _sv=hv,
                                 _b=f32v(CNB_O + 8 + oc, 1),
                                 _d=xnt(oc, 512 * tt, 512):
                                 e.activation(out=_d, in_=_sv, func=AF.Gelu, bias=_b, scale=1.0),
                                 inc=True, waits=[mmt], drain=True)
            h2_done = prev_ev
            for oc in range(2):
                for tt in range(4):
                    hv = half(2 + tt // 2, tt % 2)
                    for kc in range(4):
                        em("t", lambda e, _h=hv, _k=kc, _o=oc, _tt=tt,
                           _st=(kc == 0), _sp=(kc == 3):
                           e.matmul(_h, b16v(CN3_O, 1024).rearrange("p (k o) -> p k o", k=4)
                                    [:, _k, 128 * _o:128 * _o + 128],
                                    xnt(_k, 512 * _tt, 512), start=_st, stop=_sp),
                           inc=(kc == 3), waits=[t_cn3, h2_done, prev_ev])
                    mmt = p.tok("t")
                    prev_ev = em("s", lambda e, _o=oc, _tt=tt, _sv=hv,
                                 _b=f32v(CNB_O + 12 + oc, 1),
                                 _d=xnt(4 + oc, 512 * tt, 512):
                                 e.activation(out=_d, in_=_sv, func=AF.Gelu, bias=_b, scale=1.0),
                                 inc=True, waits=[mmt], drain=True)
            h3_done = prev_ev
            tfps = PAIRS[0].ap()[:, 0:16]
            for ti in range(NT):
                for kc in range(2):
                    em("t", lambda e, _ti=ti, _k=kc, _st=(kc == 0), _sp=(kc == 1):
                       e.matmul(tfps[:, _ti:_ti + 1],
                                xnt(4 + _k, 128 * _ti, 128),
                                b16v(CN4_O, 2).rearrange("p (k o) -> p k o", k=2)[:, _k, :],
                                start=_st, stop=_sp),
                       inc=(kc == 1 and ti == 15), waits=[t_cn4, h3_done])
            mmt = p.tok("t")
            tfe = em("s", lambda e: e.activation(out=f32v(COL_O + 9, 16), in_=tfps,
                                                 func=AF.Sigmoid, bias=f32v(COL_O + 2, 1),
                                                 scale=1.0), inc=True, waits=[mmt], drain=True)
            em("v", lambda e: e.tensor_scalar(out=f32v(COL_O + 25, 16), in0=f32v(COL_O + 9, 16),
                                              scalar1=f32v(COL_O + 1, 1), scalar2=None,
                                              op0=OP.mult), waits=[tfe], drain=True)
            em("v", lambda e: e.tensor_scalar(out=f32v(COL_O + 25, 16), in0=f32v(COL_O + 25, 16),
                                              scalar1=f32v(COL_O + 0, 1), scalar2=None,
                                              op0=OP.add), drain=True)
            cpe = em("v", lambda e: e.tensor_scalar(out=f32v(COL_O + 41, 16),
                                                    in0=f32v(COL_O + 25, 16),
                                                    scalar1=-1.0, scalar2=None, op0=OP.mult),
                     inc=True, drain=True)
            tw_last = None
            for ti in range(NT):
                cp = f32v(COL_O + 25 + ti, 1)
                ncp = f32v(COL_O + 41 + ti, 1)
                base = ACC_O + 1024 * ti

                em("v", lambda e, _ti=ti, _c=cp, _b=base: e.scalar_tensor_tensor(
                    out=f32v(TQ_O, 256), in0=f32v(_b + 512, 256), scalar=_c,
                    in1=f32v(_b, 256), op0=OP.mult, op1=OP.add), waits=[cpe, tw_last],
                   drain=True)
                em("v", lambda e, _ti=ti, _n=ncp, _b=base: e.scalar_tensor_tensor(
                    out=f32v(_b + 512, 256), in0=f32v(_b, 256), scalar=_n,
                    in1=f32v(_b + 512, 256), op0=OP.mult, op1=OP.add), drain=True)
                em("v", lambda e, _b=base: e.tensor_copy(f32v(_b, 256), f32v(TQ_O, 256)),
                   drain=True)
                em("v", lambda e, _ti=ti, _n=ncp, _b=base: e.scalar_tensor_tensor(
                    out=f32v(TQ_O, 256), in0=f32v(_b + 768, 256), scalar=_n,
                    in1=f32v(_b + 256, 256), op0=OP.mult, op1=OP.add), drain=True)
                em("v", lambda e, _ti=ti, _c=cp, _b=base: e.scalar_tensor_tensor(
                    out=f32v(_b + 768, 256), in0=f32v(_b + 256, 256), scalar=_c,
                    in1=f32v(_b + 768, 256), op0=OP.mult, op1=OP.add), drain=True)
                tw_last = em("v", lambda e, _b=base: e.tensor_copy(
                    f32v(_b + 256, 256), f32v(TQ_O, 256)), inc=True, drain=True)
            p.barrier()
        for i in range(NT):
            em("v", lambda e, _i=i: e.tensor_tensor(out=XTt(_i), in0=XTt(_i), in1=acc_t(_i),
                                                    op=OP.add), inc=(i == 15), drain=True)
        p.barrier()

    st = em("y", lambda e: e.dma_start(out=out_h.rearrange("(i p) c -> p i c", p=128), in_=xt3),
            inc=True, waits=[p.tok("v"), p.tok("s"), p.tok("t")], sem=p.dsem("st"))
    em("y", lambda e: e.nop(), waits=[st])

    for name in dbg:
        view, shp = p.views[name]
        dh = nc.dram_tensor("dbg_" + name, list(shp),
                            F32 if view.tensor.dtype == F32 else F32, kind="ExternalOutput").ap()
        qe = "y" if view.dtype == F32 else "g"
        dt_ = em(qe, lambda e, _v=view, _h=dh: e.dma_start(out=_h, in_=_v), inc=True,
                 sem=p.dsem("dbg" + qe))
        em(qe, lambda e: e.nop(), waits=[dt_])

    with nc.Block() as block:
        @block.tensor
        def _(e):
            for fn in p.q["t"]:
                fn(e)

        @block.vector
        def _(e):
            for fn in p.q["v"]:
                fn(e)

        @block.scalar
        def _(e):
            for fn in p.q["s"]:
                fn(e)

        @block.gpsimd
        def _(e):
            for fn in p.q["g"]:
                fn(e)

        @block.sync
        def _(e):
            for fn in p.q["y"]:
                fn(e)

    stack.close()
    return nc


def kernel(**inputs):
    common = _prep_common(inputs)
    x = np.asarray(inputs["L0_attn_x"], np.float32)
    nc = build_nc(dbg=tuple(DBG))
    in_maps = []
    for b in range(8):
        m = dict(common)
        m["x"] = np.ascontiguousarray(x[b])
        in_maps.append(m)
    res = run_bass_kernel_spmd(nc, in_maps, core_ids=list(range(8)))
    return np.stack([res.results[b]["out"] for b in range(8)], axis=0)


# version 03
# speedup vs baseline: 1.0328x; 1.0025x over previous
import numpy as np
import ml_dtypes
import concourse.bass as bass
import concourse.mybir as mybir
from concourse.bass_utils import run_bass_kernel_spmd
from contextlib import ExitStack

F32 = mybir.dt.float32
BF16 = mybir.dt.bfloat16
AF = mybir.ActivationFunctionType
OP = mybir.AluOpType
AX = mybir.AxisListType

T = 2048
D = 1024
NT = 16
NE = 8
EPS_LN = 1e-5
EPS_MOE_LN = 1e-12
PHI_EPS = 6.25e-8
PHI_BIAS = -2.772588722239781
INV_S = 0.3535533905932738
CQ = -0.17677669529663687
SCALE_L1 = 0.10206207261596575

PHASES = 99
NCYC = 3
DEBUG_SERIAL = False
DBG = []



def _prep_common(i):
    d = {}
    cc = np.ascontiguousarray

    def nat(w, kcn):
        w = np.asarray(w, np.float32)
        return cc(w.reshape(kcn, 128, w.shape[1]).transpose(1, 0, 2))

    def ocsl(w, kcn, ocn):
        w = np.asarray(w, np.float32)
        return cc(w.reshape(kcn, 128, ocn, 128).transpose(2, 0, 1, 3))

    Wq = np.asarray(i["L0_attn_Wq"], np.float32)
    Wk = np.asarray(i["L0_attn_Wk"], np.float32)
    Wv = np.asarray(i["L0_attn_Wv"], np.float32)
    Wo = np.asarray(i["L0_attn_Wo"], np.float32)
    d["l0_wqT"] = cc(Wq.reshape(8, 128, 8, 128).transpose(0, 2, 3, 1))
    d["l0_wkT"] = cc(Wk.reshape(8, 128, 8, 128).transpose(0, 2, 3, 1))
    d["l0_wvT"] = cc(Wv.T.reshape(8, 128, 1024))
    d["l0_woT"] = cc(Wo.T.reshape(8, 128, 1024))
    d["l0_bqk"] = cc(np.stack([np.asarray(i["L0_attn_bq"], np.float32).reshape(8, 128).T,
                               np.asarray(i["L0_attn_bk"], np.float32).reshape(8, 128).T]))
    d["l0_bv"] = cc(np.asarray(i["L0_attn_bv"], np.float32)[None, :])
    d["l0_bo"] = cc(np.asarray(i["L0_attn_bo"], np.float32)[None, :])
    d["l0_orfT"] = cc(np.asarray(i["L0_attn_orf"], np.float32).T)

    Wr = np.asarray(i["L0_ffn_Wr"], np.float32)
    d["moe_wrT"] = cc(Wr.T.reshape(8, 128, 8))
    d["moe_br"] = cc(np.asarray(i["L0_ffn_br"], np.float32)[None, :])
    W1 = np.asarray(i["L0_ffn_W1"], np.float32)
    d["moe_w1"] = cc(W1.reshape(NE, 32, 128, 8, 128).transpose(0, 1, 4, 3, 2))
    W2 = np.asarray(i["L0_ffn_W2"], np.float32)
    d["moe_w2"] = cc(W2.transpose(0, 2, 1).reshape(NE, 32, 128, 1024))
    d["moe_b1"] = cc(np.asarray(i["L0_ffn_b1"], np.float32).reshape(NE * 32, 128).T)
    d["moe_b2"] = cc(np.asarray(i["L0_ffn_b2"], np.float32))
    d["moe_lnw"] = cc(np.asarray(i["L0_ffn_ln_w"], np.float32)[None, :])
    d["moe_lnb"] = cc(np.asarray(i["L0_ffn_ln_b"], np.float32)[None, :])

    d["l1_wdq"] = nat(i["L1_attn_W_DQ"], 8)
    d["l1_wdkv"] = nat(i["L1_attn_W_DKV"], 8)
    d["l1_wuq"] = nat(i["L1_attn_W_UQ"], 2)
    d["l1_wuk"] = nat(i["L1_attn_W_UK"], 2)
    d["l1_wuv"] = nat(i["L1_attn_W_UV"], 2)
    WQR = np.asarray(i["L1_attn_W_QR"], np.float32)
    perm = np.concatenate([np.arange(0, 32, 2) + 32 * h for h in range(16)] +
                          [np.arange(1, 32, 2) + 32 * h for h in range(16)])
    d["l1_wqr"] = nat(WQR[:, perm], 2)
    WKR = np.asarray(i["L1_attn_W_KR"], np.float32)
    permk = np.concatenate([np.arange(0, 32, 2), np.arange(1, 32, 2)])
    d["l1_wkr"] = nat(WKR[:, permk], 8)
    d["l1_wo"] = cc(np.asarray(i["L1_attn_W_O"], np.float32).reshape(8, 128, 1024))

    d["cn_w1"] = ocsl(i["L1_ffn_cn_w1"], 8, 8)
    d["cn_w2"] = ocsl(i["L1_ffn_cn_w2"], 8, 4)
    d["cn_w3"] = nat(i["L1_ffn_cn_w3"], 4)
    d["cn_w4"] = nat(i["L1_ffn_cn_w4"], 2)
    d["gc_w1"] = nat(i["L1_ffn_gc_w1"], 8)
    d["gc_w2"] = nat(i["L1_ffn_gc_w2"], 4)
    d["gc_w3"] = nat(i["L1_ffn_gc_w3"], 2)
    d["cn_b1"] = cc(np.asarray(i["L1_ffn_cn_b1"], np.float32).reshape(8, 128).T)
    d["cn_b2"] = cc(np.asarray(i["L1_ffn_cn_b2"], np.float32).reshape(4, 128).T)
    d["cn_b3"] = cc(np.asarray(i["L1_ffn_cn_b3"], np.float32).reshape(2, 128).T)
    d["cn_b4"] = cc(np.asarray(i["L1_ffn_cn_b4"], np.float32)[None, :])
    d["gc_b1"] = cc(np.asarray(i["L1_ffn_gc_b1"], np.float32).reshape(4, 128).T)
    d["gc_b2"] = cc(np.asarray(i["L1_ffn_gc_b2"], np.float32).reshape(2, 128).T)
    d["gc_b3"] = cc(np.asarray(i["L1_ffn_gc_b3"], np.float32)[None, :])
    d["l1_ar"] = cc(np.asarray(i["L1_ffn_adaptive_range"], np.float32)[None, :])

    d["ident"] = np.eye(128, dtype=np.float32)
    e2 = np.zeros((128, 2), np.float32)
    e2[:64, 0] = 1.0
    e2[64:, 1] = 1.0
    d["e2"] = e2
    inv = 10000.0 ** (-np.arange(0, 32, 2, dtype=np.float32) / 32.0)
    ang = np.arange(T, dtype=np.float32)[:, None] * inv[None, :]
    d["cos_rep"] = cc(np.tile(np.cos(ang).T, (16, 1)).reshape(2, 128, T))
    d["sin_rep"] = cc(np.tile(np.sin(ang).T, (16, 1)).reshape(2, 128, T))
    qrs = np.zeros((8, 2, 128, 32), np.float32)
    for m_ in range(8):
        for j_ in range(16):
            qrs[m_, 0, 16 * m_ + j_, j_] = 1.0
            qrs[m_, 1, 16 * m_ + j_, 16 + j_] = 1.0
    d["qrsel"] = cc(qrs)
    krs = np.zeros((2, 16, 32), np.float32)
    for j_ in range(16):
        krs[0, j_, j_] = 1.0
        krs[1, j_, 16 + j_] = 1.0
    d["krsel"] = cc(krs)
    cq2 = np.zeros((2, 2, 128), np.float32)
    cq2[0, 0, :] = CQ; cq2[1, 1, :] = CQ
    d["cqsel"] = cq2
    ks2 = np.zeros((2, 2, 256), np.float32)
    ks2[0, 0, :] = CQ; ks2[1, 1, :] = CQ
    d["ksel"] = ks2
    masks = np.empty((4, 128, 512), np.float32)
    kk = np.arange(128)[:, None]
    qq = np.arange(512)[None, :]
    for mi in range(4):
        masks[mi] = np.where(kk <= qq - 128 * mi, 80.0, -1e9)
    d["l1_masks"] = masks
    return d



ENGS = ("t", "v", "s", "g", "y")


class Prog:

    def __init__(self, nc, stack):
        self.nc = nc
        self.stack = stack
        self.q = {k: [] for k in ENGS}
        self.cnt = {k: 0 for k in ENGS}
        self.step = {k: (16 if k in ("g", "y") else 1) for k in ENGS}
        self.waited = {k: {} for k in ENGS}
        self.sem = {k: stack.enter_context(nc.semaphore("sem_" + k)) for k in ENGS}
        self.isdma = {k: False for k in ENGS}
        self.views = {}

    def dsem(self, name):
        if name not in self.sem:
            self.sem[name] = self.stack.enter_context(self.nc.semaphore("sem_" + name))
            self.cnt[name] = 0
            self.step[name] = 16
            self.isdma[name] = True
        return name

    def em(self, eng, mk, inc=False, waits=(), drain=False, sem=None):
        wl = []
        wd = self.waited[eng]
        for tk in waits:
            if tk is None:
                continue
            pe, v = tk
            if pe == eng:
                continue
            if self.isdma.get(pe, False):
                v = self.cnt[pe]
            if v > 0 and wd.get(pe, 0) < v:
                wd[pe] = v
                wl.append((pe, v))
        sname = sem if sem is not None else eng
        if inc:
            self.cnt[sname] += self.step[sname]
        tok = (sname, self.cnt[sname])
        sems = self.sem

        def fn(e, _w=tuple(wl), _mk=mk, _inc=inc, _dr=drain, _sem=sems[sname],
               _stp=self.step[sname]):
            for (pe, v) in _w:
                e.wait_ge(sems[pe], v)
            if _dr:
                e.drain()
            ins = _mk(e)
            if _inc:
                ins.then_inc(_sem, _stp)
        self.q[eng].append(fn)
        return tok

    def tok(self, eng):
        return (eng, self.cnt[eng])

    def bsem(self, name):
        if name not in self.sem:
            self.sem[name] = self.stack.enter_context(self.nc.semaphore("sem_" + name))
            self.cnt[name] = 0
            self.step[name] = 1
            self.isdma[name] = False
        return name

    def barrier(self):
        toks = []
        for k in ENGS:
            toks.append(self.em(k, lambda e: e.nop(), inc=True, sem=self.bsem("b" + k)))
        allsems = [(s, self.cnt[s]) for s in list(self.cnt)
                   if self.cnt[s] > 0 and not s.startswith("b")]
        for k in ENGS:
            self.em(k, lambda e: e.nop(),
                    waits=[t for t, kk in zip(toks, ENGS) if kk != k] + allsems)


def build_nc(dbg=()):
    nc = bass.Bass("TRN2", target_bir_lowering=False)
    stack = ExitStack()
    p = Prog(nc, stack)
    em = p.em

    di = {}

    def inp(name, shape):
        di[name] = nc.dram_tensor(name, list(shape), F32, kind="ExternalInput").ap()

    inp("x", (T, D))
    inp("l0_wqT", (8, 8, 128, 128)); inp("l0_wkT", (8, 8, 128, 128))
    inp("l0_wvT", (8, 128, 1024)); inp("l0_woT", (8, 128, 1024))
    inp("l0_bqk", (2, 128, 8)); inp("l0_bv", (1, 1024)); inp("l0_bo", (1, 1024))
    inp("l0_orfT", (64, 256))
    inp("moe_wrT", (8, 128, 8)); inp("moe_br", (1, 8))
    inp("moe_w1", (NE, 32, 128, 8, 128)); inp("moe_w2", (NE, 32, 128, 1024))
    inp("moe_b1", (128, 256)); inp("moe_b2", (8, 1024))
    inp("moe_lnw", (1, 1024)); inp("moe_lnb", (1, 1024))
    inp("l1_wdq", (128, 8, 256)); inp("l1_wdkv", (128, 8, 256))
    inp("l1_wuq", (128, 2, 1024)); inp("l1_wuk", (128, 2, 1024)); inp("l1_wuv", (128, 2, 1024))
    inp("l1_wqr", (128, 2, 512)); inp("l1_wkr", (128, 8, 32))
    inp("l1_wo", (8, 128, 1024))
    inp("cn_w1", (8, 8, 128, 128)); inp("cn_w2", (4, 8, 128, 128))
    inp("cn_w3", (128, 4, 256)); inp("cn_w4", (128, 2, 1))
    inp("gc_w1", (128, 8, 512)); inp("gc_w2", (128, 4, 256)); inp("gc_w3", (128, 2, 1))
    inp("cn_b1", (128, 8)); inp("cn_b2", (128, 4)); inp("cn_b3", (128, 2)); inp("cn_b4", (1, 1))
    inp("gc_b1", (128, 4)); inp("gc_b2", (128, 2)); inp("gc_b3", (1, 1)); inp("l1_ar", (1, 1))
    inp("ident", (128, 128)); inp("e2", (128, 2))
    inp("cos_rep", (2, 128, T)); inp("sin_rep", (2, 128, T))
    inp("cqsel", (2, 2, 128)); inp("ksel", (2, 2, 256))
    inp("qrsel", (8, 2, 128, 32)); inp("krsel", (2, 16, 32))
    inp("l1_masks", (4, 128, 512))
    out_h = nc.dram_tensor("out", [T, D], F32, kind="ExternalOutput").ap()

    AREN = 52352
    _B = 16896
    xtf = nc.alloc_sbuf_tensor_at("xtf", [128, 16384], F32, offset=_B)
    xtb = nc.alloc_sbuf_tensor_at("xtb", [128, 32768], BF16, offset=_B)
    xnb = nc.alloc_sbuf_tensor_at("xnb", [128, 16384], BF16, offset=_B + 65536)
    cstf = nc.alloc_sbuf_tensor_at("cstf", [128, 2048], F32, offset=_B + 98304)
    cstb = nc.alloc_sbuf_tensor_at("cstb", [128, 4096], BF16, offset=_B + 98304)
    scrf = nc.alloc_sbuf_tensor_at("scrf", [128, 25728], F32, offset=_B + 106496)
    scrb = nc.alloc_sbuf_tensor_at("scrb", [128, 51456], BF16, offset=_B + 106496)

    class _Rt:
        def __init__(self, regions):
            self.regions = regions

        def __getitem__(self, key):
            pk, ck = key
            c0 = ck.start if ck.start is not None else 0
            c1 = ck.stop
            for (s, e, t) in self.regions:
                if c0 >= s and c1 <= e:
                    return t.ap()[pk, c0 - s:c1 - s]
            raise AssertionError("unrouted sbuf slice [%d:%d)" % (c0, c1))

    a32 = _Rt([(0, 16384, xtf), (24576, 26624, cstf), (26624, 52352, scrf)])
    a16 = _Rt([(0, 32768, xtb), (32768, 49152, xnb), (49152, 53248, cstb),
               (53248, 104704, scrb)])

    def f32v(off, n, p0=0, pn=128):
        return a32[p0:p0 + pn, off:off + n]

    def b16v(off, nb, p0=0, pn=128):
        return a16[p0:p0 + pn, 2 * off:2 * off + nb]

    XT_O = 0
    XNT_O = 16384
    CONST_O = 24576
    ACC_O = 26624
    FT_O = ACC_O + 16384
    assert FT_O + 9344 == AREN

    def XTt(i):
        return f32v(XT_O + 1024 * i, 1024)

    def acc_t(i):
        return f32v(ACC_O + 1024 * i, 1024)

    def xnt(c, t0=0, tn=T):
        o = 2 * XNT_O + 2048 * c + t0
        return a16[:, o:o + tn]

    xnt_all = a16[:, 2 * XNT_O:2 * XNT_O + 16384].rearrange("p (c t) -> p c t", t=2048)
    xt3 = a32[:, XT_O:XT_O + 16384].rearrange("p (i c) -> p i c", c=1024)

    cpool = [CONST_O]

    def calloc(n):
        o = cpool[0]
        cpool[0] += n
        assert cpool[0] <= ACC_O, "const pool overflow"
        return o

    IDENT_O = calloc(128); identv = f32v(IDENT_O, 128)
    E2_O = calloc(2); e2v = f32v(E2_O, 2)
    ONES_O = calloc(128); onesFv = f32v(ONES_O, 128, 0, 1)
    ONESC_O = calloc(1); onescv = f32v(ONESC_O, 1)
    BR_O = calloc(8); brv = f32v(BR_O, 8, 0, 1)
    BQK_O = calloc(16)
    B1C_O = calloc(256)
    CWA_O = calloc(128)
    SML_O = calloc(64)
    WR_O = calloc(64); wr32 = f32v(WR_O, 64)
    CNB_O = calloc(16)
    GCB_O = calloc(8)
    COL_O = calloc(64)
    SEL_O = calloc(128)
    assert cpool[0] <= CONST_O + 2048

    bias_cols = {}

    def cbias(val):
        val = float(val)
        if val not in bias_cols:
            o = calloc(1)
            tk = em("v", lambda e, _a=f32v(o, 1), _v=val: e.memset(_a, _v), inc=True, drain=True)
            bias_cols[val] = (f32v(o, 1), tk)
        return bias_cols[val]

    psA = stack.enter_context(nc.psum_tensor("psA", [128, 1024], F32))
    psB = stack.enter_context(nc.psum_tensor("psB", [128, 1024], F32))
    psC = stack.enter_context(nc.psum_tensor("psC", [128, 1024], F32))
    psD = stack.enter_context(nc.psum_tensor("psD", [128, 1024], F32))
    PAIRS = [psA, psB, psC, psD]

    def half(pi, h, p0=0, pn=128):
        return PAIRS[pi].ap()[p0:p0 + pn, 512 * h:512 * h + 512]

    tok_x = em("y", lambda e: e.dma_start(out=xt3, in_=di["x"].rearrange("(i p) c -> p i c", p=128)),
               inc=True, sem=p.dsem("x"))
    tok_ident = em("y", lambda e: e.dma_start(out=identv, in_=di["ident"]), inc=True, sem=p.dsem("cst"))
    tok_e2 = em("y", lambda e: e.dma_start(out=e2v, in_=di["e2"]), inc=True, sem="cst")
    tok_ones = em("v", lambda e: e.memset(onesFv, 1.0), inc=True)
    tok_onesc = em("v", lambda e: e.memset(onescv, 1.0), inc=True, drain=True)
    tok_br = em("y", lambda e: e.dma_start(out=brv, in_=di["moe_br"]), inc=True, sem="cst")
    tok_bqk = em("y", lambda e: e.dma_start(
        out=a32[:, BQK_O:BQK_O + 16].rearrange("p (a b) -> p a b", a=2),
        in_=di["l0_bqk"].rearrange("a p b -> p a b")), inc=True, sem="cst")
    tok_b1c = em("y", lambda e: e.dma_start(out=f32v(B1C_O, 256), in_=di["moe_b1"]), inc=True, sem="cst")
    tok_wr = em("y", lambda e: e.dma_start(
        out=a32[:, WR_O:WR_O + 64].rearrange("p (k e) -> p k e", k=8),
        in_=di["moe_wrT"].rearrange("k p e -> p k e")), inc=True, sem="cst")
    tok_cnb = em("y", lambda e: e.dma_start(out=f32v(CNB_O, 8), in_=di["cn_b1"]), inc=True, sem="cst")
    em("y", lambda e: e.dma_start(out=f32v(CNB_O + 8, 4), in_=di["cn_b2"]), inc=True, sem="cst")
    em("y", lambda e: e.dma_start(out=f32v(CNB_O + 12, 2), in_=di["cn_b3"]), inc=True, sem="cst")
    em("y", lambda e: e.dma_start(out=f32v(GCB_O, 4), in_=di["gc_b1"]), inc=True, sem="cst")
    em("y", lambda e: e.dma_start(out=f32v(GCB_O + 4, 2), in_=di["gc_b2"]), inc=True, sem="cst")
    em("y", lambda e: e.dma_start(out=f32v(COL_O + 5, 1, 0, 1), in_=di["gc_b3"]), inc=True, sem="cst")
    em("y", lambda e: e.dma_start(out=f32v(COL_O + 6, 1, 0, 1), in_=di["cn_b4"]), inc=True, sem="cst")
    tok_ar = em("y", lambda e: e.dma_start(out=f32v(COL_O + 7, 1, 0, 1), in_=di["l1_ar"]), inc=True, sem="cst")

    STG_O = FT_O
    LGS_O = FT_O + 2048

    def ln_tile(i, acc_off, eps, src_tile, src_tok=None):
        st = f32v(SML_O, 12)
        mv = f32v(SML_O + 12, 2)
        em("v", lambda e, _s=src_tile: e.bn_stats(f32v(SML_O, 6), _s[:, 0:512]), drain=True,
           waits=[src_tok] if src_tok else [])
        em("v", lambda e, _s=src_tile: e.bn_stats(f32v(SML_O + 6, 6), _s[:, 512:1024]), drain=True)
        t3 = em("v", lambda e: e.bn_aggr(mv, st), inc=True, drain=True)
        sd = f32v(SML_O + 14, 1)
        bc, btk = cbias(eps)
        t4 = em("s", lambda e, _b=bc: e.activation(out=sd, in_=mv[:, 1:2], func=AF.Sqrt, bias=_b,
                                                   scale=1.0), inc=True, waits=[t3, btk])
        isd = f32v(SML_O + 15, 1)
        em("v", lambda e: e.reciprocal(isd, sd), waits=[t4], drain=True)
        nmi = f32v(SML_O + 16, 1)
        t6 = em("v", lambda e: e.tensor_scalar(out=nmi, in0=mv[:, 0:1], scalar1=isd, scalar2=-1.0,
                                               op0=OP.mult, op1=OP.mult), inc=True, drain=True)
        dst = f32v(acc_off + 1024 * i, 1024)
        t7 = em("s", lambda e, _s=src_tile, _d=dst: e.activation(
            out=_d, in_=_s, func=AF.Identity, bias=nmi, scale=isd), inc=True, waits=[t6])
        return t7

    ps_free = [p.tok("v"), p.tok("v")]
    stg_free = [p.tok("t"), p.tok("t")]
    router_ps_free = [p.tok("v")]

    def transpose_router_tile(i, acc_off, apply_tok, do_router):
        vtoks = []
        for g in range(2):
            pr = g
            hv = half(pr, 0)
            last = None
            for q in range(4):
                src = f32v(acc_off + 1024 * i + 128 * (4 * g + q), 128)
                last = em("t", lambda e, _s=src, _o=hv[:, 128 * q:128 * q + 128],
                          _st=(q == 0), _sp=(q == 3):
                          e.matmul(_o, _s, identv, is_transpose=True, start=_st, stop=_sp),
                          inc=(q == 3), waits=[apply_tok, ps_free[pr], tok_ident])
            dst = xnt_all[:, 4 * g:4 * g + 4, 128 * i:128 * i + 128]
            tv = em("v", lambda e, _d=dst, _s=hv.rearrange("p (c t) -> p c t", t=128):
                    e.tensor_copy(_d, _s), inc=True, waits=[last], drain=True)
            vtoks.append(tv)
            if do_router:
                sb = i % 2
                sdst = f32v(STG_O + 1024 * sb + 512 * g, 512)
                tv2 = em("v", lambda e, _d=sdst, _s=hv: e.tensor_copy(_d, _s),
                         inc=True, waits=[last, stg_free[sb]], drain=True)
                vtoks.append(tv2)
            ps_free[pr] = p.tok("v")
        if not do_router:
            return
        sb = i % 2
        stok = p.tok("v")
        lg_ps = PAIRS[2].ap()[:, 0:8]
        for kc in range(8):
            lhs = f32v(STG_O + 1024 * sb + 128 * kc, 128)
            em("t", lambda e, _l=lhs, _w=wr32[:, 8 * kc:8 * kc + 8], _st=(kc == 0):
               e.matmul(lg_ps, _l, _w, start=_st, stop=False),
               waits=[stok, tok_wr, router_ps_free[0]])
        tmm = em("t", lambda e: e.matmul(lg_ps, onesFv, brv, start=False, stop=True),
                 inc=True, waits=[tok_ones, tok_br])
        stg_free[sb] = p.tok("t")
        lgs = f32v(LGS_O + 8 * i, 8)
        c0 = em("v", lambda e: e.tensor_copy(lgs, lg_ps), inc=True, waits=[tmm], drain=True)
        router_ps_free[0] = c0
        m1 = f32v(SML_O + 17, 1); m2 = f32v(SML_O + 18, 1)
        msk = f32v(SML_O + 19, 8); el = f32v(SML_O + 27, 8)
        dnm = f32v(SML_O + 35, 1); nm1 = f32v(SML_O + 36, 1)
        sm2 = f32v(SML_O + 37, 1); ex2 = f32v(SML_O + 38, 1)
        dn1 = f32v(SML_O + 39, 1); ge = f32v(SML_O + 40, 8)
        em("v", lambda e: e.reduce_max(m1, lgs, axis=AX.X), drain=True)
        em("v", lambda e: e.tensor_scalar(out=msk, in0=lgs, scalar1=m1, scalar2=None, op0=OP.is_ge),
           drain=True)
        em("v", lambda e: e.scalar_tensor_tensor(out=msk, in0=msk, scalar=-1e30, in1=lgs,
                                                 op0=OP.mult, op1=OP.add), drain=True)
        em("v", lambda e: e.reduce_max(m2, msk, axis=AX.X), drain=True)
        em("v", lambda e: e.tensor_scalar(out=nm1, in0=m1, scalar1=-1.0, scalar2=None, op0=OP.mult),
           drain=True)
        t8 = em("v", lambda e: e.tensor_tensor(out=sm2, in0=m2, in1=nm1, op=OP.add), inc=True,
                drain=True)
        t9 = em("s", lambda e: e.activation(out=ex2, in_=sm2, func=AF.Exp, bias=0.0, scale=1.0),
                inc=True, waits=[t8])
        em("v", lambda e: e.tensor_scalar(out=dn1, in0=ex2, scalar1=1.0, scalar2=None, op0=OP.add),
           waits=[t9], drain=True)
        t10 = em("v", lambda e: e.reciprocal(dnm, dn1), inc=True, drain=True)
        t11 = em("s", lambda e: e.activation(out=el, in_=lgs, func=AF.Exp, bias=nm1, scale=1.0),
                 inc=True, waits=[t10])
        em("v", lambda e: e.tensor_scalar(out=ge, in0=lgs, scalar1=m2, scalar2=None, op0=OP.is_ge),
           waits=[t11], drain=True)
        em("v", lambda e: e.tensor_tensor(out=ge, in0=ge, in1=el, op=OP.mult), drain=True)
        cwd = f32v(CWA_O + 8 * i, 8)
        em("v", lambda e, _d=cwd: e.tensor_scalar(out=_d, in0=ge, scalar1=dnm, scalar2=None,
                                                  op0=OP.mult), inc=True, drain=True)

    if PHASES >= 1:
        for i in range(NT):
            at = ln_tile(i, ACC_O, EPS_LN, XTt(i), tok_x)
            transpose_router_tile(i, ACC_O, at, True)
            if DEBUG_SERIAL:
                p.barrier()
        p.barrier()

    p.views["acc"] = (a32[:, ACC_O:ACC_O + 16384], (128, 16384))
    p.views["lgs"] = (f32v(LGS_O, 128), (128, 128))
    p.views["cw"] = (f32v(CWA_O, 128), (128, 128))
    p.views["xnt"] = (a16[:, 2 * XNT_O:2 * XNT_O + 16384], (128, 16384))

    H1_O = XT_O
    W2_O = XT_O + 8192
    W1_O = FT_O + 2100
    B2_O = FT_O + 3200

    def h1c(fc, t0=0, tn=T):
        o = 2 * H1_O + 2048 * fc + t0
        return a16[:, o:o + tn]

    def w2sl(qb, fc):
        o = 2 * (W2_O + 4096 * qb) + 1024 * fc
        return a16[:, o:o + 1024]

    def w1b(b):
        o = 2 * (W1_O + 512 * b)
        return a16[:, o:o + 1024].rearrange("p (k j) -> p k j", k=8)

    if PHASES >= 3:
        w1_use = [p.tok("t"), p.tok("t")]
        w2_use = [p.tok("t"), p.tok("t")]
        b2_use = p.tok("t")
        for e_ in range(NE):
            b2tok = em("y", lambda e, _e=e_: e.dma_start(out=f32v(B2_O, 1024, 0, 1),
                                                         in_=di["moe_b2"][_e:_e + 1, :]),
                       inc=True, waits=[b2_use], sem=p.dsem("b2"))
            for fq in range(4):
                qb = fq % 2
                w2toks = []
                for fc in range(8):
                    tk = em("g", lambda e, _qb=qb, _fc=fc, _e=e_, _f=8 * fq + fc:
                            e.dma_start(out=w2sl(_qb, _fc), in_=di["moe_w2"][_e, _f]),
                            inc=True, waits=[w2_use[qb]], sem=p.dsem("w2%d" % qb))
                    w2toks.append(tk)
                w2_ready = w2toks[-1]
                up_ps_free = [p.tok("s")] * 4
                gelu_last = None
                for fc in range(8):
                    fcg = 8 * fq + fc
                    b = fcg % 2
                    w1t = em("g", lambda e, _b=b, _e=e_, _f=fcg:
                             e.dma_start(out=w1b(_b), in_=di["moe_w1"][_e, _f]),
                             inc=True, waits=[w1_use[b]], sem=p.dsem("w1%d" % b))
                    for tt in range(4):
                        slot = (fc * 4 + tt) % 4
                        hv = half(slot // 2, slot % 2)
                        for kc in range(8):
                            em("t", lambda e, _h=hv, _b=b, _k=kc, _tt=tt,
                               _st=(kc == 0), _sp=(kc == 7):
                               e.matmul(_h, w1b(_b)[:, _k, :], xnt(_k, 512 * _tt, 512),
                                        start=_st, stop=_sp),
                               inc=(kc == 7), waits=[w1t, up_ps_free[slot]])
                        mmtok = p.tok("t")
                        w1_use[b] = mmtok
                        bia = f32v(B1C_O + 32 * e_ + fcg, 1)
                        gt = em("s", lambda e, _d=h1c(fc, 512 * tt, 512), _sv=hv, _b=bia:
                                e.activation(out=_d, in_=_sv, func=AF.Gelu, bias=_b, scale=1.0),
                                inc=True, waits=[mmtok, tok_b1c])
                        up_ps_free[slot] = gt
                        gelu_last = gt
                down_ps_free = [p.tok("v"), p.tok("v")]
                for ti in range(NT):
                    pr = 2 + (ti % 2)
                    first_start = True
                    if fq == 0:
                        for hh in range(2):
                            em("t", lambda e, _h=half(pr, hh), _o=512 * hh:
                               e.matmul(_h, onesFv, f32v(B2_O + _o, 512, 0, 1),
                                        start=True, stop=False),
                               waits=[b2tok, tok_ones, down_ps_free[ti % 2], gelu_last])
                        first_start = False
                    for fc in range(8):
                        for hh in range(2):
                            em("t", lambda e, _h=half(pr, hh), _ti=ti, _fc=fc, _qb=qb, _hh=hh,
                               _st=(first_start and fc == 0), _sp=(fc == 7):
                               e.matmul(_h, h1c(_fc, 128 * _ti, 128),
                                        w2sl(_qb, _fc)[:, 512 * _hh:512 * _hh + 512],
                                        start=_st, stop=_sp),
                               inc=(fc == 7 and hh == 1),
                               waits=[w2_ready, down_ps_free[ti % 2], gelu_last])
                    mmt = p.tok("t")
                    w2_use[qb] = mmt
                    if fq == 0:
                        b2_use = mmt
                    cwc = f32v(CWA_O + 8 * ti + e_, 1)
                    vt = em("v", lambda e, _p=PAIRS[pr].ap(), _a=acc_t(ti), _c=cwc:
                            e.scalar_tensor_tensor(out=_a, in0=_p, scalar=_c, in1=_a,
                                                   op0=OP.mult, op1=OP.add),
                            inc=True, waits=[mmt], drain=True)
                    down_ps_free[ti % 2] = vt
        p.barrier()
    p.views["moeacc"] = (a32[:, ACC_O:ACC_O + 16384], (128, 16384))

    if PHASES >= 3.5:
        tok_x2 = em("y", lambda e: e.dma_start(out=xt3,
                                               in_=di["x"].rearrange("(i p) c -> p i c", p=128)),
                    inc=True, sem="x")
        LNR_O = FT_O + 4224
        WB_O = FT_O + 6272
        BB_O = FT_O + 7296
        lnwrow = f32v(LNR_O, 1024, 0, 1)
        lnbrow = f32v(LNR_O + 1024, 1024, 0, 1)
        tw = em("y", lambda e: e.dma_start(out=lnwrow, in_=di["moe_lnw"]), inc=True, sem="cst")
        tb = em("y", lambda e: e.dma_start(out=lnbrow, in_=di["moe_lnb"]), inc=True, sem="cst")
        prev_ev = p.tok("v")
        for g in range(2):
            mt = None
            for hh in range(2):
                mt = em("t", lambda e, _h=half(0, hh), _o=1024 * g + 512 * hh:
                        e.matmul(_h, onesFv, f32v(LNR_O + _o, 512, 0, 1), start=True, stop=True),
                        inc=(hh == 1), waits=[tw, tb, tok_ones, prev_ev])
            dsto = WB_O if g == 0 else BB_O
            prev_ev = em("v", lambda e, _d=f32v(dsto, 1024), _s=PAIRS[0].ap():
                         e.tensor_copy(_d, _s), inc=True, waits=[mt], drain=True)
        tok_wb = prev_ev
        TMP_O = FT_O + 8320
        for i in range(NT):
            at = ln_tile(i, ACC_O, EPS_MOE_LN, acc_t(i))
            tmp = f32v(TMP_O, 1024)
            em("v", lambda e, _i=i, _t=tmp: e.tensor_tensor(out=_t, in0=acc_t(_i),
                                                            in1=f32v(WB_O, 1024), op=OP.mult),
               waits=[at, tok_wb], drain=True)
            em("v", lambda e, _t=tmp: e.tensor_tensor(out=_t, in0=_t, in1=f32v(BB_O, 1024),
                                                      op=OP.add), drain=True)
            em("v", lambda e, _i=i, _t=tmp: e.tensor_tensor(out=XTt(_i), in0=XTt(_i), in1=_t,
                                                            op=OP.add),
               inc=True, waits=[tok_x2], drain=True)
        p.barrier()
    p.views["xt_ffn0"] = (a32[:, XT_O:XT_O + 16384], (128, 16384))

    SCR4 = ACC_O
    V1_O = SCR4
    ATN_O = SCR4 + 2080
    QKP_O = SCR4 + 10272
    SQ_O = SCR4 + 12320
    PHQ_O = SCR4 + 12832
    PHK_O = SCR4 + 14880
    KVA_O = SCR4 + 16928
    NQ_O = SCR4 + 17056
    NK_O = SCR4 + 19104
    RC_O = NQ_O
    SEL_O4 = SCR4 + 21152
    WQ_O = SCR4 + 22176
    WK_O = SCR4 + 22688
    WV_O = SCR4 + 23200
    ORF_O = SCR4 + 23712
    BV_O = SCR4 + 23840
    assert BV_O + 1024 <= AREN

    def v1t(buf, ti, h2, n=65):
        o = 2 * (V1_O + 1040 * buf) + 130 * ti + 65 * h2
        return a16[:, o:o + n]

    def atn(c, t0=0, tn=T, p0=0, pn=128):
        o = 2 * ATN_O + 2048 * c + t0
        return a16[p0:p0 + pn, o:o + tn]

    def qkp(which, p0, pn, t0, tn):
        o = 2 * (QKP_O + 1024 * which) + t0
        return a16[p0:p0 + pn, o:o + tn]

    def phq(jc, t0, tn):
        o = 2 * PHQ_O + 2048 * jc + t0
        return a16[:, o:o + tn]

    def phk(ti, j0=0, jn=256):
        o = 2 * PHK_O + 256 * ti + j0
        return a16[:, o:o + jn]

    def kva(jc, c0=0, cn=65):
        o = 2 * (KVA_O + 64 * jc) + c0
        return a16[:, o:o + cn]

    if PHASES >= 4:
        t_orf = em("g", lambda e: e.dma_start(out=b16v(ORF_O, 256, 0, 64), in_=di["l0_orfT"]),
                   inc=True, sem=p.dsem("p4c"))
        t_orf2 = em("g", lambda e: e.dma_start(out=b16v(ORF_O, 256, 64, 64), in_=di["l0_orfT"]),
                    inc=True, sem="p4c")
        t_bv = em("y", lambda e: e.dma_start(out=f32v(BV_O, 1024, 0, 1), in_=di["l0_bv"]),
                  inc=True, sem=p.dsem("p4y"))
        em("y", lambda e: e.dma_start(
            out=a32[0:2, SEL_O4:SEL_O4 + 256].rearrange("p (a c) -> p a c", a=2),
            in_=di["cqsel"].rearrange("a p c -> p a c")), inc=True, sem=p.dsem("p4y"))
        em("y", lambda e: e.dma_start(
            out=a32[0:2, SEL_O4 + 256:SEL_O4 + 768].rearrange("p (a c) -> p a c", a=2),
            in_=di["ksel"].rearrange("a p c -> p a c")), inc=True, sem="p4y")
        selw = []
        selw.append(em("v", lambda e: e.memset(f32v(SEL_O4 + 768, 64, 64, 1), 1.0), drain=True))
        selw.append(em("v", lambda e: e.memset(f32v(SEL_O4 + 768 + 64, 64, 64, 1), 0.0), drain=True))
        selw.append(em("v", lambda e: e.memset(f32v(SEL_O4 + 896, 64, 64, 1), 0.0), drain=True))
        selw.append(em("v", lambda e: e.memset(f32v(SEL_O4 + 896 + 64, 64, 64, 1), 1.0), drain=True))
        for buf in range(2):
            selw.append(em("v", lambda e, _b=buf: e.memset(
                a16[:, 2 * (V1_O + 1040 * _b):2 * (V1_O + 1040 * _b) + 2080]
                .rearrange("p (t c) -> p t c", c=130)[:, :, 64:65], 1.0), drain=True))
            selw.append(em("v", lambda e, _b=buf: e.memset(
                a16[:, 2 * (V1_O + 1040 * _b):2 * (V1_O + 1040 * _b) + 2080]
                .rearrange("p (t c) -> p t c", c=130)[:, :, 129:130], 1.0), drain=True))
        tok_sel = em("v", lambda e: e.nop(), inc=True, waits=[("p4y", 32)])
        CQsel = [f32v(SEL_O4, 128, 0, 2), f32v(SEL_O4 + 128, 128, 0, 2)]
        Ksel = [f32v(SEL_O4 + 256, 256, 0, 2), f32v(SEL_O4 + 512, 256, 0, 2)]
        tok_selc = p.tok("p4y")
        E64 = [f32v(SEL_O4 + 768, 128, 64, 1), f32v(SEL_O4 + 896, 128, 64, 1)]
        bc_phi, tk_phi = cbias(PHI_BIAS)

        wqk_use = p.tok("t")
        wv_use = p.tok("t")
        v1_use = [p.tok("t"), p.tok("t")]
        qk_ready = None
        prev_head_done = p.tok("v")

        for pr_ in range(8):
            vb = pr_ % 2
            wqt = em("g", lambda e, _p=pr_: e.dma_start(
                out=b16v(WQ_O, 1024).rearrange("p (k j) -> p k j", k=8),
                in_=di["l0_wqT"][_p].rearrange("k p j -> p k j")),
                inc=True, waits=[wqk_use], sem=p.dsem("wqk"))
            wkt = em("g", lambda e, _p=pr_: e.dma_start(
                out=b16v(WK_O, 1024).rearrange("p (k j) -> p k j", k=8),
                in_=di["l0_wkT"][_p].rearrange("k p j -> p k j")),
                inc=True, sem="wqk")
            wvt = em("g", lambda e, _p=pr_: e.dma_start(
                out=b16v(WV_O, 1024).rearrange("p (k j) -> p k j", k=8),
                in_=di["l0_wvT"][:, :, 128 * _p:128 * _p + 128].rearrange("k p j -> p k j")),
                inc=True, waits=[wv_use], sem=p.dsem("wv"))
            v1ev = []
            for ti in range(NT):
                hv = half(0, ti % 2)
                hvv = PAIRS[0].ap()[:, 512 * (ti % 2):512 * (ti % 2) + 128]
                for kc in range(8):
                    em("t", lambda e, _h=hvv, _k=kc, _ti=ti, _st=(kc == 0):
                       e.matmul(_h, xnt(_k, 128 * _ti, 128),
                                b16v(WV_O, 1024).rearrange("p (k j) -> p k j", k=8)[:, _k, :],
                                start=_st, stop=False),
                       waits=[wvt, v1_use[vb], prev_head_done,
                              (v1ev[ti - 2] if ti >= 2 else None)])
                em("t", lambda e, _h=hvv, _p=pr_:
                   e.matmul(_h, onesFv, f32v(BV_O + 128 * _p, 128, 0, 1), start=False, stop=True),
                   inc=True, waits=[t_bv, tok_ones])
                mmv = p.tok("t")
                ev = em("v", lambda e, _d=v1t(vb, ti, 0, 130).rearrange("p (a c) -> p a c", c=65)[:, :, 0:64],
                        _s=hvv.rearrange("p (a c) -> p a c", c=64), _m=mmv:
                        e.tensor_copy(_d, _s), inc=True, waits=[mmv, tok_sel], drain=True)
                v1ev.append(ev)
            v1_ready = v1ev[-1]
            qev = None
            for which, wt_, bcol in ((0, wqt, 0), (1, wkt, 8)):
                for kc in range(8):
                    wsl = b16v(WQ_O + 512 * which, 1024).rearrange("p (k j) -> p k j", k=8)
                    for tt in range(4):
                        hv = half(1 + tt // 2, tt % 2)
                        em("t", lambda e, _h=hv, _w=wsl, _k=kc, _tt=tt,
                           _st=(kc == 0), _sp=(kc == 7):
                           e.matmul(_h, _w[:, _k, :], xnt(_k, 512 * _tt, 512), start=_st, stop=_sp),
                           inc=(kc == 7 and tt == 3),
                           waits=[wt_, qk_ready, prev_head_done, qev])
                mmq = p.tok("t")
                if which == 1:
                    wqk_use = mmq
                for tt in range(4):
                    hv = half(1 + tt // 2, tt % 2)
                    bia = f32v(BQK_O + bcol + pr_, 1)
                    em("s", lambda e, _d=qkp(which, 0, 128, 512 * tt, 512), _sv=hv, _b=bia:
                       e.activation(out=_d, in_=_sv, func=AF.Identity, bias=_b, scale=1.0),
                       inc=(tt == 3), waits=[mmq, tok_bqk], drain=True)
                if which == 0:
                    qev = p.tok("s")
                else:
                    kev = p.tok("s")
            nrm_toks = []
            sq_free = None
            for which, ev_, dsto in ((0, None, NQ_O), (1, None, NK_O)):
                evtok = qev if which == 0 else kev
                for tt in range(4):
                    sq = f32v(SQ_O, 512)
                    t1 = em("s", lambda e, _s=qkp(which, 0, 128, 512 * tt, 512), _d=sq:
                            e.activation(out=_d, in_=_s, func=AF.Square, bias=0.0, scale=1.0),
                            inc=True, waits=[evtok, sq_free], drain=True)
                    nps = PAIRS[2].ap()[0:2, 0:512]
                    t2 = em("t", lambda e, _s=sq: e.matmul(nps, e2v, _s, start=True, stop=True),
                            inc=True, waits=[t1, tok_e2, prev_head_done,
                                             nrm_toks[-1] if nrm_toks else None])
                    sq_free = t2
                    t3 = em("v", lambda e, _d=f32v(dsto + 512 * tt, 512, 0, 2), _s=nps:
                            e.tensor_copy(_d, _s), inc=True, waits=[t2], drain=True)
                    nrm_toks.append(t3)
            norm_done = nrm_toks[-1]
            for h2 in range(2):
                h = 2 * pr_ + h2
                phiq_toks = []
                phiq_slot_free = [None] * 4
                for jc in range(2):
                    for tt in range(4):
                        slot = (jc * 4 + tt) % 4
                        hv = half(1 + slot // 2, slot % 2)
                        em("t", lambda e, _h=hv, _j=jc, _h2=h2, _tt=tt:
                           e.matmul(_h, b16v(ORF_O, 256, 64 * _h2, 64)[:, 128 * _j:128 * _j + 128],
                                    qkp(0, 64 * _h2, 64, 512 * _tt, 512), start=True, stop=False),
                           waits=[t_orf2, norm_done, prev_head_done, phiq_slot_free[slot]])
                        em("t", lambda e, _h=hv, _h2=h2, _tt=tt:
                           e.matmul(_h, CQsel[_h2], f32v(NQ_O + 512 * _tt, 512, 0, 2),
                                    start=False, stop=True), inc=True, waits=[tok_sel])
                        mmp = p.tok("t")
                        sqv = f32v(SQ_O, 512)
                        t1 = em("s", lambda e, _d=sqv, _s=hv, _b=bc_phi:
                                e.activation(out=_d, in_=_s, func=AF.Exp, bias=_b, scale=INV_S),
                                inc=True, waits=[mmp, tk_phi, sq_free], drain=True)
                        phiq_slot_free[slot] = t1
                        t2 = em("v", lambda e, _d=phq(jc, 512 * tt, 512), _s=sqv:
                                e.tensor_scalar(out=_d, in0=_s, scalar1=PHI_EPS, scalar2=None,
                                                op0=OP.add), inc=True, waits=[t1], drain=True)
                        sq_free = t2
                        phiq_toks.append(t2)
                phiq_done = phiq_toks[-1]
                phik_toks = []
                for ti in range(NT):
                    hv = PAIRS[3].ap()[:, 512 * (ti % 2):512 * (ti % 2) + 256]
                    em("t", lambda e, _h=hv, _h2=h2, _ti=ti:
                       e.matmul(_h, qkp(1, 64 * _h2, 64, 128 * _ti, 128),
                                b16v(ORF_O, 256, 64 * _h2, 64), start=True, stop=False),
                       waits=[norm_done, prev_head_done,
                              (phik_toks[ti - 2] if ti >= 2 else None)])
                    em("t", lambda e, _h=hv, _h2=h2, _ti=ti:
                       e.matmul(_h, f32v(NK_O + 128 * _ti, 128, 0, 2), Ksel[_h2],
                                start=False, stop=True), inc=True, waits=[tok_sel])
                    mmp = p.tok("t")
                    sqv = f32v(SQ_O, 256)
                    t1 = em("s", lambda e, _d=sqv, _s=hv, _b=bc_phi:
                            e.activation(out=_d, in_=_s, func=AF.Exp, bias=_b, scale=INV_S),
                            inc=True, waits=[mmp, sq_free], drain=True)
                    t2 = em("v", lambda e, _d=phk(ti), _s=sqv:
                            e.tensor_scalar(out=_d, in0=_s, scalar1=PHI_EPS, scalar2=None,
                                            op0=OP.add), inc=True, waits=[t1], drain=True)
                    sq_free = t2
                    phik_toks.append(t2)
                phik_done = phik_toks[-1]
                kv_toks = []
                for jc in range(2):
                    hv = PAIRS[0].ap()[:, 512 * jc:512 * jc + 65]
                    for ti in range(NT):
                        em("t", lambda e, _h=hv, _j=jc, _ti=ti, _h2=h2, _vb=vb,
                           _st=(ti == 0), _sp=(ti == 15):
                           e.matmul(_h, phk(_ti, 128 * _j, 128), v1t(_vb, _ti, _h2),
                                    start=_st, stop=_sp),
                           inc=(ti == 15), waits=[phik_done, v1_ready])
                    mk = p.tok("t")
                    v1_use[vb] = mk
                    ev = em("v", lambda e, _d=kva(jc), _s=hv: e.tensor_copy(_d, _s),
                            inc=True, waits=[mk], drain=True)
                    kv_toks.append(ev)
                kv_done = kv_toks[-1]
                nd_toks = []

                def dslot(tt):
                    dpair, dh = (0, tt) if tt < 2 else (3, tt - 2)
                    return PAIRS[dpair].ap()[64:65, 512 * dh:512 * dh + 512]
                for jc in range(2):
                    for tt in range(4):
                        pi, hh = (1 + tt // 2, tt % 2)
                        if h2 == 0:
                            outp = PAIRS[pi].ap()[0:65, 512 * hh:512 * hh + 512]
                            em("t", lambda e, _o=outp, _j=jc, _tt=tt, _st=(jc == 0), _sp=(jc == 1):
                               e.matmul(_o, kva(_j), phq(_j, 512 * _tt, 512), start=_st, stop=_sp),
                               inc=(jc == 1), waits=[kv_done, phiq_done])
                        else:
                            outp = PAIRS[pi].ap()[64:128, 512 * hh:512 * hh + 512]
                            em("t", lambda e, _o=outp, _j=jc, _tt=tt, _st=(jc == 0), _sp=(jc == 1):
                               e.matmul(_o, kva(_j, 0, 64), phq(_j, 512 * _tt, 512),
                                        start=_st, stop=_sp),
                               waits=[kv_done, phiq_done])
                            em("t", lambda e, _o=dslot(tt), _j=jc, _tt=tt,
                               _st=(jc == 0), _sp=(jc == 1):
                               e.matmul(_o, kva(_j, 64, 1), phq(_j, 512 * _tt, 512),
                                        start=_st, stop=_sp), inc=(jc == 1))
                        if jc == 1:
                            nd_toks.append(p.tok("t"))
                head_v_toks = []
                for tt in range(4):
                    pi, hh = (1 + tt // 2, tt % 2)
                    ndt = nd_toks[tt]
                    if h2 == 0:
                        nsrc = PAIRS[pi].ap()[0:64, 512 * hh:512 * hh + 512]
                        dsrc = PAIRS[pi].ap()[64:65, 512 * hh:512 * hh + 512]
                    else:
                        nsrc = PAIRS[pi].ap()[64:128, 512 * hh:512 * hh + 512]
                        dsrc = dslot(tt)
                    ev = em("v", lambda e, _d=atn(pr_, 512 * tt, 512, 64 * h2, 64), _s=nsrc:
                            e.tensor_copy(_d, _s), inc=True, waits=[ndt], drain=True)
                    rc = f32v(RC_O + 512 * tt, 512, 64, 1)
                    t1 = em("s", lambda e, _d=rc, _s=dsrc: e.activation(
                        out=_d, in_=_s, func=AF.Ln, bias=0.0, scale=1.0), waits=[ndt], drain=True)
                    t2 = em("s", lambda e, _d=rc: e.activation(
                        out=_d, in_=_d, func=AF.Exp, bias=0.0, scale=-1.0), inc=True, drain=True)
                    rtok = p.tok("s")
                    bcp = PAIRS[pi].ap()[:, 512 * hh:512 * hh + 512]
                    t3 = em("t", lambda e, _o=bcp, _h2=h2, _r=rc: e.matmul(
                        _o, E64[_h2], _r, start=True, stop=True),
                        inc=True, waits=[rtok, tok_sel, ev])
                    bct = p.tok("t")
                    t4 = em("v", lambda e, _a=atn(pr_, 512 * tt, 512, 64 * h2, 64),
                            _b=bcp[64 * h2:64 * h2 + 64, :]:
                            e.tensor_tensor(out=_a, in0=_a, in1=_b, op=OP.mult),
                            inc=True, waits=[bct], drain=True)
                    head_v_toks.append(t4)
                prev_head_done = head_v_toks[-1]
            qk_ready = prev_head_done
        bo_row = f32v(QKP_O, 1024, 0, 1)
        t_bo = em("y", lambda e: e.dma_start(out=bo_row, in_=di["l0_bo"]), inc=True, sem="p4y",
                  waits=[prev_head_done])
        wo_use = [p.tok("t"), p.tok("t")]
        fold_evs = []
        for grp in range(4):
            for kc in range(8):
                wb = kc % 2
                wo_dst = b16v(WV_O if wb == 0 else WQ_O, 1024)
                wot = em("g", lambda e, _d=wo_dst, _k=kc: e.dma_start(out=_d, in_=di["l0_woT"][_k]),
                         inc=True, waits=[wo_use[wb], prev_head_done], sem=p.dsem("wo%d" % wb))
                for t_ in range(4):
                    ti = 4 * grp + t_
                    pv = PAIRS[t_].ap()
                    if kc == 0:
                        for hh in range(2):
                            em("t", lambda e, _h=half(t_, hh), _o=512 * hh:
                               e.matmul(_h, onesFv, f32v(QKP_O + _o, 512, 0, 1),
                                        start=True, stop=False),
                               waits=[t_bo, tok_ones, prev_head_done,
                                      (fold_evs[4 * (grp - 1) + t_] if grp > 0 else None)])
                    for hh in range(2):
                        em("t", lambda e, _h=half(t_, hh), _k=kc, _ti=ti, _hh=hh, _w=wo_dst,
                           _sp=(kc == 7):
                           e.matmul(_h, atn(_k, 128 * _ti, 128),
                                    _w[:, 512 * _hh:512 * _hh + 512], start=False, stop=_sp),
                           inc=((kc == 7 and hh == 1) or (t_ == 3 and hh == 1)), waits=[wot])
                    if kc == 7:
                        mmt = p.tok("t")
                        ev = em("v", lambda e, _x=XTt(ti), _s=pv:
                                e.tensor_tensor(out=_x, in0=_x, in1=_s, op=OP.add),
                                inc=True, waits=[mmt], drain=True)
                        fold_evs.append(ev)
                wo_use[wb] = p.tok("t")
        p.barrier()
    p.views["atn0"] = (a16[:, 2 * ATN_O:2 * ATN_O + 16384], (128, 16384))

    if PHASES >= 5:
        for i in range(NT):
            at = ln_tile(i, ACC_O, EPS_LN, XTt(i))
            transpose_router_tile(i, ACC_O, at, False)
        p.barrier()
    p.views["xnt1"] = (a16[:, 2 * XNT_O:2 * XNT_O + 16384], (128, 16384))

    if PHASES >= 6:
        GC1_O = FT_O
        GC2_O = FT_O + 2048
        GC3_O = FT_O + 2560
        GCB_B = FT_O + 2568
        H1G_O = FT_O + 2572
        H2G_O = FT_O + 2574
        g1t = em("g", lambda e: e.dma_start(
            out=b16v(GC1_O, 4096).rearrange("p (k o) -> p k o", k=8), in_=di["gc_w1"]),
            inc=True, sem=p.dsem("gcw"))
        g2t = em("g", lambda e: e.dma_start(
            out=b16v(GC2_O, 1024).rearrange("p (k o) -> p k o", k=4), in_=di["gc_w2"]),
            inc=True, sem="gcw")
        g3t = em("g", lambda e: e.dma_start(
            out=b16v(GC3_O, 2).rearrange("p (k o) -> p k o", k=2), in_=di["gc_w3"]),
            inc=True, sem="gcw")
        gps = PAIRS[0].ap()[:, 0:1]
        gc_ev = None
        for oc in range(8):
            for ti in range(NT):
                em("t", lambda e, _o=oc, _ti=ti, _st=(ti == 0), _sp=(ti == 15):
                   e.matmul(gps, f32v(ACC_O + 1024 * _ti + 128 * _o, 128), onescv,
                            start=_st, stop=_sp),
                   inc=(ti == 15), waits=[tok_onesc, gc_ev])
            mmt = p.tok("t")
            gc_ev = em("s", lambda e, _d=a16[:, 2 * GCB_B + oc:2 * GCB_B + oc + 1], _s=gps:
                       e.activation(out=_d, in_=_s, func=AF.Identity, bias=0.0,
                                    scale=float(1.0 / 2048.0)),
                       inc=True, waits=[mmt], drain=True)
        h_ev = gc_ev
        for oc in range(4):
            for kc in range(8):
                em("t", lambda e, _k=kc, _o=oc, _st=(kc == 0), _sp=(kc == 7):
                   e.matmul(gps, b16v(GC1_O, 4096).rearrange("p (k o) -> p k o", k=8)
                            [:, _k, 128 * _o:128 * _o + 128],
                            a16[:, 2 * GCB_B + _k:2 * GCB_B + _k + 1],
                            start=_st, stop=_sp),
                   inc=(kc == 7), waits=[g1t, h_ev])
            mmt = p.tok("t")
            h_ev = em("s", lambda e, _d=a16[:, 2 * H1G_O + oc:2 * H1G_O + oc + 1],
                      _s=gps, _b=f32v(GCB_O + oc, 1):
                      e.activation(out=_d, in_=_s, func=AF.Gelu, bias=_b, scale=1.0),
                      inc=True, waits=[mmt, tok_cnb], drain=True)
        for oc in range(2):
            for kc in range(4):
                em("t", lambda e, _k=kc, _o=oc, _st=(kc == 0), _sp=(kc == 3):
                   e.matmul(gps, b16v(GC2_O, 1024).rearrange("p (k o) -> p k o", k=4)
                            [:, _k, 128 * _o:128 * _o + 128],
                            a16[:, 2 * H1G_O + _k:2 * H1G_O + _k + 1],
                            start=_st, stop=_sp),
                   inc=(kc == 3), waits=[g2t, h_ev])
            mmt = p.tok("t")
            h_ev = em("s", lambda e, _d=a16[:, 2 * H2G_O + oc:2 * H2G_O + oc + 1],
                      _s=gps, _b=f32v(GCB_O + 4 + oc, 1):
                      e.activation(out=_d, in_=_s, func=AF.Gelu, bias=_b, scale=1.0),
                      inc=True, waits=[mmt], drain=True)
        for kc in range(2):
            em("t", lambda e, _k=kc, _st=(kc == 0), _sp=(kc == 1):
               e.matmul(gps[0:1, 0:1],
                        b16v(GC3_O, 2).rearrange("p (k o) -> p k o", k=2)[:, _k, :],
                        a16[:, 2 * H2G_O + _k:2 * H2G_O + _k + 1], start=_st, stop=_sp),
               inc=(kc == 1), waits=[g3t, h_ev])
        mmt = p.tok("t")
        gf_ev = em("s", lambda e: e.activation(out=f32v(COL_O + 8, 1, 0, 1),
                                               in_=gps[0:1, 0:1], func=AF.Sigmoid,
                                               bias=f32v(COL_O + 5, 1, 0, 1), scale=1.0),
                   inc=True, waits=[mmt, tok_ar], drain=True)
        bc_ev = gf_ev
        for srcc, dstc in ((8, 3), (7, 4), (6, 2)):
            mt = em("t", lambda e, _s=f32v(COL_O + srcc, 1, 0, 1):
                    e.matmul(gps, onesFv, _s, start=True, stop=True),
                    inc=True, waits=[bc_ev, tok_ones])
            bc_ev = em("v", lambda e, _d=f32v(COL_O + dstc, 1), _s=gps:
                       e.tensor_copy(_d, _s), inc=True, waits=[mt], drain=True)
        em("v", lambda e: e.tensor_tensor(out=f32v(COL_O + 0, 1), in0=f32v(COL_O + 4, 1),
                                          in1=f32v(COL_O + 3, 1), op=OP.mult), drain=True)
        em("v", lambda e: e.tensor_scalar(out=f32v(COL_O + 0, 1), in0=f32v(COL_O + 0, 1),
                                          scalar1=1.4, scalar2=None, op0=OP.mult), drain=True)
        em("v", lambda e: e.scalar_tensor_tensor(out=f32v(COL_O + 0, 1), in0=f32v(COL_O + 4, 1),
                                                 scalar=-1.0, in1=f32v(COL_O + 0, 1),
                                                 op0=OP.mult, op1=OP.add), drain=True)
        em("v", lambda e: e.tensor_scalar(out=f32v(COL_O + 0, 1), in0=f32v(COL_O + 0, 1),
                                          scalar1=0.1, scalar2=None, op0=OP.add), drain=True)
        em("v", lambda e: e.tensor_scalar(out=f32v(COL_O + 1, 1), in0=f32v(COL_O + 4, 1),
                                          scalar1=0.6, scalar2=None, op0=OP.mult),
           inc=True, drain=True)
        p.barrier()

    SCR8 = ACC_O
    ATN1_O = SCR8
    TBL_O = SCR8
    CQT_O = SCR8 + 8192
    CKV_O = SCR8 + 10240
    QRA_O = SCR8 + 12288
    QRP_O = SCR8 + 16384
    QH_O = SCR8 + 16384
    KH_O = SCR8 + 18432
    RA_O = SCR8 + 20480
    assert RA_O + 5248 == AREN

    def atn1(c, t0=0, tn=T, p0=0, pn=128):
        o = 2 * ATN1_O + 2048 * c + t0
        return a16[p0:p0 + pn, o:o + tn]

    def cqt(c, t0=0, tn=T):
        o = 2 * CQT_O + 2048 * c + t0
        return a16[:, o:o + tn]

    def ckvt(c, t0=0, tn=T):
        o = 2 * CKV_O + 2048 * c + t0
        return a16[:, o:o + tn]

    def qrav(c, t0=0, tn=T, p0=0, pn=128):
        o = 2 * QRA_O + 2048 * c + t0
        return a16[p0:p0 + pn, o:o + tn]

    def qhv(b, p0=0, pn=128, t0=0, tn=T):
        o = 2 * (QH_O + 1024 * b) + t0
        return a16[p0:p0 + pn, o:o + tn]

    def khv(b, p0=0, pn=128, t0=0, tn=T):
        o = 2 * (KH_O + 1024 * b) + t0
        return a16[p0:p0 + pn, o:o + tn]

    def esb(b):
        o = 2 * (ES_O8 + 256 * b)
        return a16[:, o:o + 512]

    def v1l(ti, h2, c0=0, cn=65):
        o = 2 * V1L_O8 + 130 * ti + 65 * h2 + c0
        return a16[:, o:o + cn]

    ES_O8 = RA_O
    V1L_O8 = RA_O + 512
    MSK_O8 = RA_O + 1552
    RC_O8 = RA_O + 2576
    UQ_O8 = RA_O + 3088
    SEL_O8 = RA_O + 3472
    QKS_O8 = RA_O + 4960

    if PHASES >= 6.5:
        for ci, (nm, cc_) in enumerate((("cos_rep", 0), ("cos_rep", 1),
                                        ("sin_rep", 0), ("sin_rep", 1))):
            em("y", lambda e, _n=nm, _c=cc_, _i=ci: e.dma_start(
                out=f32v(TBL_O + 2048 * _i, 2048), in_=di[_n][_c]),
                inc=True, sem=p.dsem("tbl"))
        t_wdq = em("g", lambda e: e.dma_start(
            out=b16v(RA_O, 2048).rearrange("p (k o) -> p k o", k=8), in_=di["l1_wdq"]),
            inc=True, sem=p.dsem("l1w"))
        t_wdkv = em("g", lambda e: e.dma_start(
            out=b16v(RA_O + 1024, 2048).rearrange("p (k o) -> p k o", k=8), in_=di["l1_wdkv"]),
            inc=True, sem="l1w")
        t_wqr = em("g", lambda e: e.dma_start(
            out=b16v(RA_O + 2048, 1024).rearrange("p (k o) -> p k o", k=2), in_=di["l1_wqr"]),
            inc=True, sem="l1w")
        t_wkr = em("g", lambda e: e.dma_start(
            out=b16v(RA_O + 2560, 256).rearrange("p (k o) -> p k o", k=8), in_=di["l1_wkr"]),
            inc=True, sem="l1w")
        t_qrs = em("g", lambda e: e.dma_start(
            out=b16v(QKS_O8, 512).rearrange("p (m j) -> p m j", m=16),
            in_=di["qrsel"].rearrange("m v p j -> p (m v) j")), inc=True, sem="l1w")
        t_krs = em("g", lambda e: e.dma_start(
            out=b16v(QKS_O8 + 256, 64, 0, 16).rearrange("p (v j) -> p v j", v=2),
            in_=di["krsel"].rearrange("v p j -> p v j")), inc=True, sem="l1w")

        def wdq_(k, o0, on):
            return b16v(RA_O, 2048).rearrange("p (k o) -> p k o", k=8)[:, k, o0:o0 + on]

        def wdkv_(k, o0, on):
            return b16v(RA_O + 1024, 2048).rearrange("p (k o) -> p k o", k=8)[:, k, o0:o0 + on]

        def wqr_(k, o0, on):
            return b16v(RA_O + 2048, 1024).rearrange("p (k o) -> p k o", k=2)[:, k, o0:o0 + on]

        def qrsel_(m, v):
            return b16v(QKS_O8, 512).rearrange("p (m j) -> p m j", m=16)[:, 2 * m + v, :]

        def krsel_(v):
            return b16v(QKS_O8 + 256, 64, 0, 16).rearrange("p (v j) -> p v j", v=2)[:, v, :]

        prev_ev = None
        for dsti, (dst, wf, wt_) in enumerate(((cqt, wdq_, t_wdq), (ckvt, wdkv_, t_wdkv))):
            for oc in range(2):
                for tt in range(4):
                    hv = half(tt // 2, tt % 2)
                    for kc in range(8):
                        em("t", lambda e, _h=hv, _w=wf, _k=kc, _o=oc, _tt=tt,
                           _st=(kc == 0), _sp=(kc == 7):
                           e.matmul(_h, _w(_k, 128 * _o, 128), xnt(_k, 512 * _tt, 512),
                                    start=_st, stop=_sp),
                           inc=(kc == 7), waits=[wt_, prev_ev])
                    mmt = p.tok("t")
                    prev_ev = em("v", lambda e, _d=dst(oc, 512 * tt, 512), _s=hv:
                                 e.tensor_copy(_d, _s), inc=True, waits=[mmt], drain=True)
        cq_done = prev_ev
        for jc in range(4):
            for tt in range(4):
                hv = half(2 + tt // 2, tt % 2)
                for kc in range(2):
                    em("t", lambda e, _h=hv, _j=jc, _k=kc, _tt=tt,
                       _st=(kc == 0), _sp=(kc == 1):
                       e.matmul(_h, wqr_(_k, 128 * _j, 128), cqt(_k, 512 * _tt, 512),
                                start=_st, stop=_sp),
                       inc=(kc == 1), waits=[t_wqr, cq_done, prev_ev])
                mmt = p.tok("t")
                prev_ev = em("v", lambda e, _j=jc, _tt=tt, _s=hv: e.tensor_copy(
                    a16[:, 2 * QRP_O + 2048 * _j + 512 * _tt:
                        2 * QRP_O + 2048 * _j + 512 * _tt + 512], _s),
                    inc=True, waits=[mmt], drain=True)
        kx1 = lambda t0, tn: a16[0:16, 2 * RA_O + t0:2 * RA_O + t0 + tn]
        kx2 = lambda t0, tn: a16[0:16, 2 * (RA_O + 1024) + t0:2 * (RA_O + 1024) + t0 + tn]
        kb1 = f32v(RA_O + 2688, 512, 0, 16)
        kb2 = f32v(RA_O + 3200, 512, 0, 16)
        ktmp = a16[0:16, 2 * (RA_O + 3712):2 * (RA_O + 3712) + 512]
        cos16 = lambda t0, tn: f32v(TBL_O + t0, tn, 0, 16)
        sin16 = lambda t0, tn: f32v(TBL_O + 4096 + t0, tn, 0, 16)
        for tt in range(4):
            kp1 = PAIRS[0].ap()[0:16, 0:512]
            kp2 = PAIRS[0].ap()[0:16, 512:1024]
            for kc in range(8):
                em("t", lambda e, _k=kc, _tt=tt, _st=(kc == 0), _sp=(kc == 7):
                   e.matmul(kp1, b16v(RA_O + 2560, 256).rearrange("p (k o) -> p k o", k=8)
                            [:, _k, 0:16], xnt(_k, 512 * _tt, 512), start=_st, stop=_sp),
                   waits=[t_wkr, prev_ev])
                em("t", lambda e, _k=kc, _tt=tt, _st=(kc == 0), _sp=(kc == 7):
                   e.matmul(kp2, b16v(RA_O + 2560, 256).rearrange("p (k o) -> p k o", k=8)
                            [:, _k, 16:32], xnt(_k, 512 * _tt, 512), start=_st, stop=_sp),
                   inc=(kc == 7), waits=[])
            mmt = p.tok("t")
            e1 = em("v", lambda e, _d=kb1: e.tensor_copy(_d, kp1), inc=True,
                    waits=[mmt], drain=True)
            e2_ = em("v", lambda e, _d=kb2: e.tensor_copy(_d, kp2), inc=True, drain=True)
            em("v", lambda e, _tt=tt: e.tensor_tensor(
                out=kx1(512 * _tt, 512), in0=kb1, in1=cos16(512 * _tt, 512), op=OP.mult),
               waits=[("tbl", 64)], drain=True)
            em("v", lambda e, _tt=tt: e.tensor_tensor(
                out=ktmp, in0=kb2, in1=sin16(512 * _tt, 512), op=OP.mult), drain=True)
            em("v", lambda e, _tt=tt: e.tensor_tensor(
                out=kx1(512 * _tt, 512), in0=kx1(512 * _tt, 512), in1=ktmp, op=OP.subtract),
               drain=True)
            em("v", lambda e, _tt=tt: e.tensor_tensor(
                out=kx2(512 * _tt, 512), in0=kb1, in1=sin16(512 * _tt, 512), op=OP.mult),
               drain=True)
            em("v", lambda e, _tt=tt: e.tensor_tensor(
                out=ktmp, in0=kb2, in1=cos16(512 * _tt, 512), op=OP.mult), drain=True)
            prev_ev = em("v", lambda e, _tt=tt: e.tensor_tensor(
                out=kx2(512 * _tt, 512), in0=kx2(512 * _tt, 512), in1=ktmp, op=OP.add),
                inc=True, drain=True)
        kr_done = prev_ev
        def tmpf(b, t0=0, tn=T):
            o = 2 * (RA_O + (2688 if b == 0 else 3936)) + t0
            return a16[:, o:o + tn]

        def qrp_(c):
            return a16[:, 2 * QRP_O + 2048 * c:2 * QRP_O + 2048 * c + 2048]

        rope_last = prev_ev
        for c in range(2):
            cosv = f32v(TBL_O + 2048 * c, 2048)
            sinv = f32v(TBL_O + 4096 + 2048 * c, 2048)
            em("v", lambda e, _c=c, _cv=cosv: e.tensor_tensor(out=tmpf(0), in0=qrp_(_c),
                                                              in1=_cv, op=OP.mult),
               waits=[("tbl", 64)], drain=True)
            em("v", lambda e, _c=c, _sv=sinv: e.tensor_tensor(out=tmpf(1), in0=qrp_(_c + 2),
                                                              in1=_sv, op=OP.mult), drain=True)
            em("v", lambda e, _c=c: e.tensor_tensor(out=qrav(_c), in0=tmpf(0), in1=tmpf(1),
                                                    op=OP.subtract), drain=True)
            em("v", lambda e, _c=c, _sv=sinv: e.tensor_tensor(out=tmpf(0), in0=qrp_(_c),
                                                              in1=_sv, op=OP.mult), drain=True)
            em("v", lambda e, _c=c, _cv=cosv: e.tensor_tensor(out=tmpf(1), in0=qrp_(_c + 2),
                                                              in1=_cv, op=OP.mult), drain=True)
            rope_last = em("v", lambda e, _c=c: e.tensor_tensor(out=qrav(_c + 2), in0=tmpf(0),
                                                                in1=tmpf(1), op=OP.add),
                           inc=True, drain=True)
        prev_ev = rope_last
        for b in range(2):
            for tt in range(4):
                hv = PAIRS[1].ap()[64:96, 0:512]
                em("t", lambda e, _h=hv, _tt=tt: e.matmul(_h, krsel_(0), kx1(512 * _tt, 512),
                                                          start=True, stop=False),
                   waits=[kr_done, t_krs, prev_ev])
                em("t", lambda e, _h=hv, _tt=tt: e.matmul(_h, krsel_(1), kx2(512 * _tt, 512),
                                                          start=False, stop=True), inc=True)
                mmt = p.tok("t")
                prev_ev = em("v", lambda e, _b=b, _tt=tt, _s=hv: e.tensor_copy(
                    khv(_b, 64, 32, 512 * _tt, 512), _s), inc=True, waits=[mmt], drain=True)
        p.barrier()
        t_msk = em("g", lambda e: e.dma_start(out=b16v(MSK_O8, 2048).rearrange(
            "p (m j) -> p m j", m=4), in_=di["l1_masks"].rearrange("m p j -> p m j")),
            inc=True, sem=p.dsem("l1m"))
        selw8 = []
        selw8.append(em("v", lambda e: e.memset(f32v(SEL_O8, 64, 64, 1), 1.0), drain=True))
        selw8.append(em("v", lambda e: e.memset(f32v(SEL_O8 + 64, 64, 64, 1), 0.0), drain=True))
        selw8.append(em("v", lambda e: e.memset(f32v(SEL_O8 + 128, 64, 32, 1), 0.0), drain=True))
        selw8.append(em("v", lambda e: e.memset(f32v(SEL_O8 + 128 + 64, 64, 32, 1), 1.0),
                        drain=True))
        selw8.append(em("v", lambda e: e.memset(
            a16[:, 2 * V1L_O8:2 * V1L_O8 + 2080].rearrange("p (t c) -> p t c", c=130)
            [:, :, 64:65], 1.0), drain=True))
        selw8.append(em("v", lambda e: e.memset(
            a16[:, 2 * V1L_O8:2 * V1L_O8 + 2080].rearrange("p (t c) -> p t c", c=130)
            [:, :, 129:130], 1.0), drain=True))
        tok_sel8 = em("v", lambda e: e.nop(), inc=True)
        E64e = f32v(SEL_O8, 128, 64, 1)
        E64o = f32v(SEL_O8 + 128, 128, 32, 1)
        uq_use = p.tok("t")
        v1_use8 = p.tok("t")
        rc_free = p.tok("t")
        qh_free = [p.tok("t"), p.tok("t")]
        kh_free = [p.tok("t"), p.tok("t")]
        es_free = [None, None]
        head_done8 = p.tok("v")
        av_done_pair = p.tok("s")
        for pr_ in range(8):
            tuq = em("g", lambda e, _p=pr_: e.dma_start(
                out=b16v(UQ_O8, 256).rearrange("p (k o) -> p k o", k=2),
                in_=di["l1_wuq"][:, :, 128 * _p:128 * _p + 128]), inc=True,
                waits=[uq_use], sem=p.dsem("uqkv"))
            tuk = em("g", lambda e, _p=pr_: e.dma_start(
                out=b16v(UQ_O8 + 128, 256).rearrange("p (k o) -> p k o", k=2),
                in_=di["l1_wuk"][:, :, 128 * _p:128 * _p + 128]), inc=True, sem="uqkv")
            tuv = em("g", lambda e, _p=pr_: e.dma_start(
                out=b16v(UQ_O8 + 256, 256).rearrange("p (k o) -> p k o", k=2),
                in_=di["l1_wuv"][:, :, 128 * _p:128 * _p + 128]), inc=True, sem="uqkv")

            def uq8(w, k, o0, on):
                return b16v(UQ_O8 + 128 * w, 256).rearrange("p (k o) -> p k o", k=2)[:, k, o0:o0 + on]

            v1ev8 = []
            for ti in range(NT):
                hvv = PAIRS[2].ap()[:, 512 * (ti % 2):512 * (ti % 2) + 128]
                for kc in range(2):
                    em("t", lambda e, _h=hvv, _k=kc, _ti=ti, _st=(kc == 0), _sp=(kc == 1):
                       e.matmul(_h, ckvt(_k, 128 * _ti, 128), uq8(2, _k, 0, 128),
                                start=_st, stop=_sp),
                       inc=(kc == 1), waits=[tuv, v1_use8, head_done8, cq_done,
                                             (v1ev8[ti - 2] if ti >= 2 else None)])
                mmv = p.tok("t")
                ev = em("v", lambda e, _ti=ti,
                        _d=v1l(_ti0 := ti, 0, 0, 130).rearrange("p (a c) -> p a c", c=65)[:, :, 0:64],
                        _s=hvv.rearrange("p (a c) -> p a c", c=64):
                        e.tensor_copy(_d, _s), inc=True, waits=[mmv, tok_sel8], drain=True)
                v1ev8.append(ev)
            v1_ready8 = v1ev8[-1]
            for h2 in range(2):
                h = 2 * pr_ + h2
                b = h2
                asm_toks = []
                for which, wsel, dstf, freetok in ((0, 0, qhv, qh_free[b]),
                                                   (1, 1, khv, kh_free[b])):
                    for tt in range(4):
                        hv = PAIRS[3].ap()[0:64, 512 * (tt % 2):512 * (tt % 2) + 512]
                        for kc in range(2):
                            em("t", lambda e, _h=hv, _w=which, _k=kc, _h2=h2, _tt=tt,
                               _st=(kc == 0), _sp=(kc == 1):
                               e.matmul(_h, uq8(_w, _k, 64 * _h2, 64), cqt(_k, 512 * _tt, 512)
                                        if _w == 0 else ckvt(_k, 512 * _tt, 512),
                                        start=_st, stop=_sp),
                               inc=(kc == 1),
                               waits=[tuq, tuk, freetok, head_done8,
                                      (asm_toks[-2] if len(asm_toks) >= 2 else None)])
                        mmt = p.tok("t")
                        tk = em("v", lambda e, _b=b, _f=dstf, _tt=tt, _s=hv: e.tensor_copy(
                            _f(_b, 0, 64, 512 * _tt, 512), _s), inc=True, waits=[mmt],
                            drain=True)
                        asm_toks.append(tk)
                    if which == 0:
                        for tt in range(4):
                            hv = PAIRS[3].ap()[64:96, 512 * (tt % 2):512 * (tt % 2) + 512]
                            em("t", lambda e, _h=hv, _m=h % 8, _c=h // 8, _tt=tt:
                               e.matmul(_h, qrsel_(_m, 0), qrav(_c, 512 * _tt, 512),
                                        start=True, stop=False),
                               waits=[t_qrs, rope_last, qh_free[b], head_done8,
                                      (asm_toks[-2] if len(asm_toks) >= 2 else None)])
                            em("t", lambda e, _h=hv, _m=h % 8, _c=2 + h // 8, _tt=tt:
                               e.matmul(_h, qrsel_(_m, 1), qrav(_c, 512 * _tt, 512),
                                        start=False, stop=True), inc=True)
                            mmt = p.tok("t")
                            tk = em("v", lambda e, _b=b, _tt=tt, _s=hv: e.tensor_copy(
                                qhv(_b, 64, 32, 512 * _tt, 512), _s), inc=True,
                                waits=[mmt], drain=True)
                            asm_toks.append(tk)
                asm_done = asm_toks[-1]
                es_guard = v1_use8
                es_used = [None, None]
                av_toks = []
                esi = 0
                nd_stop_tok = [None] * 4
                for kc in range(NT):
                    jmin = kc // 4
                    sblks = []
                    for j in range(jmin, 4):
                        sp_i = (kc + j) % 2
                        sps = PAIRS[2].ap()[:, 512 * sp_i:512 * sp_i + 512]
                        em("t", lambda e, _o=sps, _b=b, _kc=kc, _j=j:
                           e.matmul(_o, khv(_b, 0, 96, 128 * _kc, 128),
                                    qhv(_b, 0, 96, 512 * _j, 512), start=True, stop=True),
                           inc=True,
                           waits=[asm_done, v1_ready8,
                                  (sblks[-2][1] if len(sblks) >= 2 else es_used[sp_i])])
                        smt = p.tok("t")
                        if j == kc // 4:
                            ct = em("v", lambda e, _o=sps, _i=kc % 4: e.scalar_tensor_tensor(
                                out=_o, in0=_o, scalar=-80.0,
                                in1=a16[:, 2 * (MSK_O8 + 256 * _i):2 * (MSK_O8 + 256 * _i) + 512],
                                op0=OP.max, op1=OP.min), inc=True,
                                waits=[smt, t_msk], drain=True)
                        else:
                            ct = em("v", lambda e, _o=sps: e.tensor_scalar(
                                out=_o, in0=_o, scalar1=80.0, scalar2=-80.0,
                                op0=OP.min, op1=OP.max), inc=True, waits=[smt], drain=True)
                        eb = esi % 2
                        et = em("s", lambda e, _d=esb(eb), _s=sps: e.activation(
                            out=_d, in_=_s, func=AF.Exp, bias=0.0, scale=SCALE_L1),
                            inc=True,
                            waits=[ct, (av_toks[-2][1] if len(av_toks) >= 2 else es_guard)],
                            drain=True)
                        es_used[0 if False else (esi % 2)] = et
                        sblks.append((j, et))
                        esi += 1
                        pj, hj = (j // 2, j % 2)
                        if h2 == 0:
                            outp = PAIRS[pj].ap()[0:65, 512 * hj:512 * hj + 512]
                            avt = em("t", lambda e, _o=outp, _ti=kc, _h2=h2, _eb=eb,
                                     _st=(kc == 0), _sp=(kc == 4 * j + 3):
                                     e.matmul(_o, v1l(_ti, _h2), esb(_eb), start=_st, stop=_sp),
                                     inc=True, waits=[et, av_done_pair])
                        else:
                            outp = PAIRS[pj].ap()[64:128, 512 * hj:512 * hj + 512]
                            em("t", lambda e, _o=outp, _ti=kc, _h2=h2, _eb=eb,
                               _st=(kc == 0), _sp=(kc == 4 * j + 3):
                               e.matmul(_o, v1l(_ti, _h2, 0, 64), esb(_eb), start=_st,
                                        stop=_sp), waits=[et, av_done_pair])
                            dout = PAIRS[pj].ap()[32:33, 512 * hj:512 * hj + 512]
                            avt = em("t", lambda e, _o=dout, _ti=kc, _h2=h2, _eb=eb,
                                     _st=(kc == 0), _sp=(kc == 4 * j + 3):
                                     e.matmul(_o, v1l(_ti, _h2, 64, 1), esb(_eb),
                                              start=_st, stop=_sp), inc=True)
                        av_toks.append((j, avt))
                        if kc == 4 * j + 3:
                            nd_stop_tok[j] = avt
                last_av = av_toks[-1][1]
                v1_use8 = last_av
                uq_use = last_av
                hv_toks = []
                for j in range(4):
                    pj, hj = (j // 2, j % 2)
                    ndt = nd_stop_tok[j]
                    if h2 == 0:
                        nsrc = PAIRS[pj].ap()[0:64, 512 * hj:512 * hj + 512]
                        dsrc = PAIRS[pj].ap()[64:65, 512 * hj:512 * hj + 512]
                        rcp = 64
                        esel = E64e
                    else:
                        nsrc = PAIRS[pj].ap()[64:128, 512 * hj:512 * hj + 512]
                        dsrc = PAIRS[pj].ap()[32:33, 512 * hj:512 * hj + 512]
                        rcp = 32
                        esel = E64o
                    ev = em("v", lambda e, _d=atn1(pr_, 512 * j, 512, 64 * h2, 64), _s=nsrc:
                            e.tensor_copy(_d, _s), inc=True, waits=[ndt], drain=True)
                    rc = f32v(RC_O8, 512, rcp, 1)
                    em("s", lambda e, _d=rc, _s=dsrc: e.activation(
                        out=_d, in_=_s, func=AF.Ln, bias=0.0, scale=1.0),
                       waits=[ndt, rc_free], drain=True)
                    t2 = em("s", lambda e, _d=rc: e.activation(
                        out=_d, in_=_d, func=AF.Exp, bias=0.0, scale=-1.0),
                        inc=True, drain=True)
                    rtok = p.tok("s")
                    bcp = PAIRS[pj].ap()[:, 512 * hj:512 * hj + 512]
                    mt = em("t", lambda e, _o=bcp, _sl=esel, _r=rc: e.matmul(
                        _o, _sl, _r, start=True, stop=True),
                        inc=True, waits=[rtok, tok_sel8, ev])
                    bct = p.tok("t")
                    rc_free = bct
                    t4 = em("v", lambda e, _a=atn1(pr_, 512 * j, 512, 64 * h2, 64),
                            _b=bcp[64 * h2:64 * h2 + 64, :]:
                            e.tensor_tensor(out=_a, in0=_a, in1=_b, op=OP.mult),
                            inc=True, waits=[bct], drain=True)
                    hv_toks.append(t4)
                head_done8 = hv_toks[-1]
                qh_free[b] = last_av
                kh_free[b] = last_av
                av_done_pair = p.tok("s")
        wo_use8 = [head_done8, head_done8]
        fold_evs8 = []
        for grp in range(4):
            for kc in range(8):
                wb = kc % 2
                wo_dst = b16v(QH_O if wb == 0 else KH_O, 2048)[:, 0:1024] if False else \
                    a16[:, 2 * (QH_O if wb == 0 else KH_O):2 * (QH_O if wb == 0 else KH_O) + 1024]
                wot = em("g", lambda e, _d=wo_dst, _k=kc: e.dma_start(out=_d, in_=di["l1_wo"][_k]),
                         inc=True, waits=[wo_use8[wb], head_done8], sem=p.dsem("wo8%d" % wb))
                for t_ in range(4):
                    ti = 4 * grp + t_
                    pv = PAIRS[t_].ap()
                    for hh in range(2):
                        em("t", lambda e, _h=half(t_, hh), _k=kc, _ti=ti, _hh=hh, _w=wo_dst,
                           _st=(kc == 0), _sp=(kc == 7):
                           e.matmul(_h, atn1(_k, 128 * _ti, 128),
                                    _w[:, 512 * _hh:512 * _hh + 512], start=_st, stop=_sp),
                           inc=((kc == 7 and hh == 1) or (t_ == 3 and hh == 1)),
                           waits=[wot, head_done8,
                                  (fold_evs8[4 * (grp - 1) + t_] if (grp > 0 and kc == 0) else None)])
                    if kc == 7:
                        mmt = p.tok("t")
                        ev = em("v", lambda e, _x=XTt(ti), _s=pv:
                                e.tensor_tensor(out=_x, in0=_x, in1=_s, op=OP.add),
                                inc=True, waits=[mmt], drain=True)
                        fold_evs8.append(ev)
                wo_use8[wb] = p.tok("t")
        p.barrier()
    p.views["atn1"] = (a16[:, 2 * ATN1_O:2 * ATN1_O + 16384], (128, 16384))
    p.views["cq1"] = (a16[:, 2 * CQT_O:2 * CQT_O + 4096], (128, 4096))
    p.views["qh"] = (a16[:, 2 * QH_O:2 * QH_O + 4096], (128, 4096))
    p.views["kh"] = (a16[:, 2 * KH_O:2 * KH_O + 4096], (128, 4096))
    p.views["v1l8"] = (a16[:, 2 * V1L_O8:2 * V1L_O8 + 4160], (128, 4160))
    p.views["ckv1"] = (a16[:, 2 * CKV_O:2 * CKV_O + 4096], (128, 4096))
    p.views["qra"] = (a16[:, 2 * QRA_O:2 * QRA_O + 8192], (128, 8192))
    p.views["kr1"] = (a16[:, 2 * RA_O:2 * RA_O + 4096], (128, 4096))

    if PHASES >= 6.8:
        H1C_O = FT_O
        CNS_O = FT_O + 8192
        CN3_O = calloc(512)
        CN4_O = calloc(2)
        TQ_O = FT_O + 7936
        assert CNS_O + 1024 <= AREN
        t_cn3 = em("g", lambda e: e.dma_start(
            out=b16v(CN3_O, 1024).rearrange("p (k o) -> p k o", k=4), in_=di["cn_w3"]),
            inc=True, sem=p.dsem("cnw"))
        t_cn4 = em("g", lambda e: e.dma_start(
            out=b16v(CN4_O, 2).rearrange("p (k o) -> p k o", k=2), in_=di["cn_w4"]),
            inc=True, sem="cnw")
        XSTG_O = FT_O
        tr_ev = None
        cast_ev = None
        for i in range(NT):
            for g in range(2):
                pq = g
                hvg = half(pq, 0)
                stg4 = f32v(XSTG_O, 512)
                cts = []
                for q in range(4):
                    c = 4 * g + q
                    cts.append(em("v", lambda e, _d=f32v(XSTG_O + 128 * q, 128),
                                  _s=xnt(c, 128 * i, 128), _w=tr_ev:
                                  e.tensor_copy(_d, _s), inc=(q == 3),
                                  waits=[tr_ev], drain=True))
                cast_ev = cts[-1]
                last = None
                for q in range(4):
                    last = em("t", lambda e, _o=hvg[:, 128 * q:128 * q + 128],
                              _s=f32v(XSTG_O + 128 * q, 128), _st=(q == 0), _sp=(q == 3):
                              e.matmul(_o, _s, identv, is_transpose=True, start=_st, stop=_sp),
                              inc=(q == 3), waits=[tok_ident, cast_ev, tr_ev])
                tr_ev = em("v", lambda e, _d=f32v(ACC_O + 1024 * i + 512 * g, 512), _s=hvg:
                           e.tensor_copy(_d, _s), inc=True, waits=[last], drain=True)
        p.barrier()
        for cyc in range(NCYC if PHASES >= 7 else 0):
            if cyc == 0:
                def outt(c, t0=0, tn=T):
                    return xnt(c, t0, tn)
            else:
                def outt(c, t0=0, tn=T):
                    return xnt(c, t0, tn)
                tr_ev = None
                for i in range(NT):
                    pq = i % 2
                    hv = half(pq, 0)
                    last = None
                    for g in range(2):
                        hvg = half(pq, g)
                        for q in range(4):
                            c = 4 * g + q
                            last = em("t", lambda e, _o=hvg[:, 128 * q:128 * q + 128],
                                      _s=f32v(ACC_O + 1024 * i + 128 * c, 128),
                                      _st=(q == 0), _sp=(q == 3):
                                      e.matmul(_o, _s, identv, is_transpose=True,
                                               start=_st, stop=_sp),
                                      inc=(q == 3), waits=[tok_ident, tr_ev])
                        dst = xnt_all[:, 4 * g:4 * g + 4, 128 * i:128 * i + 128]
                        tr_ev = em("v", lambda e, _d=dst,
                                   _s=hvg.rearrange("p (c t) -> p c t", t=128):
                                   e.tensor_copy(_d, _s), inc=True, waits=[last], drain=True)
                p.barrier()
            cn_use = [p.tok("t"), p.tok("t")]
            prev_ev = None
            for oc in range(8):
                cb = oc % 2
                w1t = em("g", lambda e, _b=cb, _o=oc: e.dma_start(
                    out=b16v(CNS_O + 512 * _b, 1024).rearrange("p (k j) -> p k j", k=8),
                    in_=di["cn_w1"][_o].rearrange("k p j -> p k j")),
                    inc=True, waits=[cn_use[cb]], sem=p.dsem("cn%d" % cb))
                for tt in range(4):
                    hv = half(2 + tt // 2, tt % 2)
                    for kc in range(8):
                        em("t", lambda e, _h=hv, _b=cb, _k=kc, _tt=tt,
                           _st=(kc == 0), _sp=(kc == 7):
                           e.matmul(_h, b16v(CNS_O + 512 * _b, 1024)
                                    .rearrange("p (k j) -> p k j", k=8)[:, _k, :],
                                    outt(_k, 512 * _tt, 512), start=_st, stop=_sp),
                           inc=(kc == 7), waits=[w1t, prev_ev])
                    mmt = p.tok("t")
                    cn_use[cb] = mmt
                    prev_ev = em("s", lambda e, _o=oc, _tt=tt, _sv=hv,
                                 _b=f32v(CNB_O + oc, 1), _d=a16[:, 2 * H1C_O + 2048 * oc + 512 * tt:
                                                               2 * H1C_O + 2048 * oc + 512 * tt + 512]:
                                 e.activation(out=_d, in_=_sv, func=AF.Gelu, bias=_b, scale=1.0),
                                 inc=True, waits=[mmt, tok_cnb], drain=True)
            h1_done = prev_ev

            def h1ct(c, t0=0, tn=T):
                return a16[:, 2 * H1C_O + 2048 * c + t0:2 * H1C_O + 2048 * c + t0 + tn]

            for oc in range(4):
                cb = oc % 2
                w2t = em("g", lambda e, _b=cb, _o=oc: e.dma_start(
                    out=b16v(CNS_O + 512 * _b, 1024).rearrange("p (k j) -> p k j", k=8),
                    in_=di["cn_w2"][_o].rearrange("k p j -> p k j")),
                    inc=True, waits=[cn_use[cb], h1_done], sem="cn%d" % cb)
                for tt in range(4):
                    hv = half(2 + tt // 2, tt % 2)
                    for kc in range(8):
                        em("t", lambda e, _h=hv, _b=cb, _k=kc, _tt=tt,
                           _st=(kc == 0), _sp=(kc == 7):
                           e.matmul(_h, b16v(CNS_O + 512 * _b, 1024)
                                    .rearrange("p (k j) -> p k j", k=8)[:, _k, :],
                                    h1ct(_k, 512 * _tt, 512), start=_st, stop=_sp),
                           inc=(kc == 7), waits=[w2t, h1_done, prev_ev])
                    mmt = p.tok("t")
                    cn_use[cb] = mmt
                    prev_ev = em("s", lambda e, _o=oc, _tt=tt, _sv=hv,
                                 _b=f32v(CNB_O + 8 + oc, 1),
                                 _d=xnt(oc, 512 * tt, 512):
                                 e.activation(out=_d, in_=_sv, func=AF.Gelu, bias=_b, scale=1.0),
                                 inc=True, waits=[mmt], drain=True)
            h2_done = prev_ev
            for oc in range(2):
                for tt in range(4):
                    hv = half(2 + tt // 2, tt % 2)
                    for kc in range(4):
                        em("t", lambda e, _h=hv, _k=kc, _o=oc, _tt=tt,
                           _st=(kc == 0), _sp=(kc == 3):
                           e.matmul(_h, b16v(CN3_O, 1024).rearrange("p (k o) -> p k o", k=4)
                                    [:, _k, 128 * _o:128 * _o + 128],
                                    xnt(_k, 512 * _tt, 512), start=_st, stop=_sp),
                           inc=(kc == 3), waits=[t_cn3, h2_done, prev_ev])
                    mmt = p.tok("t")
                    prev_ev = em("s", lambda e, _o=oc, _tt=tt, _sv=hv,
                                 _b=f32v(CNB_O + 12 + oc, 1),
                                 _d=xnt(4 + oc, 512 * tt, 512):
                                 e.activation(out=_d, in_=_sv, func=AF.Gelu, bias=_b, scale=1.0),
                                 inc=True, waits=[mmt], drain=True)
            h3_done = prev_ev
            tfps = PAIRS[0].ap()[:, 0:16]
            for ti in range(NT):
                for kc in range(2):
                    em("t", lambda e, _ti=ti, _k=kc, _st=(kc == 0), _sp=(kc == 1):
                       e.matmul(tfps[:, _ti:_ti + 1],
                                xnt(4 + _k, 128 * _ti, 128),
                                b16v(CN4_O, 2).rearrange("p (k o) -> p k o", k=2)[:, _k, :],
                                start=_st, stop=_sp),
                       inc=(kc == 1 and ti == 15), waits=[t_cn4, h3_done])
            mmt = p.tok("t")
            tfe = em("s", lambda e: e.activation(out=f32v(COL_O + 9, 16), in_=tfps,
                                                 func=AF.Sigmoid, bias=f32v(COL_O + 2, 1),
                                                 scale=1.0), inc=True, waits=[mmt], drain=True)
            em("v", lambda e: e.tensor_scalar(out=f32v(COL_O + 25, 16), in0=f32v(COL_O + 9, 16),
                                              scalar1=f32v(COL_O + 1, 1), scalar2=None,
                                              op0=OP.mult), waits=[tfe], drain=True)
            em("v", lambda e: e.tensor_scalar(out=f32v(COL_O + 25, 16), in0=f32v(COL_O + 25, 16),
                                              scalar1=f32v(COL_O + 0, 1), scalar2=None,
                                              op0=OP.add), drain=True)
            cpe = em("v", lambda e: e.tensor_scalar(out=f32v(COL_O + 41, 16),
                                                    in0=f32v(COL_O + 25, 16),
                                                    scalar1=-1.0, scalar2=None, op0=OP.mult),
                     inc=True, drain=True)
            tw_last = None
            for ti in range(NT):
                cp = f32v(COL_O + 25 + ti, 1)
                ncp = f32v(COL_O + 41 + ti, 1)
                base = ACC_O + 1024 * ti

                em("v", lambda e, _ti=ti, _c=cp, _b=base: e.scalar_tensor_tensor(
                    out=f32v(TQ_O, 256), in0=f32v(_b + 512, 256), scalar=_c,
                    in1=f32v(_b, 256), op0=OP.mult, op1=OP.add), waits=[cpe, tw_last],
                   drain=True)
                em("v", lambda e, _ti=ti, _n=ncp, _b=base: e.scalar_tensor_tensor(
                    out=f32v(_b + 512, 256), in0=f32v(_b, 256), scalar=_n,
                    in1=f32v(_b + 512, 256), op0=OP.mult, op1=OP.add), drain=True)
                em("v", lambda e, _b=base: e.tensor_copy(f32v(_b, 256), f32v(TQ_O, 256)),
                   drain=True)
                em("v", lambda e, _ti=ti, _n=ncp, _b=base: e.scalar_tensor_tensor(
                    out=f32v(TQ_O, 256), in0=f32v(_b + 768, 256), scalar=_n,
                    in1=f32v(_b + 256, 256), op0=OP.mult, op1=OP.add), drain=True)
                em("v", lambda e, _ti=ti, _c=cp, _b=base: e.scalar_tensor_tensor(
                    out=f32v(_b + 768, 256), in0=f32v(_b + 256, 256), scalar=_c,
                    in1=f32v(_b + 768, 256), op0=OP.mult, op1=OP.add), drain=True)
                tw_last = em("v", lambda e, _b=base: e.tensor_copy(
                    f32v(_b + 256, 256), f32v(TQ_O, 256)), inc=True, drain=True)
            p.barrier()
        if PHASES >= 7:
            for i in range(NT):
                em("v", lambda e, _i=i: e.tensor_tensor(out=XTt(_i), in0=XTt(_i), in1=acc_t(_i),
                                                        op=OP.add), inc=(i == 15), drain=True)
        p.barrier()

    st = em("y", lambda e: e.dma_start(out=out_h.rearrange("(i p) c -> p i c", p=128), in_=xt3),
            inc=True, waits=[p.tok("v"), p.tok("s"), p.tok("t")], sem=p.dsem("st"))
    em("y", lambda e: e.nop(), waits=[st])

    for name in dbg:
        view, shp = p.views[name]
        dh = nc.dram_tensor("dbg_" + name, list(shp),
                            F32 if view.tensor.dtype == F32 else F32, kind="ExternalOutput").ap()
        qe = "y" if view.dtype == F32 else "g"
        dt_ = em(qe, lambda e, _v=view, _h=dh: e.dma_start(out=_h, in_=_v), inc=True,
                 sem=p.dsem("dbg" + qe))
        em(qe, lambda e: e.nop(), waits=[dt_])

    with nc.Block() as block:
        @block.tensor
        def _(e):
            for fn in p.q["t"]:
                fn(e)

        @block.vector
        def _(e):
            for fn in p.q["v"]:
                fn(e)

        @block.scalar
        def _(e):
            for fn in p.q["s"]:
                fn(e)

        @block.gpsimd
        def _(e):
            for fn in p.q["g"]:
                fn(e)

        @block.sync
        def _(e):
            for fn in p.q["y"]:
                fn(e)

    stack.close()
    return nc


def kernel(**inputs):
    common = _prep_common(inputs)
    x = np.asarray(inputs["L0_attn_x"], np.float32)
    nc = build_nc(dbg=tuple(DBG))
    in_maps = []
    for b in range(8):
        m = dict(common)
        m["x"] = np.ascontiguousarray(x[b])
        in_maps.append(m)
    res = run_bass_kernel_spmd(nc, in_maps, core_ids=list(range(8)))
    return np.stack([res.results[b]["out"] for b in range(8)], axis=0)
